# version 02
# baseline (speedup 1.0000x reference)
import math
import numpy as np
from contextlib import ExitStack

import concourse.bass as bass
import concourse.mybir as mybir
from concourse.bass_utils import run_bass_kernel_spmd

F32 = mybir.dt.float32
BF16 = mybir.dt.bfloat16
AF = mybir.ActivationFunctionType
ALU = mybir.AluOpType
AX = mybir.AxisListType

D = 1024
T = 2048
DT = D // 128
NCH = T // 512
EPS = 1e-5

ENGS = ("sync", "vector", "scalar", "tensor", "gpsimd")


class Sem:
    __slots__ = ("h", "n")
    def __init__(self, h):
        self.h = h
        self.n = 0


class EngQ:
    def __init__(self, prog, name):
        self.p = prog
        self.name = name

    def __call__(self, fn, inc=None):
        if inc is not None:
            sem, k = inc
            sem.n += k
            def wrapped(e, fn=fn, semh=sem.h, k=k):
                fn(e).then_inc(semh, k)
            self.p.ops[self.name].append(wrapped)
        else:
            self.p.ops[self.name].append(fn)

    def wait(self, sem, val=None):
        v = sem.n if val is None else val
        self.p.ops[self.name].append(lambda e, h=sem.h, v=v: e.wait_ge(h, v))

    def fence(self, inc=None):
        self(lambda e: e.drain(), inc=inc)


class ScalarQ(EngQ):
    def fence(self, inc=None):
        self(lambda e: e.drain())
        if inc is not None:
            scr = self.p._s_scr
            self(lambda e, s=scr: e.activation(s[0:1, 0:1], s[0:1, 0:1],
                                               AF.Identity), inc=inc)


class Prog:
    def __init__(self, name="prog"):
        self.nc = bass.Bass("TRN2", target_bir_lowering=False, debug=False)
        self.ops = {e: [] for e in ENGS}
        self.es = ExitStack()
        self.es_stack = [self.es]
        self._sems = {}
        self._drams = {}
        self.Y = EngQ(self, "sync")
        self.V = EngQ(self, "vector")
        self.S = ScalarQ(self, "scalar")
        self.T = EngQ(self, "tensor")
        self.G = EngQ(self, "gpsimd")
        self._names = set()
        self._bar = None
        self._barn = 0
        self._s_scr = self.sbuf("s_scr11", [1, 8], F32)
        self.S(lambda e, s=self._s_scr: e.memzero(s[:]))

    def push_scope(self):
        es = ExitStack()
        self.es_stack.append(es)
        return es

    def pop_scope(self):
        self.es_stack.pop().close()

    def barrier(self):
        b = self.sem("bar")
        for q in (self.Y, self.V, self.S, self.T, self.G):
            if q is self.S:
                q(lambda e, s=self._s_scr: e.activation(
                    s[0:1, 1:2], s[0:1, 1:2], AF.Identity), inc=(b, 1))
            else:
                q(lambda e: e.drain(), inc=(b, 1))
        for q in (self.Y, self.V, self.S, self.T, self.G):
            q.wait(b)

    def _nm(self, base):
        i, nm = 0, base
        while nm in self._names:
            i += 1
            nm = f"{base}_{i}"
        self._names.add(nm)
        return nm

    def sem(self, name):
        if name not in self._sems:
            self._sems[name] = Sem(
                self.es.enter_context(self.nc.semaphore(self._nm(name))))
        return self._sems[name]

    def sbuf(self, name, shape, dt=F32):
        return self.es_stack[-1].enter_context(
            self.nc.sbuf_tensor(self._nm(name), shape, dt))

    def sbuf_root(self, name, shape, dt=F32):
        return self.es.enter_context(
            self.nc.sbuf_tensor(self._nm(name), shape, dt))

    def psum(self, name, shape, dt=F32):
        return self.es_stack[-1].enter_context(
            self.nc.psum_tensor(self._nm(name), shape, dt))

    def dram_in(self, name, shape, dt=F32):
        if name not in self._drams:
            self._names.add(name)
            self._drams[name] = self.nc.dram_tensor(
                name, shape, dt, kind="ExternalInput").ap()
        return self._drams[name]

    def dram_out(self, name, shape, dt=F32):
        self._names.add(name)
        return self.nc.dram_tensor(name, shape, dt, kind="ExternalOutput").ap()

    def finalize(self):
        with self.nc.Block() as block:
            for eng in ENGS:
                ops = self.ops[eng]
                if not ops:
                    continue
                def runner(e, ops=ops):
                    for f in ops:
                        f(e)
                getattr(block, eng)(runner)
        self.es.close()
        return self.nc



def load_fm(p, name, hbm_ap, rows, cols, ld, bf16=True, queue="Y"):
    kt = (rows + 127) // 128
    f32 = p.sbuf(name + "_f", [128, kt * cols], F32)
    src = hbm_ap.rearrange("(k p) c -> p k c", p=128)
    Q = getattr(p, queue)
    Q(lambda e, d=f32, s=src, cols=cols: e.dma_start(
        out=d[:].rearrange("p (k c) -> p k c", c=cols), in_=s), inc=(ld, 16))
    b16 = p.sbuf(name + "_b", [128, kt * cols], BF16) if bf16 else None
    return f32, b16


def cast_w(p, pairs, eng="V"):
    E = getattr(p, eng)
    for f32, b16 in pairs:
        if b16 is None:
            continue
        if eng == "S":
            E(lambda e, d=b16, s=f32: e.copy(d[:], s[:]))
        else:
            E(lambda e, d=b16, s=f32: e.tensor_copy(d[:], s[:]))


def emit_ln(p, res, xc_b, rstd, ps0, ps1, sq, ones_f, tt, sv, ss,
            out_f32=None, tag="ln", res_cols=None):
    mean = p.sbuf(tag + "_mean", [128, T], F32)
    ex2 = rstd
    for c in range(NCH):
        p.S(lambda e, d=sq, s=res, c=c: e.activation(
            d[:].rearrange("p (k n) -> p k n", n=512), _chunk3d(s, c), AF.Square))
        p.S.fence(inc=(ss, 1))
        p.T.wait(ss)
        for di in range(DT):
            p.T(lambda e, o=ps0, w=ones_f, s=res, di=di, c=c:
                e.matmul(o[:], w[:], s[:, di * T + c * 512: di * T + c * 512 + 512],
                         start=(di == 0), stop=(di == DT - 1)))
        for di in range(DT):
            p.T(lambda e, o=ps1, w=ones_f, s=sq, di=di:
                e.matmul(o[:], w[:], s[:, di * 512: di * 512 + 512],
                         start=(di == 0), stop=(di == DT - 1)),
                inc=(tt, 1) if di == DT - 1 else None)
        p.V.wait(tt)
        p.V(lambda e, d=mean, s=ps0, c=c: e.tensor_scalar_mul(
            d[:, c * 512:c * 512 + 512], s[:], 1.0 / D))
        p.V(lambda e, d=ex2, s=ps1, c=c: e.tensor_scalar_mul(
            d[:, c * 512:c * 512 + 512], s[:], 1.0 / D))
        p.V.fence(inc=(sv, 1))
        p.T.wait(sv)
        p.S.wait(sv)
    p.S(lambda e, d=sq, s=mean: e.activation(d[:, 0:T], s[:], AF.Square))
    p.S.fence(inc=(ss, 1))
    p.V.wait(ss)
    p.V(lambda e, d=rstd, b=sq: e.tensor_sub(d[:], d[:], b[:, 0:T]))
    p.V.fence()
    p.V(lambda e, d=rstd: e.tensor_scalar_add(d[:], d[:], EPS))
    p.V.fence()
    p.V(lambda e, d=rstd: e.reciprocal(d[:], d[:]))
    p.V.fence(inc=(sv, 1))
    p.S.wait(sv)
    p.S(lambda e, d=rstd: e.activation(d[:], d[:], AF.Sqrt))
    p.S.fence(inc=(ss, 1))
    if xc_b is not None:
        for di in range(DT):
            p.V(lambda e, d=xc_b, s=res, m=mean, di=di: e.tensor_sub(
                d[:, di * T:(di + 1) * T], s[:, di * T:(di + 1) * T], m[:]))
    p.V.fence(inc=(sv, 1))
    if out_f32 is not None:
        p.V.wait(ss)
        for di in range(DT):
            p.V(lambda e, d=out_f32, s=xc_b, r=rstd, di=di: e.tensor_mul(
                d[:, di * T:(di + 1) * T], s[:, di * T:(di + 1) * T], r[:]))
        p.V.fence(inc=(sv, 1))
    return mean


def _chunk3d(res_sb, c):
    return res_sb[:].rearrange("p (k t) -> p k t", t=T)[:, :, c * 512:(c + 1) * 512]


def ln_prologue(p, ld, rin_ap=None):
    rin = rin_ap if rin_ap is not None else p.dram_in("rin", [D, T], F32)
    res = p.sbuf("res", [128, DT * T], F32)
    src = rin.rearrange("(k p) t -> p k t", p=128)
    p.Y(lambda e, d=res, s=src: e.dma_start(
        out=d[:].rearrange("p (k t) -> p k t", t=T), in_=s), inc=(ld, 16))
    return res


def store_rout(p, res, st, wait_sems=(), rout_ap=None):
    rout = rout_ap if rout_ap is not None else p.dram_out("rout", [D, T], F32)
    dst = rout.rearrange("(k p) t -> p k t", p=128)
    for s in wait_sems:
        p.G.wait(s)
    p.G(lambda e, d=dst, s=res: e.dma_start(
        out=d, in_=s[:].rearrange("p (k t) -> p k t", t=T)), inc=(st, 16))
    p.G.wait(st)


def std_sems(p):
    return (p.sem("ld"), p.sem("st"), p.sem("sv"), p.sem("ss"), p.sem("tt"),
            p.sem("sg"))


def _with_ones(d):
    d["onesg"] = np.ones((128, 128), np.float32)
    return d


def _fin(p, fused):
    return None if fused else p.finalize()


def _fz_head(fz, name):
    if fz is None:
        return Prog(name), None, None, None, "", False
    return fz["p"], fz["rin"], fz["rout"], fz.get("rin3"), fz["px"], True



def build_l2_ffn(phase=99, fz=None):
    p, _rin, _rout, _rin3, px, fused = _fz_head(fz, "l2_ffn")
    ld, st, sv, ss, tt, sg = std_sems(p)
    res = ln_prologue(p, ld, _rin)
    w_wt = p.dram_in(px + "WT", [D, D], F32)
    w_vr = p.dram_in(px + "VR", [D, 32], F32)
    w_ct = p.dram_in(px + "CTP", [128, D], F32)
    w_b = p.dram_in(px + "bcol", [128, DT], F32)
    ones = p.dram_in(px + "ones", [128, 128], F32)

    ones_f = ones_root(p, ld)
    bcol = p.sbuf("bcol_sb", [128, DT], F32)
    p.Y(lambda e, d=bcol, s=w_b: e.dma_start(out=d[:], in_=s), inc=(ld, 16))
    wtf, wtb = load_fm(p, "wt", w_wt, D, D, ld)
    vrf, vrb = load_fm(p, "vr", w_vr, D, 32, ld)
    ctf, ctb = load_fm(p, "ct", w_ct, 128, D, ld)

    p.V.wait(ld)
    p.S.wait(ld)
    p.T.wait(ld)
    cast_w(p, [(wtf, wtb), (vrf, vrb)], "V")
    cast_w(p, [(ctf, ctb)], "S")
    halfb = p.sbuf("halfb", [128, DT], F32)
    p.V(lambda e, d=halfb, s=bcol: e.tensor_scalar_mul(d[:], s[:], 0.5))
    r32b = p.sbuf("r32b", [128, T], BF16)
    p.V(lambda e, d=r32b: e.memset(d[32:64, :], 0.0))
    p.V(lambda e, d=r32b: e.memset(d[64:128, :], 0.0))
    p.V.fence(inc=(sv, 1))
    p.S.fence(inc=(ss, 1))

    xc_b = p.sbuf("xc_b", [128, DT * T], BF16)
    rstd = p.sbuf("rstd", [128, T], F32)
    sq = p.sbuf("sq", [128, DT * 512], F32)
    ps0, ps1 = p.psum("ps0", [128, 512]), p.psum("ps1", [128, 512])
    emit_ln(p, res, xc_b, rstd, ps0, ps1, sq, ones_f, tt, sv, ss)

    ps_r = p.psum("ps_r", [128, 512])
    p.T.wait(sv)
    p.T.wait(ss)
    p.V.wait(ss)
    for c in range(NCH):
        for ki in range(DT):
            p.T(lambda e, o=ps_r, w=vrb, x=xc_b, ki=ki, c=c:
                e.matmul(o[0:32, 0:512], w[:, ki * 32: ki * 32 + 32],
                         x[:, ki * T + c * 512: ki * T + c * 512 + 512],
                         start=(ki == 0), stop=(ki == DT - 1)),
                inc=(tt, 1) if ki == DT - 1 else None)
        p.V.wait(tt)
        p.V(lambda e, d=r32b, s=ps_r, c=c: e.tensor_copy(
            d[0:32, c * 512:c * 512 + 512], s[0:32, 0:512]))
        p.V.fence(inc=(sv, 1))
        p.T.wait(sv)

    banks = [p.psum(f"pm{i}", [128, 512]) for i in range(4)]
    for mi in range(DT):
        for c in range(NCH):
            bank = banks[c]
            for ki in range(DT):
                p.T(lambda e, o=bank, w=wtb, x=xc_b, ki=ki, mi=mi, c=c:
                    e.matmul(o[:], w[:, ki * D + mi * 128: ki * D + mi * 128 + 128],
                             x[:, ki * T + c * 512: ki * T + c * 512 + 512],
                             start=(ki == 0), stop=False))
            p.T(lambda e, o=bank, w=ctb, r=r32b, mi=mi, c=c:
                e.matmul(o[:], w[:, mi * 128: mi * 128 + 128],
                         r[:, c * 512:c * 512 + 512], start=False, stop=True),
                inc=(tt, 1))
        p.V.wait(tt)
        for c in range(NCH):
            bank = banks[c]
            p.V(lambda e, o=bank, r=rstd, c=c: e.tensor_mul(
                o[:], o[:], r[:, c * 512:c * 512 + 512]))
            p.V.fence()
            p.V(lambda e, o=bank, d=res, mi=mi, c=c: e.scalar_tensor_tensor(
                d[:, mi * T + c * 512: mi * T + c * 512 + 512], o[:], 0.5,
                d[:, mi * T + c * 512: mi * T + c * 512 + 512],
                ALU.mult, ALU.add))
            p.V.fence()
        p.V(lambda e, d=res, b=halfb, mi=mi: e.tensor_scalar_add(
            d[:, mi * T:(mi + 1) * T], d[:, mi * T:(mi + 1) * T], b[:, mi:mi + 1]))
        p.V.fence(inc=(sv, 1))
        p.T.wait(sv)
    store_rout(p, res, st, wait_sems=[sv], rout_ap=_rout)
    return _fin(p, fused)


def l2_ffn_inputs(inputs, core, rin_fm):
    W = inputs["L2_ffn_W"]
    return _with_ones({
        "rin": rin_fm,
        "WT": np.ascontiguousarray(W.T),
        "VR": np.ascontiguousarray(inputs["L2_ffn_V_r"]),
        "CTP": np.ascontiguousarray(
            np.concatenate([inputs["L2_ffn_C"].T,
                            np.zeros((96, D), np.float32)], 0)),
        "bcol": np.ascontiguousarray(inputs["L2_ffn_b"].reshape(DT, 128).T),
        "ones": np.ones((128, 128), np.float32),
    })



def _fm(x):
    return np.ascontiguousarray(np.asarray(x).T)


_PROGS = {}


def _prog(name, builder):
    if name not in _PROGS:
        _PROGS[name] = builder()
    return _PROGS[name]


def kernel(**inputs):
    n = 8
    x = np.asarray(inputs["L0_attn_x"])
    nc = _prog("mega", build_fused)
    dummy = np.zeros((2, 2), np.float32)
    in_maps = []
    for b in range(n):
        d = {"rin0": np.ascontiguousarray(x[b].T),
             "onesg": np.ones((128, 128), np.float32)}
        for key, _, mk in STAGES:
            sub = mk(inputs, b, dummy) if mk is not l0_attn_moe_inputs else \
                mk(inputs, b, dummy, dummy)
            for k2, v in sub.items():
                if k2 in ("rin", "rin2", "rin3"):
                    continue
                if k2 in ("ident", "identb", "onesg"):
                    d[k2] = v
                else:
                    d[key + "_" + k2] = v
        in_maps.append(d)
    rr = run_bass_kernel_spmd(nc, in_maps, core_ids=list(range(n)))
    return np.stack([rr.results[b]["rout"].T for b in range(n)]).astype(np.float32)


def kernel_multi(**inputs):
    n = 8
    x = np.asarray(inputs["L0_attn_x"])
    resid = [np.ascontiguousarray(x[b].T) for b in range(n)]

    def launch(name, builder, mk, rin, extra=None):
        nc = _prog(name, builder)
        in_maps = []
        for b in range(n):
            if extra is None:
                m = mk(inputs, b, rin[b])
            else:
                m = mk(inputs, b, rin[b], extra[b])
            m["onesg"] = np.ones((128, 128), np.float32)
            in_maps.append(m)
        rr = run_bass_kernel_spmd(nc, in_maps, core_ids=list(range(n)))
        return [np.ascontiguousarray(rr.results[b]["rout"]) for b in range(n)]

    resid = launch("l0ffn", build_l0_ffn, l0_ffn_inputs, resid)
    inner = launch("l0wkv", build_l0_attn_wkv, l0_attn_wkv_inputs, resid)
    resid = launch("l0moe", build_l0_attn_moe, l0_attn_moe_inputs, inner,
                   extra=resid)
    resid = launch("l0ffn", build_l0_ffn, l0_ffn_inputs, resid)
    resid = launch("l1ffn", build_l1_ffn, l1_ffn_inputs, resid)
    resid = launch("l1attn", build_l1_attn, l1_attn_inputs, resid)
    resid = launch("l1ffn", build_l1_ffn, l1_ffn_inputs, resid)
    resid = launch("l2ffn", build_l2_ffn, l2_ffn_inputs, resid)
    resid = launch("l2attn", build_l2_attn, l2_attn_inputs, resid)
    resid = launch("l2ffn", build_l2_ffn, l2_ffn_inputs, resid)
    resid = launch("l3gsa", build_l3_ffn, l3_ffn_inputs, resid)
    resid = launch("l3attn", build_l3_attn, l3_attn_inputs, resid)
    resid = launch("l3gsa", build_l3_ffn, l3_ffn_inputs, resid)
    return np.stack([r.T for r in resid]).astype(np.float32)



CS = 1920

def build_l2_attn(phase=99, fz=None):
    p, _rin, _rout, _rin3, px, fused = _fz_head(fz, "l2_attn")
    ld, st, sv, ss, tt, sg = std_sems(p)
    res = ln_prologue(p, ld, _rin)
    w_wv = p.dram_in(px + "WVT", [D, D], F32)
    w_wo = p.dram_in(px + "WOT", [D, D], F32)
    w_unused = [p.dram_in(px + nm, shp, F32) for nm, shp in
                [("UWQ", [D, D]), ("UWK", [D, D]), ("UCW", [D, 256]),
                 ("UCB", [1, D])]]
    ones = p.dram_in(px + "ones", [128, 128], F32)
    ones_f = ones_root(p, ld)
    utouch = p.sbuf("utouch", [1, 1024], F32)
    for iu, u in enumerate(w_unused):
        p.Y(lambda e, d=utouch, s=u, iu=iu: e.dma_start(
            out=d[0:1, iu * 256:(iu + 1) * 256], in_=s[0:1, 0:256]),
            inc=(ld, 16))
    stage = p.sbuf("stage", [128, DT * D], F32)
    wvb = p.sbuf("wvb", [128, DT * D], BF16)
    wob = p.sbuf("wob", [128, DT * D], BF16)
    p.Y(lambda e, d=stage, s=w_wv.rearrange("(k p) c -> p k c", p=128):
        e.dma_start(out=d[:].rearrange("p (k c) -> p k c", c=D), in_=s),
        inc=(ld, 16))
    p.V.wait(ld)
    p.V(lambda e, d=wvb, s=stage: e.tensor_copy(d[:], s[:]))
    p.V.fence(inc=(sv, 1))
    p.Y.wait(sv)
    p.Y(lambda e, d=stage, s=w_wo.rearrange("(k p) c -> p k c", p=128):
        e.dma_start(out=d[:].rearrange("p (k c) -> p k c", c=D), in_=s),
        inc=(ld, 16))
    p.V.wait(ld)
    p.V(lambda e, d=wob, s=stage: e.tensor_copy(d[:], s[:]))
    p.V.fence(inc=(sv, 1))
    p.S.wait(ld)
    p.T.wait(ld)

    xc_b = p.sbuf("xc_b", [128, DT * T], BF16)
    rstd = p.sbuf("rstd", [128, T], F32)
    sq = p.sbuf("sq", [128, DT * 512], F32)
    ps0, ps1 = p.psum("ps0", [128, 512]), p.psum("ps1", [128, 512])
    emit_ln(p, res, xc_b, rstd, ps0, ps1, sq, ones_f, tt, sv, ss)

    psA = p.psum("psA", [128, DT])
    psB = p.psum("psB", [128, DT])
    v0b = p.sbuf("v0b", [128, DT], BF16)
    rcol = p.sbuf("rcol", [128, DT], F32)
    p.T.wait(sv)
    p.T.wait(ss)
    for mi in range(DT):
        for ki in range(DT):
            p.T(lambda e, o=psA, w=wvb, x=xc_b, ki=ki, mi=mi:
                e.matmul(o[:, mi:mi + 1], w[:, ki * D + mi * 128: ki * D + mi * 128 + 128],
                         x[:, ki * T + CS: ki * T + CS + 1],
                         start=(ki == 0), stop=(ki == DT - 1)),
                inc=(tt, 1) if ki == DT - 1 else None)
    p.V.wait(tt)
    p.V(lambda e, d=v0b, s=psA: e.tensor_copy(d[:], s[:]))
    p.V.fence(inc=(sv, 1))
    p.T.wait(sv)
    for mi in range(DT):
        for ki in range(DT):
            p.T(lambda e, o=psB, w=wob, x=v0b, ki=ki, mi=mi:
                e.matmul(o[:, mi:mi + 1], w[:, ki * D + mi * 128: ki * D + mi * 128 + 128],
                         x[:, ki:ki + 1],
                         start=(ki == 0), stop=(ki == DT - 1)),
                inc=(tt, 1) if ki == DT - 1 else None)
    p.V.wait(tt)
    p.V(lambda e, d=rcol, s=psB, r=rstd: e.tensor_scalar(
        d[:], s[:], r[:, CS:CS + 1], None, ALU.mult))
    p.V.fence()
    tmp = sq
    for di in range(DT):
        p.V(lambda e, d=tmp, s=xc_b, r=rstd, di=di: e.tensor_mul(
            d[:, 0:CS], s[:, di * T: di * T + CS], r[:, 0:CS]))
        p.V.fence()
        p.V(lambda e, d=res, s=tmp, di=di: e.tensor_add(
            d[:, di * T: di * T + CS], d[:, di * T: di * T + CS], s[:, 0:CS]))
        p.V(lambda e, d=res, r=rcol, di=di: e.tensor_scalar(
            d[:, di * T + CS: di * T + T], d[:, di * T + CS: di * T + T],
            r[:, di:di + 1], None, ALU.add))
        p.V.fence()
    p.V.fence(inc=(sv, 1))
    store_rout(p, res, st, wait_sems=[sv], rout_ap=_rout)
    return _fin(p, fused)


def l2_attn_inputs(inputs, core, rin_fm):
    return _with_ones({
        "rin": rin_fm,
        "WVT": np.ascontiguousarray(inputs["L2_attn_Wv"].T),
        "WOT": np.ascontiguousarray(inputs["L2_attn_Wo"].T),
        "UWQ": np.asarray(inputs["L2_attn_Wq"]),
        "UWK": np.asarray(inputs["L2_attn_Wk"]),
        "UCW": np.asarray(inputs["L2_attn_conv_w"]).reshape(D, 256),
        "UCB": np.asarray(inputs["L2_attn_conv_b"]).reshape(1, D),
        "ones": np.ones((128, 128), np.float32),
    })



def emit_router_tm(p, res, mean, rstd, rtf, E_, ps_lg, ps_tr, lg_fm, prob_tm,
                   ident_f, ones_f, tt, sv, ss, softmax=True, bias_col=None):
    swc = p.sbuf("rt_swc", [128, 1], F32)
    for ki in range(DT):
        p.T(lambda e, o=ps_lg, w=rtf, x=ones_f, ki=ki, E_=E_:
            e.matmul(o[0:E_, 0:1], w[:, ki * E_: (ki + 1) * E_], x[:, 0:1],
                     start=(ki == 0), stop=(ki == DT - 1)),
            inc=(tt, 1) if ki == DT - 1 else None)
    p.V.wait(tt)
    p.V(lambda e, d=swc, s=ps_lg, E_=E_: e.tensor_copy(d[0:E_, :], s[0:E_, 0:1]))
    p.V.fence(inc=(sv, 1))
    p.T.wait(sv)
    for c in range(NCH):
        for ki in range(DT):
            p.T(lambda e, o=ps_lg, w=rtf, x=res, ki=ki, c=c, E_=E_:
                e.matmul(o[0:E_, 0:512], w[:, ki * E_: (ki + 1) * E_],
                         x[:, ki * T + c * 512: ki * T + c * 512 + 512],
                         start=(ki == 0), stop=(ki == DT - 1)),
                inc=(tt, 1) if ki == DT - 1 else None)
        p.V.wait(tt)
        p.V(lambda e, d=lg_fm, m=mean, w=swc, c=c, E_=E_: e.tensor_scalar(
            d[0:E_, c * 512:c * 512 + 512], m[0:E_, c * 512:c * 512 + 512],
            w[0:E_, 0:1], None, ALU.mult))
        p.V.fence()
        p.V(lambda e, d=lg_fm, s=ps_lg, c=c, E_=E_: e.tensor_sub(
            d[0:E_, c * 512:c * 512 + 512], s[0:E_, 0:512],
            d[0:E_, c * 512:c * 512 + 512]))
        p.V.fence()
        p.V(lambda e, d=lg_fm, r=rstd, c=c, E_=E_: e.tensor_mul(
            d[0:E_, c * 512:c * 512 + 512], d[0:E_, c * 512:c * 512 + 512],
            r[0:E_, c * 512:c * 512 + 512]))
        if bias_col is not None:
            p.V.fence()
            p.V(lambda e, d=lg_fm, b=bias_col, c=c, E_=E_: e.tensor_scalar(
                d[0:E_, c * 512:c * 512 + 512], d[0:E_, c * 512:c * 512 + 512],
                b[0:E_, 0:1], None, ALU.add))
        p.V.fence(inc=(sv, 1))
        p.T.wait(sv)
    for b2 in range(T // 256):
        for half in range(2):
            tt_i = b2 * 2 + half
            p.T(lambda e, o=ps_tr, s=lg_fm, tt_i=tt_i, half=half, idf=ident_f:
                e.transpose(o[:, half * 128:half * 128 + 128],
                            s[:, tt_i * 128:tt_i * 128 + 128], idf[:]),
                inc=(tt, 1) if half == 1 else None)
        p.V.wait(tt)
        for half in range(2):
            tt_i = b2 * 2 + half
            p.V(lambda e, d=prob_tm, s=ps_tr, tt_i=tt_i, half=half, E_=E_:
                e.tensor_copy(d[:, tt_i * E_:(tt_i + 1) * E_],
                              s[:, half * 128:half * 128 + E_]))
        p.V.fence(inc=(sv, 1))
        p.T.wait(sv)
    if softmax:
        nmx = p.sbuf("rt_nmx", [128, 16], F32)
        den = p.sbuf("rt_den", [128, 16], F32)
        p.V(lambda e, d=nmx, s=prob_tm, E_=E_: e.tensor_reduce(
            d[:].rearrange("p (a b) -> p a b", b=1),
            s[:].rearrange("p (a b) -> p a b", b=E_), AX.X, ALU.max, negate=True))
        p.V.fence(inc=(sv, 1))
        p.S.wait(sv)
        for ti in range(16):
            p.S(lambda e, d=prob_tm, s=prob_tm, m=nmx, ti=ti, E_=E_: e.activation(
                d[:, ti * E_:(ti + 1) * E_], s[:, ti * E_:(ti + 1) * E_],
                AF.Exp, bias=m[:, ti:ti + 1]))
        p.S.fence(inc=(ss, 1))
        p.V.wait(ss)
        p.V(lambda e, d=den, s=prob_tm, E_=E_: e.tensor_reduce(
            d[:].rearrange("p (a b) -> p a b", b=1),
            s[:].rearrange("p (a b) -> p a b", b=E_), AX.X, ALU.add))
        p.V.fence()
        p.V(lambda e, d=den: e.reciprocal(d[:], d[:]))
        p.V.fence()
        for ti in range(16):
            p.V(lambda e, d=prob_tm, m=den, ti=ti, E_=E_: e.tensor_scalar(
                d[:, ti * E_:(ti + 1) * E_], d[:, ti * E_:(ti + 1) * E_],
                m[:, ti:ti + 1], None, ALU.mult))
        p.V.fence(inc=(sv, 1))


def emit_top2_combine(p, prob_tm, comb_tm, E_, sv):
    mx8 = p.sbuf("t2_mx8", [128, 8], F32)
    s2 = p.sbuf("t2_s2", [128, 1], F32)
    zap = p.sbuf("t2_zap", [128, E_], F32)
    for ti in range(16):
        sl = (ti * E_, (ti + 1) * E_)
        p.V(lambda e, d=mx8, s=prob_tm, sl=sl: e.max(out=d[:], in_=s[:, sl[0]:sl[1]]))
        p.V.fence()
        p.V(lambda e, d=s2, s=mx8: e.tensor_add(d[:], s[:, 0:1], s[:, 1:2]))
        p.V(lambda e, d=mx8: e.memset(d[:, 2:8], -1e30))
        p.V.fence()
        p.V(lambda e, d=zap, s=prob_tm, m=mx8, sl=sl: e.match_replace(
            out=d[:], in_to_replace=m[:], in_values=s[:, sl[0]:sl[1]], imm_value=0.0))
        p.V(lambda e, d=s2: e.reciprocal(d[:], d[:]))
        p.V.fence()
        p.V(lambda e, d=comb_tm, s=prob_tm, z=zap, sl=sl: e.tensor_sub(
            d[:, sl[0]:sl[1]], s[:, sl[0]:sl[1]], z[:]))
        p.V.fence()
        p.V(lambda e, d=comb_tm, m=s2, sl=sl: e.tensor_scalar(
            d[:, sl[0]:sl[1]], d[:, sl[0]:sl[1]], m[:, 0:1], None, ALU.mult))
        p.V.fence()
    p.V.fence(inc=(sv, 1))


def emit_comb_fm(p, comb_tm, comb_fm, E_, ident_f, ps_tr, tt, sv):
    p.T.wait(sv)
    for b2 in range(T // 256):
        for half in range(2):
            ti = b2 * 2 + half
            p.T(lambda e, o=ps_tr, s=comb_tm, ti=ti, half=half, idf=ident_f, E_=E_:
                e.transpose(o[0:E_, half * 128:half * 128 + 128],
                            s[:, ti * E_:(ti + 1) * E_], idf[:]),
                inc=(tt, 1) if half == 1 else None)
        p.V.wait(tt)
        for half in range(2):
            ti = b2 * 2 + half
            p.V(lambda e, d=comb_fm, s=ps_tr, ti=ti, half=half, E_=E_:
                e.tensor_copy(d[0:E_, ti * 128:(ti + 1) * 128],
                              s[0:E_, half * 128:half * 128 + 128]))
        p.V.fence(inc=(sv, 1))
        p.T.wait(sv)


def build_l1_ffn(phase=99, fz=None):
    p, _rin, _rout, _rin3, px, fused = _fz_head(fz, "l1_ffn")
    E_, H_ = 8, 4096
    FH = 256
    NFH = H_ // FH
    p0 = p
    ld, st, sv, ss, tt, sg = std_sems(p)
    res = ln_prologue(p, ld, _rin)
    w_rt = p.dram_in(px + "RWT", [D, E_], F32)
    w_fc = p.dram_in(px + "WFC", [E_ * D, H_], F32)
    w_pj = p.dram_in(px + "WPJ", [E_ * H_, D], F32)
    w_bf = p.dram_in(px + "BFC", [128, E_ * (H_ // 128)], F32)
    w_bp = p.dram_in(px + "BPJ", [128, D], BF16)
    ones = p.dram_in(px + "ones", [128, 128], F32)
    selh = p.dram_in(px + "SELH", [E_, E_ * 128], F32)

    ones_f = ones_root(p, ld)
    idf = ones_f_id(p, ld)
    rtf, _ = load_fm(p, "rt", w_rt, D, E_, ld, bf16=False)
    bfc = p.sbuf("bfc", [128, E_ * (H_ // 128)], F32)
    p.Y(lambda e, d=bfc, s=w_bf: e.dma_start(out=d[:], in_=s), inc=(ld, 16))
    bpj_b = p.sbuf("bpj_b", [128, D], BF16)
    p.Y(lambda e, d=bpj_b, s=w_bp: e.dma_start(out=d[:], in_=s), inc=(ld, 16))
    self_f = p.sbuf("self_f", [128, E_ * 128], F32)
    p.Y(lambda e, d=self_f, s=selh: e.dma_start(out=d[0:E_, :], in_=s), inc=(ld, 16))

    p.V.wait(ld)
    p.S.wait(ld)
    p.T.wait(ld)
    p.V.fence(inc=(sv, 1))

    xc_b = p.sbuf("xc_b", [128, DT * T], BF16)
    rstd = p.sbuf("rstd", [128, T], F32)
    sq = p.sbuf("sq", [128, DT * 512], F32)
    ps0, ps1 = p.psum("ps0", [128, 512]), p.psum("ps1", [128, 512])
    mean = emit_ln(p, res, xc_b, rstd, ps0, ps1, sq, ones_f, tt, sv, ss)

    lg_fm = p.sbuf("lg_fm", [128, T], F32)
    prob_tm = p.sbuf("prob_tm", [128, 16 * E_], F32)
    p.T.wait(sv)
    p.T.wait(ss)
    p.V.wait(ss)
    emit_router_tm(p, res, mean, rstd, rtf, E_, ps0, ps1, lg_fm, prob_tm,
                   idf, ones_f, tt, sv, ss)
    comb_tm = p.sbuf("comb_tm", [128, 16 * E_], F32)
    emit_top2_combine(p, prob_tm, comb_tm, E_, sv)
    comb_fm = lg_fm
    emit_comb_fm(p, comb_tm, comb_fm, E_, idf, ps1, tt, sv)
    if phase == 1:
        rout1 = _rout if fused else p.dram_out("rout", [D, T], F32)
        p.G.wait(sv)
        p.G(lambda e, d=rout1, s=comb_fm: e.dma_start(out=d[0:128, :], in_=s[:]),
            inc=(st, 16))
        p.G.wait(st)
        return _fin(p, fused)

    y = res
    combh_b = p.sbuf("combh_b", [128, T], BF16)
    p.V(lambda e, d=combh_b, s=comb_fm: e.tensor_scalar_mul(
        d[0:E_, :], s[0:E_, :], 0.5))
    p.V.fence(inc=(sv, 1))
    p.T.wait(sv)
    banks = [p.psum(f"pm{i}", [128, 512]) for i in range(4)]
    for mi in range(DT):
        for c in range(NCH):
            p.T(lambda e, o=banks[c], w=bpj_b, x=combh_b, mi=mi, c=c, E_=E_:
                e.matmul(o[:], w[0:E_, mi * 128 + 0: mi * 128 + 128] if False else
                         w[0:E_, mi * 128: mi * 128 + 128],
                         x[0:E_, c * 512: c * 512 + 512], start=True, stop=True),
                inc=(tt, 1))
        p.V.wait(tt)
        for c in range(NCH):
            p.V(lambda e, d=y, s=banks[c], mi=mi, c=c: e.tensor_copy(
                d[:, mi * T + c * 512: mi * T + c * 512 + 512], s[:]))
        p.V.fence(inc=(sv, 1))
        p.T.wait(sv)

    if phase == 2:
        rout2 = p.dram_out("rout", [D, T], F32)
        p.G.wait(sv)
        p.G(lambda e, d=rout2, s=y: e.dma_start(
            out=d.rearrange("(k p) t -> p k t", p=128),
            in_=s[:].rearrange("p (k t) -> p k t", t=T)), inc=(st, 16))
        p.G.wait(st)
        return _fin(p, fused)

    l2 = p.sem("l2")
    stage_fc = p.sbuf("stage_fc", [128, DT * FH], F32)
    stage_pj = p.sbuf("stage_pj", [128, (FH // 128) * D], F32)
    wfc_b = p.sbuf("wfc_b", [128, DT * FH], BF16)
    wpj_b = p.sbuf("wpj_b", [128, (FH // 128) * D], BF16)
    hblk = p.sbuf("hblk", [128, (FH // 128) * 512], F32)
    hb = p.sbuf("hb", [128, (FH // 128) * 512], BF16)
    cbr_b = p.sbuf("cbr_b", [128, T], BF16)
    pc = ps0
    ph = [p.psum("ph0", [128, 512]), ps1]
    ldw = p.sem("ldw")
    evh = p.sem("evh")
    sgb = sg.n
    n_exp = 1 if phase == 3 else E_
    for e_i in range(n_exp):
        p.T.wait(sv)
        for c in range(NCH):
            p.T(lambda e, o=pc, s=self_f, cf=comb_fm, e_i=e_i, c=c, E_=E_:
                e.matmul(o[:], s[0:E_, e_i * 128:(e_i + 1) * 128],
                         cf[0:E_, c * 512:c * 512 + 512], start=True, stop=True),
                inc=(tt, 1))
            p.V.wait(tt)
            p.V(lambda e, d=cbr_b, s=pc, c=c: e.tensor_copy(
                d[:, c * 512:c * 512 + 512], s[:]))
            p.V.fence(inc=(sv, 1))
            p.T.wait(sv)
        for fh in range(NFH):
            it = e_i * NFH + fh
            if it > 0:
                p.Y.wait(sg, sgb + it)
                p.G.wait(sg, sgb + it)
            p.Y(lambda e, d=stage_fc, s=w_fc, e_i=e_i, fh=fh: e.dma_start(
                out=d[:].rearrange("p (k c) -> p k c", c=FH),
                in_=s.rearrange("(E k p) h -> p E k h", p=128, E=E_)[
                    :, e_i, :, fh * FH:(fh + 1) * FH]), inc=(l2, 16))
            p.G(lambda e, d=stage_pj, s=w_pj, e_i=e_i, fh=fh: e.dma_start(
                out=d[:].rearrange("p (k c) -> p k c", c=D),
                in_=s.rearrange("(E k p) h -> p E k h", p=128, E=E_)[
                    :, e_i, fh * (FH // 128):(fh + 1) * (FH // 128), :]),
                inc=(st, 16))
            p.V.wait(l2)
            p.V.wait(st)
            p.V(lambda e, d=wfc_b, s=stage_fc: e.tensor_copy(d[:], s[:]))
            p.V(lambda e, d=wpj_b, s=stage_pj: e.tensor_copy(d[:], s[:]))
            p.V.fence(inc=(sg, 1))
            p.T.wait(sg)
            for c in range(NCH):
                for hm in range(FH // 128):
                    bank = ph[hm % 2]
                    if it + c + hm > 0:
                        p.T.wait(evh)
                    for ki in range(DT):
                        p.T(lambda e, o=bank, w=wfc_b, x=xc_b, ki=ki, hm=hm, c=c:
                            e.matmul(o[:], w[:, ki * FH + hm * 128: ki * FH + hm * 128 + 128],
                                     x[:, ki * T + c * 512: ki * T + c * 512 + 512],
                                     start=(ki == 0), stop=(ki == DT - 1)),
                            inc=(tt, 1) if ki == DT - 1 else None)
                    p.V.wait(tt)
                    p.V(lambda e, d=hblk, s=bank, r=rstd, hm=hm, c=c: e.tensor_mul(
                        d[:, hm * 512: hm * 512 + 512], s[:],
                        r[:, c * 512: c * 512 + 512]))
                    p.V.fence(inc=(sv, 1), )
                    p.S.wait(sv)
                    p.S(lambda e, d=hblk, s=hblk, b=bfc, e_i=e_i, fh=fh, hm=hm:
                        e.activation(d[:, hm * 512: hm * 512 + 512],
                                     s[:, hm * 512: hm * 512 + 512],
                                     AF.Gelu_apprx_tanh,
                                     bias=b[:, e_i * 32 + fh * (FH // 128) + hm: e_i * 32 + fh * (FH // 128) + hm + 1]))
                    p.S.fence(inc=(ss, 1))
                    p.V.wait(ss)
                    p.V(lambda e, d=hb, s=hblk, cb=cbr_b, hm=hm, c=c: e.tensor_mul(
                        d[:, hm * 512: hm * 512 + 512], s[:, hm * 512: hm * 512 + 512],
                        cb[:, c * 512: c * 512 + 512]))
                    p.V.fence(inc=(evh, 1))
                    p.T.wait(evh)
                for mi in range(DT):
                    bank = banks[mi % 2 + 2]
                    for ki in range(FH // 128):
                        p.T(lambda e, o=bank, w=wpj_b, x=hb, ki=ki, mi=mi:
                            e.matmul(o[:], w[:, ki * D + mi * 128: ki * D + mi * 128 + 128],
                                     x[:, ki * 512: ki * 512 + 512],
                                     start=(ki == 0), stop=(ki == FH // 128 - 1)),
                            inc=(tt, 1) if ki == FH // 128 - 1 else None)
                    p.V.wait(tt)
                    p.V(lambda e, d=y, s=bank, mi=mi, c=c: e.tensor_add(
                        d[:, mi * T + c * 512: mi * T + c * 512 + 512],
                        d[:, mi * T + c * 512: mi * T + c * 512 + 512], s[:]))
                    p.V.fence(inc=(sv, 1))
                    p.T.wait(sv)
    rin2 = _rin if fused else p.dram_in("rin2", [D, T], F32)
    rout = _rout if fused else p.dram_out("rout", [D, T], F32)
    rld = sq
    l3 = p.sem("l3")
    for mi in range(DT):
        p.Y(lambda e, d=rld, s=rin2, mi=mi: e.dma_start(
            out=d[:, 0:T], in_=s.rearrange("(k p) t -> p k t", p=128)[:, mi, :]),
            inc=(l3, 16))
        p.V.wait(l3)
        p.V(lambda e, d=y, s=rld, mi=mi: e.tensor_add(
            d[:, mi * T:(mi + 1) * T], d[:, mi * T:(mi + 1) * T], s[:, 0:T]))
        p.V.fence(inc=(sv, 1))
        p.G.wait(sv)
        p.G(lambda e, d=rout, s=y, mi=mi: e.dma_start(
            out=d.rearrange("(k p) t -> p k t", p=128)[:, mi, :],
            in_=s[:, mi * T:(mi + 1) * T]), inc=(st, 16))
        p.Y.wait(sv, sv.n)
    p.G.wait(st)
    return _fin(p, fused)


def ones_root(p, ld):
    if not hasattr(p, "_onesf"):
        src2 = p.dram_in("onesg", [128, 128], F32)
        of = p.sbuf_root("onesf_g", [128, 128], F32)
        p.Y(lambda e, d=of, s=src2: e.dma_start(out=d[:], in_=s), inc=(ld, 16))
        p._onesf = of
    return p._onesf


def ones_f_id(p, ld):
    if not hasattr(p, "_idf"):
        iden = p.dram_in("ident", [128, 128], F32)
        idf = p.sbuf_root("idf", [128, 128], F32)
        p.Y(lambda e, d=idf, s=iden: e.dma_start(out=d[:], in_=s), inc=(ld, 16))
        p._idf = idf
    return p._idf


def l1_ffn_inputs(inputs, core, rin_fm):
    E_, H_ = 8, 4096
    bfc = inputs["L1_ffn_b_fc"]
    bfc_cols = np.zeros((128, E_ * (H_ // 128)), np.float32)
    for e in range(E_):
        for hm in range(H_ // 128):
            bfc_cols[:, e * 32 + hm] = bfc[e, hm * 128:(hm + 1) * 128]
    bpj = np.zeros((128, D), np.float32)
    bpj[0:E_, :] = inputs["L1_ffn_b_proj"]
    selh = np.zeros((E_, E_ * 128), np.float32)
    for e in range(E_):
        selh[e, e * 128:(e + 1) * 128] = 0.5
    import ml_dtypes
    return _with_ones({
        "rin": rin_fm,
        "rin2": rin_fm,
        "RWT": np.ascontiguousarray(inputs["L1_ffn_router_w"].T),
        "WFC": np.ascontiguousarray(inputs["L1_ffn_w_fc"].reshape(E_ * D, H_)),
        "WPJ": np.ascontiguousarray(inputs["L1_ffn_w_proj"].reshape(E_ * H_, D)),
        "BFC": bfc_cols,
        "BPJ": bpj.astype(ml_dtypes.bfloat16),
        "ones": np.ones((128, 128), np.float32),
        "ident": np.eye(128, dtype=np.float32),
        "SELH": selh,
    })



def emit_group_top4(p, prob_tm, comb_tm, sv):
    E_ = 16
    gmax = p.sbuf("g_m1", [128, 4], F32)
    gmax2 = p.sbuf("g_m2", [128, 4], F32)
    geq = p.sbuf("g_geq", [128, E_], F32)
    gsc = p.sbuf("g_gsc", [128, 4], F32)
    thr = p.sbuf("g_thr", [128, 1], F32)
    m1g = p.sbuf("g_m1g", [128, 1], F32)
    allow = p.sbuf("g_allow", [128, 4], F32)
    m16 = p.sbuf("g_m16", [128, E_], F32)
    mx8 = p.sbuf("g_mx8", [128, 8], F32)
    zap = p.sbuf("g_zap", [128, E_], F32)
    ind = p.sbuf("g_ind", [128, E_], F32)
    wsum = p.sbuf("g_ws", [128, 1], F32)
    for ti in range(16):
        sl = (ti * E_, (ti + 1) * E_)
        sc = lambda s, sl=sl: s[:, sl[0]:sl[1]]
        p.V(lambda e, d=gmax, s=prob_tm, sl=sl: e.tensor_reduce(
            d[:].rearrange("p (a b) -> p a b", b=1),
            s[:, sl[0]:sl[1]].rearrange("p (a b) -> p a b", b=4), AX.X, ALU.max))
        p.V.fence()
        p.V(lambda e, d=geq, s=prob_tm, g=gmax, sl=sl: e.tensor_tensor(
            d[:].rearrange("p (a b) -> p a b", b=4),
            s[:, sl[0]:sl[1]].rearrange("p (a b) -> p a b", b=4),
            g[:].rearrange("p (a b) -> p a b", b=1).to_broadcast([128, 4, 4]),
            ALU.is_ge))
        p.V.fence()
        p.V(lambda e, d=geq, s=prob_tm, sl=sl: e.scalar_tensor_tensor(
            d[:], d[:], -1e30, s[:, sl[0]:sl[1]], ALU.mult, ALU.add))
        p.V.fence()
        p.V(lambda e, d=gmax2, s=geq: e.tensor_reduce(
            d[:].rearrange("p (a b) -> p a b", b=1),
            s[:].rearrange("p (a b) -> p a b", b=4), AX.X, ALU.max))
        p.V.fence()
        p.V(lambda e, d=gsc, a=gmax, b=gmax2: e.tensor_add(d[:], a[:], b[:]))
        p.V.fence()
        p.V(lambda e, d=m1g, s=gsc: e.tensor_reduce(
            d[:].rearrange("p (a b) -> p a b", b=1),
            s[:].rearrange("p (a b) -> p a b", b=4), AX.X, ALU.max))
        p.V.fence()
        p.V(lambda e, d=allow, s=gsc, m=m1g: e.tensor_scalar(
            d[:], s[:], m[:, 0:1], None, ALU.is_ge))
        p.V.fence()
        p.V(lambda e, d=allow, s=gsc: e.scalar_tensor_tensor(
            d[:], d[:], -1e30, s[:], ALU.mult, ALU.add))
        p.V.fence()
        p.V(lambda e, d=thr, s=allow: e.tensor_reduce(
            d[:].rearrange("p (a b) -> p a b", b=1),
            s[:].rearrange("p (a b) -> p a b", b=4), AX.X, ALU.max))
        p.V.fence()
        p.V(lambda e, d=allow, s=gsc, m=thr: e.tensor_scalar(
            d[:], s[:], m[:, 0:1], None, ALU.is_ge))
        p.V.fence()
        p.V(lambda e, d=m16, s=prob_tm, a=allow, sl=sl: e.tensor_tensor(
            d[:].rearrange("p (a b) -> p a b", b=4),
            a[:].rearrange("p (a b) -> p a b", b=1).to_broadcast([128, 4, 4]),
            s[:, sl[0]:sl[1]].rearrange("p (a b) -> p a b", b=4), ALU.bypass)
            if False else
            e.scalar_tensor_tensor(
                d[:].rearrange("p (a b) -> p a b", b=4),
                a[:].rearrange("p (a b) -> p a b", b=1).to_broadcast([128, 4, 4]),
                10.0,
                s[:, sl[0]:sl[1]].rearrange("p (a b) -> p a b", b=4),
                ALU.mult, ALU.add))
        p.V.fence()
        p.V(lambda e, d=mx8, s=m16: e.max(out=d[:], in_=s[:]))
        p.V.fence()
        p.V(lambda e, d=mx8: e.memset(d[:, 4:8], -1e30))
        p.V.fence()
        p.V(lambda e, d=zap, s=m16, m=mx8: e.match_replace(
            out=d[:], in_to_replace=m[:], in_values=s[:], imm_value=0.0))
        p.V.fence()
        p.V(lambda e, d=m16, s=m16, z=zap: e.tensor_sub(d[:], s[:], z[:]))
        p.V.fence()
        p.V(lambda e, d=ind, s=m16: e.tensor_scalar(
            d[:], s[:], 5.0, None, ALU.is_gt))
        p.V.fence()
        p.V(lambda e, d=m16, i=ind, s=m16: e.scalar_tensor_tensor(
            d[:], i[:], -10.0, s[:], ALU.mult, ALU.add))
        p.V.fence()
        p.V(lambda e, d=wsum, s=m16: e.tensor_reduce(
            d[:].rearrange("p (a b) -> p a b", b=1),
            s[:].rearrange("p (a b) -> p a b", b=E_), AX.X, ALU.add))
        p.V.fence()
        p.V(lambda e, d=wsum: e.tensor_scalar_add(d[:], d[:], 1e-6))
        p.V.fence()
        p.V(lambda e, d=wsum: e.reciprocal(d[:], d[:]))
        p.V.fence()
        p.V(lambda e, d=comb_tm, s=m16, m=wsum, sl=sl: e.tensor_scalar(
            d[:, sl[0]:sl[1]], s[:], m[:, 0:1], None, ALU.mult))
        p.V.fence()
    p.V.fence(inc=(sv, 1))


def build_l0_ffn(phase=99, fz=None):
    p, _rin, _rout, _rin3, px, fused = _fz_head(fz, "l0_ffn")
    E_ = 16
    NE = 18
    F_ = 512
    FH = 256
    NFH = F_ // FH
    ld, st, sv, ss, tt, sg = std_sems(p)
    res = ln_prologue(p, ld, _rin)
    w_rt = p.dram_in(px + "RWT", [D, E_], F32)
    w_1 = p.dram_in(px + "W1CAT", [NE * D, F_], F32)
    w_3 = p.dram_in(px + "W3CAT", [NE * D, F_], F32)
    w_2 = p.dram_in(px + "W2CAT", [NE * F_, D], F32)
    ones = p.dram_in(px + "ones", [128, 128], F32)
    selh = p.dram_in(px + "SELH", [E_, E_ * 128], F32)

    ones_f = ones_root(p, ld)
    idf = ones_f_id(p, ld)
    rtf, _ = load_fm(p, "rt", w_rt, D, E_, ld, bf16=False)
    self_f = p.sbuf("self_f", [128, E_ * 128], F32)
    p.Y(lambda e, d=self_f, s=selh: e.dma_start(out=d[0:E_, :], in_=s), inc=(ld, 16))

    p.V.wait(ld)
    p.S.wait(ld)
    p.T.wait(ld)
    p.V.fence(inc=(sv, 1))

    xc_b = p.sbuf("xc_b", [128, DT * T], BF16)
    rstd = p.sbuf("rstd", [128, T], F32)
    sq = p.sbuf("sq", [128, DT * 512], F32)
    ps0, ps1 = p.psum("ps0", [128, 512]), p.psum("ps1", [128, 512])
    mean = emit_ln(p, res, xc_b, rstd, ps0, ps1, sq, ones_f, tt, sv, ss)

    lg_fm = p.sbuf("lg_fm", [128, T], F32)
    prob_tm = p.sbuf("prob_tm", [128, 16 * E_], F32)
    p.T.wait(sv)
    p.T.wait(ss)
    p.V.wait(ss)
    emit_router_tm(p, res, mean, rstd, rtf, E_, ps0, ps1, lg_fm, prob_tm,
                   idf, ones_f, tt, sv, ss, softmax=False)
    p.S.wait(sv)
    p.S(lambda e, d=prob_tm: e.activation(d[:], d[:], AF.Sigmoid))
    p.S.fence(inc=(ss, 1))
    p.V.wait(ss)
    comb_tm = p.sbuf("comb_tm", [128, 16 * E_], F32)
    emit_group_top4(p, prob_tm, comb_tm, sv)
    comb_fm = lg_fm
    emit_comb_fm(p, comb_tm, comb_fm, E_, idf, ps1, tt, sv)
    if phase == 1:
        rout1 = _rout if fused else p.dram_out("rout", [D, T], F32)
        p.G.wait(sv)
        p.G(lambda e, d=rout1, s=comb_fm: e.dma_start(out=d[0:128, :], in_=s[:]),
            inc=(st, 16))
        p.G.wait(st)
        return _fin(p, fused)

    y = res
    p.V.wait(sv)
    p.V(lambda e, d=y: e.memset(d[:], 0.0))
    p.V.fence(inc=(sv, 1))

    l2 = p.sem("l2")
    stage_1 = p.sbuf("stage_1", [128, DT * FH], F32)
    stage_3 = p.sbuf("stage_3", [128, DT * FH], F32)
    stage_2 = p.sbuf("stage_2", [128, (FH // 128) * D], F32)
    w1b = p.sbuf("w1b", [128, DT * FH], BF16)
    w3b = p.sbuf("w3b", [128, DT * FH], BF16)
    w2b = p.sbuf("w2b", [128, (FH // 128) * D], BF16)
    gblk = p.sbuf("gblk", [128, (FH // 128) * 512], F32)
    ublk = p.sbuf("ublk", [128, (FH // 128) * 512], F32)
    gub = p.sbuf("gub", [128, (FH // 128) * 512], BF16)
    cbr_b = p.sbuf("cbr_b", [128, T], BF16)
    pc = ps0
    ph = [p.psum("ph0", [128, 512]), ps1]
    banks = [p.psum(f"pm{i}", [128, 512]) for i in range(4)]
    evh = p.sem("evh")
    sgb = sg.n
    n_exp = 1 if phase == 3 else NE
    for e_i in range(n_exp):
        p.T.wait(sv)
        if e_i < 16:
            for c in range(NCH):
                p.T(lambda e, o=pc, s=self_f, cf=comb_fm, e_i=e_i, c=c, E_=E_:
                    e.matmul(o[:], s[0:E_, e_i * 128:(e_i + 1) * 128],
                             cf[0:E_, c * 512:c * 512 + 512], start=True, stop=True),
                    inc=(tt, 1))
                p.V.wait(tt)
                p.V(lambda e, d=cbr_b, s=pc, c=c: e.tensor_copy(
                    d[:, c * 512:c * 512 + 512], s[:]))
                p.V.fence(inc=(sv, 1))
                p.T.wait(sv)
        elif e_i == 16:
            p.V(lambda e, d=cbr_b: e.memset(d[:], 0.5))
            p.V.fence(inc=(sv, 1))
            p.T.wait(sv)
        for fh in range(NFH):
            it = e_i * NFH + fh
            if it > 0:
                p.Y.wait(sg, sgb + it)
                p.G.wait(sg, sgb + it)
            p.Y(lambda e, d=stage_1, s=w_1, e_i=e_i, fh=fh: e.dma_start(
                out=d[:].rearrange("p (k c) -> p k c", c=FH),
                in_=s.rearrange("(E k p) h -> p E k h", p=128, E=NE)[
                    :, e_i, :, fh * FH:(fh + 1) * FH]), inc=(l2, 16))
            p.Y(lambda e, d=stage_3, s=w_3, e_i=e_i, fh=fh: e.dma_start(
                out=d[:].rearrange("p (k c) -> p k c", c=FH),
                in_=s.rearrange("(E k p) h -> p E k h", p=128, E=NE)[
                    :, e_i, :, fh * FH:(fh + 1) * FH]), inc=(l2, 16))
            p.G(lambda e, d=stage_2, s=w_2, e_i=e_i, fh=fh: e.dma_start(
                out=d[:].rearrange("p (k c) -> p k c", c=D),
                in_=s.rearrange("(E k p) h -> p E k h", p=128, E=NE, k=F_ // 128)[
                    :, e_i, fh * (FH // 128):(fh + 1) * (FH // 128), :]),
                inc=(st, 16))
            p.V.wait(l2)
            p.V.wait(st)
            p.V(lambda e, d=w1b, s=stage_1: e.tensor_copy(d[:], s[:]))
            p.V(lambda e, d=w3b, s=stage_3: e.tensor_copy(d[:], s[:]))
            p.V(lambda e, d=w2b, s=stage_2: e.tensor_copy(d[:], s[:]))
            p.V.fence(inc=(sg, 1))
            p.T.wait(sg)
            for c in range(NCH):
                for hm in range(FH // 128):
                    bankg = ph[0]
                    banku = ph[1]
                    p.T.wait(evh)
                    for ki in range(DT):
                        p.T(lambda e, o=bankg, w=w1b, x=xc_b, ki=ki, hm=hm, c=c:
                            e.matmul(o[:], w[:, ki * FH + hm * 128: ki * FH + hm * 128 + 128],
                                     x[:, ki * T + c * 512: ki * T + c * 512 + 512],
                                     start=(ki == 0), stop=(ki == DT - 1)))
                    for ki in range(DT):
                        p.T(lambda e, o=banku, w=w3b, x=xc_b, ki=ki, hm=hm, c=c:
                            e.matmul(o[:], w[:, ki * FH + hm * 128: ki * FH + hm * 128 + 128],
                                     x[:, ki * T + c * 512: ki * T + c * 512 + 512],
                                     start=(ki == 0), stop=(ki == DT - 1)),
                            inc=(tt, 1) if ki == DT - 1 else None)
                    p.V.wait(tt)
                    p.V(lambda e, d=gblk, s=bankg, r=rstd, hm=hm, c=c: e.tensor_mul(
                        d[:, hm * 512: hm * 512 + 512], s[:],
                        r[:, c * 512: c * 512 + 512]))
                    p.V(lambda e, d=ublk, s=banku, r=rstd, hm=hm, c=c: e.tensor_mul(
                        d[:, hm * 512: hm * 512 + 512], s[:],
                        r[:, c * 512: c * 512 + 512]))
                    p.V.fence(inc=(sv, 1))
                    p.S.wait(sv)
                    p.S(lambda e, d=gblk, hm=hm: e.activation(
                        d[:, hm * 512: hm * 512 + 512], d[:, hm * 512: hm * 512 + 512],
                        AF.Silu))
                    p.S.fence(inc=(ss, 1))
                    p.V.wait(ss)
                    p.V(lambda e, d=gub, g=gblk, u=ublk, hm=hm: e.tensor_mul(
                        d[:, hm * 512: hm * 512 + 512], g[:, hm * 512: hm * 512 + 512],
                        u[:, hm * 512: hm * 512 + 512]))
                    p.V.fence()
                    p.V(lambda e, d=gub, cb=cbr_b, hm=hm, c=c: e.tensor_mul(
                        d[:, hm * 512: hm * 512 + 512], d[:, hm * 512: hm * 512 + 512],
                        cb[:, c * 512: c * 512 + 512]))
                    p.V.fence(inc=(evh, 1))
                    p.T.wait(evh)
                for mi in range(DT):
                    bank = banks[mi % 2]
                    for ki in range(FH // 128):
                        p.T(lambda e, o=bank, w=w2b, x=gub, ki=ki, mi=mi:
                            e.matmul(o[:], w[:, ki * D + mi * 128: ki * D + mi * 128 + 128],
                                     x[:, ki * 512: ki * 512 + 512],
                                     start=(ki == 0), stop=(ki == FH // 128 - 1)),
                            inc=(tt, 1) if ki == FH // 128 - 1 else None)
                    p.V.wait(tt)
                    p.V(lambda e, d=y, s=bank, mi=mi, c=c: e.tensor_add(
                        d[:, mi * T + c * 512: mi * T + c * 512 + 512],
                        d[:, mi * T + c * 512: mi * T + c * 512 + 512], s[:]))
                    p.V.fence(inc=(sv, 1))
                    p.T.wait(sv)
    rin2 = _rin if fused else p.dram_in("rin2", [D, T], F32)
    rout = _rout if fused else p.dram_out("rout", [D, T], F32)
    rld = sq
    l3 = p.sem("l3")
    for mi in range(DT):
        p.Y(lambda e, d=rld, s=rin2, mi=mi: e.dma_start(
            out=d[:, 0:T], in_=s.rearrange("(k p) t -> p k t", p=128)[:, mi, :]),
            inc=(l3, 16))
        p.V.wait(l3)
        p.V(lambda e, d=y, s=rld, mi=mi: e.tensor_add(
            d[:, mi * T:(mi + 1) * T], d[:, mi * T:(mi + 1) * T], s[:, 0:T]))
        p.V.fence(inc=(sv, 1))
        p.G.wait(sv)
        p.G(lambda e, d=rout, s=y, mi=mi: e.dma_start(
            out=d.rearrange("(k p) t -> p k t", p=128)[:, mi, :],
            in_=s[:, mi * T:(mi + 1) * T]), inc=(st, 16))
        p.Y.wait(sv, sv.n)
    p.G.wait(st)
    return _fin(p, fused)


def l0_ffn_inputs(inputs, core, rin_fm):
    E_, F_, NE = 16, 512, 18
    w1 = inputs["L0_ffn_w1_e"]
    w3 = inputs["L0_ffn_w3_e"]
    w2 = inputs["L0_ffn_w2_e"]
    w1s = inputs["L0_ffn_w1_s"]
    w3s = inputs["L0_ffn_w3_s"]
    w2s = inputs["L0_ffn_w2_s"]
    W1 = np.zeros((NE * D, F_), np.float32)
    W3 = np.zeros((NE * D, F_), np.float32)
    W2 = np.zeros((NE * F_, D), np.float32)
    for e in range(16):
        W1[e * D:(e + 1) * D] = w1[e].T
        W3[e * D:(e + 1) * D] = w3[e].T
        W2[e * F_:(e + 1) * F_] = w2[e].T
    w1sT, w3sT, w2sT = w1s.T, w3s.T, w2s.T
    W1[16 * D:17 * D] = w1sT[:, 0:512]
    W1[17 * D:18 * D] = w1sT[:, 512:1024]
    W3[16 * D:17 * D] = w3sT[:, 0:512]
    W3[17 * D:18 * D] = w3sT[:, 512:1024]
    W2[16 * F_:17 * F_] = w2sT[0:512]
    W2[17 * F_:18 * F_] = w2sT[512:1024]
    selh = np.zeros((E_, E_ * 128), np.float32)
    for e in range(E_):
        selh[e, e * 128:(e + 1) * 128] = 0.5
    return _with_ones({
        "rin": rin_fm,
        "rin2": rin_fm,
        "RWT": np.ascontiguousarray(inputs["L0_ffn_w_gate"].T),
        "W1CAT": W1, "W3CAT": W3, "W2CAT": W2,
        "ones": np.ones((128, 128), np.float32),
        "ident": np.eye(128, dtype=np.float32),
        "SELH": selh,
    })



LAMBDA_INIT = 0.8 - 0.6 * math.exp(-0.3 * 12)

def build_l1_attn(phase=99, fz=None):
    p, _rin, _rout, _rin3, px, fused = _fz_head(fz, "l1_attn")
    ld, st, sv, ss, tt, sg = std_sems(p)
    res = ln_prologue(p, ld, _rin)
    w_q = p.dram_in(px + "WQT", [D, D], F32)
    w_k = p.dram_in(px + "WKT", [D, D], F32)
    w_v = p.dram_in(px + "WVT", [D, D], F32)
    w_o = p.dram_in(px + "WOT", [D, D], F32)
    ones = p.dram_in(px + "ones", [128, 128], F32)
    lam_in = p.dram_in(px + "LAM", [1, 128], F32)
    gam_in = p.dram_in(px + "GAM", [1, 64], F32)
    cos_in = p.dram_in(px + "COS", [128, 256], F32)
    sin_in = p.dram_in(px + "SIN", [128, 256], F32)
    tril_in = p.dram_in(px + "TRILN", [128, 128], F32)

    ones_f = ones_root(p, ld)
    idf = ones_f_id(p, ld)
    lam = p.sbuf("lam", [1, 128], F32)
    p.Y(lambda e, d=lam, s=lam_in: e.dma_start(out=d[:], in_=s), inc=(ld, 16))
    gam = p.sbuf("gam", [1, 64], F32)
    p.Y(lambda e, d=gam, s=gam_in: e.dma_start(out=d[:], in_=s), inc=(ld, 16))
    cosb = p.sbuf("cosb", [128, 256], F32)
    p.Y(lambda e, d=cosb, s=cos_in: e.dma_start(out=d[:], in_=s), inc=(ld, 16))
    sinb = p.sbuf("sinb", [128, 256], F32)
    p.Y(lambda e, d=sinb, s=sin_in: e.dma_start(out=d[:], in_=s), inc=(ld, 16))
    trilb = p.sbuf("trilb", [128, 128], F32)
    p.Y(lambda e, d=trilb, s=tril_in: e.dma_start(out=d[:], in_=s), inc=(ld, 16))
    idf_b(p, ld)

    xc_b = p.sbuf("xc_b", [128, DT * T], BF16)
    rstd = p.sbuf("rstd", [128, T], F32)
    sq = p.sbuf("sq", [128, DT * 512], F32)
    ps0, ps1 = p.psum("ps0", [128, 512]), p.psum("ps1", [128, 512])
    pa, pb = p.psum("pa", [128, 512]), p.psum("pb", [128, 512])
    pt = p.psum("pt", [128, 512])
    po = p.psum("po", [128, 512])
    po2 = p.psum("po2", [128, 512])
    ptb = p.psum("ptb", [128, 1024], BF16)
    p.V.wait(ld)
    p.S.wait(ld)
    p.T.wait(ld)
    emit_ln(p, res, xc_b, rstd, ps0, ps1, sq, ones_f, tt, sv, ss)

    rstd_tm = p.sbuf("rstd_tm", [128, 16], F32)
    p.T.wait(ss)
    for tt_i in range(16):
        p.T(lambda e, o=pt, s=rstd, tt_i=tt_i, idf=idf: e.transpose(
            o[:, (tt_i % 4) * 128:(tt_i % 4) * 128 + 128],
            s[:, tt_i * 128:(tt_i + 1) * 128], idf[:]),
            inc=(tt, 1) if tt_i % 4 == 3 else None)
        if tt_i % 4 == 3:
            p.V.wait(tt)
            for j in range(tt_i - 3, tt_i + 1):
                p.V(lambda e, d=rstd_tm, o=pt, j=j: e.tensor_copy(
                    d[:, j:j + 1], o[:, (j % 4) * 128:(j % 4) * 128 + 1]))
            p.V.fence(inc=(sv, 1))
            p.T.wait(sv)

    lsc = p.sbuf("lsc", [1, 64], F32)
    negl = p.sbuf("negl", [128, 1], F32)
    gamb = p.sbuf("gamb", [128, 64], F32)
    p.V(lambda e, d=lsc, s=lam: e.tensor_tensor(
        d[0:1, 0:32], s[0:1, 0:32], s[0:1, 32:64], ALU.mult))
    p.V.fence()
    p.V(lambda e, d=lsc, s=lsc: e.tensor_reduce(
        d[0:1, 36:37], s[0:1, 0:32], AX.X, ALU.add))
    p.V.fence()
    p.V(lambda e, d=lsc, s=lam: e.tensor_tensor(
        d[0:1, 0:32], s[0:1, 64:96], s[0:1, 96:128], ALU.mult))
    p.V.fence()
    p.V(lambda e, d=lsc, s=lsc: e.tensor_reduce(
        d[0:1, 37:38], s[0:1, 0:32], AX.X, ALU.add))
    p.V.fence(inc=(sv, 1))
    p.S.wait(sv)
    p.S(lambda e, d=lsc: e.activation(d[0:1, 36:38], d[0:1, 36:38], AF.Exp))
    p.S.fence(inc=(ss, 1))
    p.V.wait(ss)
    p.V(lambda e, d=lsc: e.tensor_sub(d[0:1, 36:37], d[0:1, 37:38], d[0:1, 36:37]))
    p.V.fence()
    p.V(lambda e, d=lsc: e.tensor_scalar(
        d[0:1, 36:37], d[0:1, 36:37], LAMBDA_INIT, None, ALU.subtract))
    p.V.fence(inc=(sv, 1))
    p.T.wait(sv)
    p.T(lambda e, o=pt, w=ones_f, s=lsc: e.matmul(
        o[:, 0:1], w[0:1, :], s[0:1, 36:37], start=True, stop=True), inc=(tt, 1))
    p.T(lambda e, o=pt, w=ones_f, s=gam: e.matmul(
        o[:, 128:192], w[0:1, :], s[0:1, :], start=True, stop=True), inc=(tt, 1))
    p.V.wait(tt)
    p.V(lambda e, d=negl, s=pt: e.tensor_copy(d[:], s[:, 0:1]))
    p.V(lambda e, d=gamb, s=pt: e.tensor_scalar_mul(
        d[:], s[:, 128:192], 1.0 - LAMBDA_INIT))
    p.V.fence(inc=(sv, 1))

    resb = res[:].bitcast(BF16)
    QOF, KOF = 0, DT * T
    v_tm = p.sbuf("v_tm", [128, 16 * D], BF16)
    stage = p.sbuf("stage", [128, DT * D // 4], F32)
    wb = p.sbuf("wb", [128, DT * D], BF16)
    qsc = p.sbuf("qsc", [128, D], F32)
    t_A, t_B, t_C, t_D2 = (SqView(sq, i) for i in range(4))

    l2 = p.sem("l2")

    def load_w(whbm):
        for q4 in range(4):
            p.Y.wait(sg, sg.n)
            p.Y(lambda e, d=stage, s=whbm, q4=q4: e.dma_start(
                out=d[:].rearrange("p (k c) -> p k c", c=D),
                in_=s.rearrange("(k p) c -> p k c", p=128)[:, q4 * 2:(q4 + 1) * 2, :]),
                inc=(l2, 16))
            p.V.wait(l2)
            p.V(lambda e, d=wb, s=stage, q4=q4: e.tensor_copy(
                d[:, q4 * 2 * D:(q4 + 1) * 2 * D], s[:]))
            p.V.fence(inc=(sg, 1))
        p.T.wait(sg)

    SCL = 1.0 / math.sqrt(32.0)

    def proj_pass(which):
        for tt_i in range(16):
            for jc in range(2):
                bank = ps0 if jc == 0 else ps1
                for ki in range(DT):
                    p.T(lambda e, o=bank, x=xc_b, w=wb, ki=ki, jc=jc, tt_i=tt_i:
                        e.matmul(o[:], x[:, ki * T + tt_i * 128: ki * T + tt_i * 128 + 128],
                                 w[:, ki * D + jc * 512: ki * D + jc * 512 + 512],
                                 start=(ki == 0), stop=(ki == DT - 1)),
                        inc=(tt, 1) if ki == DT - 1 else None)
            p.V.wait(tt)
            if which == "v":
                for jc in range(2):
                    bank = ps0 if jc == 0 else ps1
                    p.V(lambda e, d=v_tm, s=bank, r=rstd_tm, tt_i=tt_i, jc=jc:
                        e.tensor_scalar(d[:, tt_i * D + jc * 512: tt_i * D + jc * 512 + 512],
                                        s[:], r[:, tt_i:tt_i + 1], None, ALU.mult))
                p.V.fence(inc=(sv, 1))
                p.T.wait(sv)
                continue
            for jc in range(2):
                bank = ps0 if jc == 0 else ps1
                p.V(lambda e, d=qsc, s=bank, r=rstd_tm, tt_i=tt_i, jc=jc:
                    e.tensor_scalar(d[:, jc * 512:jc * 512 + 512], s[:],
                                    r[:, tt_i:tt_i + 1], None, ALU.mult))
            p.V.fence(inc=(sv, 1))
            p.T.wait(sv)
            x1 = qsc[:].rearrange("p (a b) -> p a b", b=2)[:, :, 0].rearrange(
                "p (a b) -> p a b", b=16)
            x2 = qsc[:].rearrange("p (a b) -> p a b", b=2)[:, :, 1].rearrange(
                "p (a b) -> p a b", b=16)
            ca = cosb[:, tt_i * 16:(tt_i + 1) * 16].unsqueeze(1).to_broadcast(
                [128, 32, 16])
            sa = sinb[:, tt_i * 16:(tt_i + 1) * 16].unsqueeze(1).to_broadcast(
                [128, 32, 16])
            v3 = lambda t: t[:].rearrange("p (a b) -> p a b", b=16)
            p.V(lambda e, d=v3(t_A), x=x1, c=ca: e.tensor_tensor(d, x, c, ALU.mult))
            p.V(lambda e, d=v3(t_B), x=x1, c=sa: e.tensor_tensor(d, x, c, ALU.mult))
            p.V(lambda e, d=v3(t_C), x=x2, c=sa: e.tensor_tensor(d, x, c, ALU.mult))
            p.V(lambda e, d=v3(t_D2), x=x2, c=ca: e.tensor_tensor(d, x, c, ALU.mult))
            p.V.fence()
            p.V(lambda e, o=x1, a=v3(t_A), c=v3(t_C): e.tensor_tensor(
                o, a, c, ALU.subtract))
            p.V(lambda e, o=x2, b=v3(t_B), d2=v3(t_D2): e.tensor_tensor(
                o, b, d2, ALU.add))
            p.V.fence()
            if which == "q":
                p.V(lambda e, d=qsc: e.tensor_scalar_mul(d[:], d[:], SCL))
                p.V.fence()
            p.V.fence(inc=(sv, 1))
            p.T.wait(sv)
            base = QOF if which == "q" else KOF
            for blk in range(DT):
                p.T(lambda e, o=pt, s=qsc, blk=blk, idf=idf: e.transpose(
                    o[:, (blk % 4) * 128:(blk % 4) * 128 + 128],
                    s[:, blk * 128:(blk + 1) * 128], idf[:]),
                    inc=(tt, 1) if blk % 4 == 3 else None)
                if blk % 4 == 3:
                    p.V.wait(tt)
                    for j in range(blk - 3, blk + 1):
                        p.V(lambda e, d=resb, o=pt, j=j, tt_i=tt_i, base=base:
                            e.tensor_copy(
                                d[:, base + j * T + tt_i * 128: base + j * T + tt_i * 128 + 128],
                                o[:, (j % 4) * 128:(j % 4) * 128 + 128]))
                    p.V.fence(inc=(sv, 1))
                    p.T.wait(sv)

    load_w(w_q)
    proj_pass("q")
    load_w(w_k)
    proj_pass("k")
    load_w(w_v)
    proj_pass("v")

    if phase == 1:
        rout1 = _rout if fused else p.dram_out("rout", [D, T], F32)
        p.G.wait(sv)
        p.G(lambda e, d=rout1, s=res: e.dma_start(
            out=d.rearrange("(k p) t -> p k t", p=128),
            in_=s[:].rearrange("p (k t) -> p k t", t=T)), inc=(st, 16))
        p.G.wait(st)
        return _fin(p, fused)

    wv_tm = xc_b[:].bitcast(BF16)
    P_b = p.sbuf("P_b", [128, 2048], BF16)
    PT_b = p.sbuf("PT_b", [128, 2048], BF16)
    nmx = p.sbuf("nmx", [128, 4], F32)
    nm1 = p.sbuf("nm1", [128, 1], F32)
    den = p.sbuf("den", [128, 4], F32)
    rden = p.sbuf("rden", [128, 2], F32)
    wvt = qsc
    sc1 = p.sbuf("sc1", [128, 64], F32)
    rmsb = p.sbuf("rmsb", [128, 16], F32)
    sbanks = [ps0, ps1, pa, pb]
    for qt in range(16):
        nk = qt // 4 + 1
        klen = (qt + 1) * 128
        for h in range(16):
            for ch in range(2):
                row = (2 * h + ch) * 32
                jt, p0 = row // 128, row % 128
                p.T.wait(ss)
                for kc in range(nk):
                    cl = min(512, klen - kc * 512)
                    bank = sbanks[kc]
                    p.T(lambda e, o=bank, q=res, jt=jt, p0=p0, qt=qt, kc=kc, cl=cl:
                        e.matmul(o[:, 0:cl],
                                 q[:].bitcast(BF16)[p0:p0 + 32, QOF + jt * T + qt * 128: QOF + jt * T + qt * 128 + 128],
                                 q[:].bitcast(BF16)[p0:p0 + 32, KOF + jt * T + kc * 512: KOF + jt * T + kc * 512 + cl],
                                 start=True, stop=True,
                                 tile_position=(p0, 0)),
                        inc=(tt, 1))
                p.V.wait(tt)
                dk = qt * 128 - (nk - 1) * 512
                p.V(lambda e, o=sbanks[nk - 1], m=trilb, dk=dk: e.tensor_add(
                    o[:, dk:dk + 128], o[:, dk:dk + 128], m[:]))
                p.V.fence()
                for kc in range(nk):
                    cl = min(512, klen - kc * 512)
                    p.V(lambda e, d=nmx, s=sbanks[kc], kc=kc, cl=cl: e.reduce_max(
                        d[:, kc:kc + 1], s[:, 0:cl], AX.X, negate=True))
                p.V.fence()
                p.V(lambda e, d=nm1, s=nmx, nk=nk: e.tensor_reduce(
                    d[:], s[:, 0:nk], AX.X, ALU.min))
                p.V.fence(inc=(sv, 1))
                p.S.wait(sv)
                p.S.wait(tt)
                for kc in range(nk):
                    cl = min(512, klen - kc * 512)
                    p.S(lambda e, d=P_b, s=sbanks[kc], m=nm1, a=den, kc=kc, cl=cl:
                        e.activation(d[:, kc * 512:kc * 512 + cl], s[:, 0:cl],
                                     AF.Exp, bias=m[:, 0:1],
                                     accum_out=a[:, kc:kc + 1]))
                p.S.fence(inc=(ss, 1))
                p.V.wait(ss)
                p.V(lambda e, d=rden, s=den, nk=nk, ch=ch: e.tensor_reduce(
                    d[:, ch:ch + 1], s[:, 0:nk], AX.X, ALU.add))
                p.V.fence()
                p.V(lambda e, d=rden, ch=ch: e.reciprocal(
                    d[:, ch:ch + 1], d[:, ch:ch + 1]))
                p.V.fence(inc=(sv, 1))
                nkb = (klen + 127) // 128
                p.T.wait(sv)
                idb = idf_b(p, ld)
                for g4 in range((nkb + 3) // 4):
                    hi = min(g4 * 4 + 4, nkb)
                    for j in range(g4 * 4, hi):
                        p.T(lambda e, o=ptb, s=P_b, j=j, idb=idb: e.transpose(
                            o[:, (j % 4) * 128:(j % 4) * 128 + 128],
                            s[:, j * 128:(j + 1) * 128], idb[:]),
                            inc=(tt, 1) if (j % 4 == 3 or j == nkb - 1) else None)
                    p.V.wait(tt)
                    p.V(lambda e, d=PT_b, o=ptb, g4=g4, w=(hi - g4 * 4) * 128:
                        e.tensor_copy(d[:, g4 * 512:g4 * 512 + w], o[:, 0:w]))
                    p.V.fence(inc=(sv, 1))
                    p.T.wait(sv)
                for j in range(nkb):
                    p.T(lambda e, o=(po if ch == 0 else po2), s=PT_b, v=v_tm,
                        j=j, h=h:
                        e.matmul(o[:, 0:64],
                                 s[:, j * 128:(j + 1) * 128],
                                 v[:, j * D + h * 64: j * D + h * 64 + 64],
                                 start=(j == 0), stop=(j == nkb - 1),
                                 skip_group_check=True),
                        inc=(tt, 1) if j == nkb - 1 else None)
            p.V.wait(tt)
            p.V(lambda e, d=sc1, s=po, r=rden: e.tensor_scalar(
                d[:], s[:, 0:64], r[:, 0:1], None, ALU.mult))
            p.V.fence()
            p.V(lambda e, d=wvt, s=po2, r=rden, nl=negl, h=h: e.tensor_scalar(
                d[:, h * 64:h * 64 + 64], s[:, 0:64], r[:, 1:2], nl[:, 0:1],
                ALU.mult, ALU.mult))
            p.V.fence()
            p.V(lambda e, d=wvt, s=sc1, h=h: e.tensor_add(
                d[:, h * 64:h * 64 + 64], d[:, h * 64:h * 64 + 64], s[:]))
            p.V.fence(inc=(sv, 1))
            p.T.wait(sv)
        p.S.wait(sv)
        p.S(lambda e, d=sq, s=wvt: e.activation(d[:, 0:1024], s[:], AF.Square))
        p.S.fence(inc=(ss, 1))
        p.V.wait(ss)
        p.V(lambda e, d=rmsb, s=sq: e.tensor_reduce(
            d[:].rearrange("p (a b) -> p a b", b=1),
            s[:, 0:1024].rearrange("p (a b) -> p a b", b=64), AX.X, ALU.add))
        p.V.fence()
        p.V(lambda e, d=rmsb: e.tensor_scalar(
            d[:], d[:], 1.0 / 64.0, EPS, ALU.mult, ALU.add))
        p.V.fence()
        p.V(lambda e, d=rmsb: e.reciprocal(d[:], d[:]))
        p.V.fence(inc=(sv, 1))
        p.S.wait(sv)
        p.S(lambda e, d=rmsb: e.activation(d[:], d[:], AF.Sqrt))
        p.S.fence(inc=(ss, 1))
        p.V.wait(ss)
        p.V(lambda e, d=wvt, r=rmsb: e.tensor_tensor(
            d[:].rearrange("p (a b) -> p a b", b=64),
            d[:].rearrange("p (a b) -> p a b", b=64),
            r[:].rearrange("p (a b) -> p a b", b=1).to_broadcast([128, 16, 64]),
            ALU.mult))
        p.V.fence()
        p.V(lambda e, s=wvt, g=gamb: e.tensor_tensor(
            s[:].rearrange("p (a b) -> p a b", b=64),
            s[:].rearrange("p (a b) -> p a b", b=64),
            g[:].unsqueeze(1).to_broadcast([128, 16, 64]), ALU.mult))
        p.V.fence()
        p.V(lambda e, d=xc_b, s=wvt, qt=qt: e.tensor_copy(
            d[:, qt * 1024:(qt + 1) * 1024], s[:]))
        p.V.fence(inc=(sv, 1))

    load_w(w_o)
    wv_fm = resb
    for tt_i in range(16):
        p.T.wait(sv)
        idb = idf_b(p, ld)
        for blk in range(DT):
            p.T(lambda e, o=ptb, s=xc_b, blk=blk, tt_i=tt_i, idb=idb: e.transpose(
                o[:, (blk % 4) * 128:(blk % 4) * 128 + 128],
                s[:, tt_i * 1024 + blk * 128: tt_i * 1024 + blk * 128 + 128],
                idb[:]),
                inc=(tt, 1) if blk % 4 == 3 else None)
            if blk % 4 == 3:
                p.V.wait(tt)
                for j in range(blk - 3, blk + 1):
                    p.V(lambda e, d=wv_fm, o=ptb, j=j, tt_i=tt_i: e.tensor_copy(
                        d[:, j * T + tt_i * 128: j * T + tt_i * 128 + 128],
                        o[:, (j % 4) * 128:(j % 4) * 128 + 128]))
                p.V.fence(inc=(sv, 1))
                p.T.wait(sv)

    rin2 = _rin if fused else p.dram_in("rin2", [D, T], F32)
    rout = _rout if fused else p.dram_out("rout", [D, T], F32)
    rld = sq
    obuf = stage
    l3b = p.sem("l3b")
    for mi in range(DT):
        p.V.wait(st, st.n)
        p.Y(lambda e, d=rld, s=rin2, mi=mi: e.dma_start(
            out=d[:, 0:T], in_=s.rearrange("(k p) t -> p k t", p=128)[:, mi, :]),
            inc=(l3b, 16))
        for c in range(NCH):
            bank = sbanks[c]
            for ki in range(DT):
                p.T(lambda e, o=bank, w=wb, x=wv_fm, ki=ki, mi=mi, c=c:
                    e.matmul(o[:], w[:, ki * D + mi * 128: ki * D + mi * 128 + 128],
                             x[:, ki * T + c * 512: ki * T + c * 512 + 512],
                             start=(ki == 0), stop=(ki == DT - 1)),
                    inc=(tt, 1) if ki == DT - 1 else None)
        p.V.wait(tt)
        p.V.wait(l3b)
        for c in range(NCH):
            p.V(lambda e, d=obuf, s=sbanks[c], r=rld, c=c: e.tensor_add(
                d[:, c * 512:c * 512 + 512], s[:], r[:, c * 512:c * 512 + 512]))
        p.V.fence(inc=(sv, 1))
        p.G.wait(sv)
        p.G(lambda e, d=rout, s=obuf, mi=mi: e.dma_start(
            out=d.rearrange("(k p) t -> p k t", p=128)[:, mi, :], in_=s[:, 0:T]),
            inc=(st, 16))
        p.T.wait(sv)
        p.Y.wait(sv, sv.n)
    p.G.wait(st)
    return _fin(p, fused)


class SqView:
    def __init__(self, sb, i):
        self.sb, self.i = sb, i
    def __getitem__(self, key):
        assert key == slice(None)
        return self.sb[:, self.i * 512:(self.i + 1) * 512]


class Carve:
    def __init__(self, sb, off_f32, n, dt=F32):
        self.sb, self.off, self.n, self.dt = sb, off_f32, n, dt
    def __getitem__(self, key):
        ap = self.sb[:]
        if self.dt == BF16:
            ap = ap.bitcast(BF16)
            off = 2 * self.off
        else:
            off = self.off
        if key == slice(None):
            return ap[:, off:off + self.n]
        ps, cs = key
        start = cs.start or 0
        stop = cs.stop if cs.stop is not None else self.n
        return ap[ps, off + start: off + stop]


class CarveB:
    def __init__(self, sb, off, n):
        self.sb, self.off, self.n = sb, off, n
    def __getitem__(self, key):
        ap = self.sb[:].bitcast(F32)
        if key == slice(None):
            return ap[:, self.off:self.off + self.n]
        ps, cs = key
        start = cs.start or 0
        stop = cs.stop if cs.stop is not None else self.n
        return ap[:, self.off + start: self.off + stop]


def idf_b(p, ld):
    if not hasattr(p, "_idfb"):
        iden = p.dram_in("identb", [128, 128], BF16)
        idb = p.sbuf_root("idb", [128, 128], BF16)
        p.Y(lambda e, d=idb, s=iden: e.dma_start(out=d[:], in_=s), inc=(ld, 16))
        p._idfb = idb
    return p._idfb


def l1_attn_inputs(inputs, core, rin_fm):
    import ml_dtypes
    lam = np.zeros((1, 128), np.float32)
    lam[0, 0:32] = inputs["L1_attn_lambda_q1"]
    lam[0, 32:64] = inputs["L1_attn_lambda_k1"]
    lam[0, 64:96] = inputs["L1_attn_lambda_q2"]
    lam[0, 96:128] = inputs["L1_attn_lambda_k2"]
    inv = 1.0 / 10000.0 ** (np.arange(0, 32, 2, dtype=np.float32) / 32)
    t = np.arange(T, dtype=np.float32)
    fr = np.outer(t, inv)
    cos = np.cos(fr).reshape(16, 128, 16).transpose(1, 0, 2).reshape(128, 256)
    sin = np.sin(fr).reshape(16, 128, 16).transpose(1, 0, 2).reshape(128, 256)
    tril = np.where(np.arange(128)[:, None] >= np.arange(128)[None, :],
                    0.0, -1e30).astype(np.float32)
    return _with_ones({
        "rin": rin_fm, "rin2": rin_fm,
        "WQT": np.ascontiguousarray(inputs["L1_attn_Wq"].T),
        "WKT": np.ascontiguousarray(inputs["L1_attn_Wk"].T),
        "WVT": np.ascontiguousarray(inputs["L1_attn_Wv"].T),
        "WOT": np.ascontiguousarray(inputs["L1_attn_Wo"].T),
        "ones": np.ones((128, 128), np.float32),
        "ident": np.eye(128, dtype=np.float32),
        "identb": np.eye(128, dtype=np.float32).astype(ml_dtypes.bfloat16),
        "LAM": lam,
        "GAM": inputs["L1_attn_gamma"].reshape(1, 64).astype(np.float32),
        "COS": np.ascontiguousarray(cos), "SIN": np.ascontiguousarray(sin),
        "TRILN": tril,
    })



def build_l3_attn(phase=99, fz=None):
    p, _rin, _rout, _rin3, px, fused = _fz_head(fz, "l3_attn")
    TC = 256
    NTC = T // TC
    DI = 2048
    DP = DI // 128
    ld, st, sv, ss, tt, sg = std_sems(p)
    res = ln_prologue(p, ld, _rin)
    w_in = p.dram_in(px + "WIN", [D, 2 * DI], F32)
    w_out = p.dram_in(px + "WOUT", [DI, D], F32)
    w_x = p.dram_in(px + "WXP", [DI, 128], F32)
    w_dt = p.dram_in(px + "WDT", [128, DI], F32)
    ones = p.dram_in(px + "ones", [128, 128], F32)
    selj = p.dram_in(px + "SELJ", [128, 16 * 128], BF16)
    selbc = p.dram_in(px + "SELBC", [128, 2 * 128], F32)
    dmask = p.dram_in(px + "DMASK", [128, 16 * 128], BF16)
    acol_in = p.dram_in(px + "ACOL", [128, 256], F32)
    convw_in = p.dram_in(px + "CONVW", [128, DP * 4], F32)
    convb_in = p.dram_in(px + "CONVB", [128, DP], F32)
    dskip_in = p.dram_in(px + "DSKIP", [128, DP], F32)
    bdt_in = p.dram_in(px + "BDT", [128, DP], F32)
    lng_in = p.dram_in(px + "LNG", [128, DT], F32)
    lnb_in = p.dram_in(px + "LNB", [128, DT], F32)

    ones_f = ones_root(p, ld)
    selj_b = p.sbuf("selj_b", [128, 16 * 128], BF16)
    p.Y(lambda e, d=selj_b, s=selj: e.dma_start(out=d[:], in_=s), inc=(ld, 16))
    selbc_f = p.sbuf("selbc_f", [128, 2 * 128], F32)
    p.Y(lambda e, d=selbc_f, s=selbc: e.dma_start(out=d[:], in_=s), inc=(ld, 16))
    dmask_b = p.sbuf("dmask_b", [128, 16 * 128], BF16)
    p.Y(lambda e, d=dmask_b, s=dmask: e.dma_start(out=d[:], in_=s), inc=(ld, 16))
    acol = p.sbuf("acol", [128, 256], F32)
    p.Y(lambda e, d=acol, s=acol_in: e.dma_start(out=d[:], in_=s), inc=(ld, 16))
    convw = p.sbuf("convw", [128, DP * 4], F32)
    p.Y(lambda e, d=convw, s=convw_in: e.dma_start(out=d[:], in_=s), inc=(ld, 16))
    convb = p.sbuf("convb", [128, DP], F32)
    p.Y(lambda e, d=convb, s=convb_in: e.dma_start(out=d[:], in_=s), inc=(ld, 16))
    dskip = p.sbuf("dskip", [128, DP], F32)
    p.Y(lambda e, d=dskip, s=dskip_in: e.dma_start(out=d[:], in_=s), inc=(ld, 16))
    bdt = p.sbuf("bdt", [128, DP], F32)
    p.Y(lambda e, d=bdt, s=bdt_in: e.dma_start(out=d[:], in_=s), inc=(ld, 16))
    lng = p.sbuf("lng", [128, DT], F32)
    p.Y(lambda e, d=lng, s=lng_in: e.dma_start(out=d[:], in_=s), inc=(ld, 16))
    lnb = p.sbuf("lnb", [128, DT], F32)
    p.Y(lambda e, d=lnb, s=lnb_in: e.dma_start(out=d[:], in_=s), inc=(ld, 16))
    wx_f = p.sbuf("wx_f", [128, DP * 128], F32)
    p.Y(lambda e, d=wx_f, s=w_x.rearrange("(k p) c -> p k c", p=128):
        e.dma_start(out=d[:].rearrange("p (k c) -> p k c", c=128), in_=s),
        inc=(ld, 16))
    wdt_f = p.sbuf("wdt_f", [128, DI], F32)
    p.Y(lambda e, d=wdt_f, s=w_dt: e.dma_start(out=d[:], in_=s), inc=(ld, 16))

    p.V.wait(ld)
    p.S.wait(ld)
    p.T.wait(ld)
    wx_b = p.sbuf("wx_b", [128, DP * 128], BF16)
    cast_w(p, [(wx_f, wx_b)], "V")
    p.S(lambda e, d=acol: e.activation(d[:], d[:], AF.Exp))
    p.S.fence()
    p.V.fence(inc=(sv, 1))
    p.S(lambda e, d=acol: e.activation(d[:], d[:], AF.Identity, scale=-1.0))
    p.S.fence(inc=(ss, 1))

    xc_b = p.sbuf("xc_b", [128, DT * T], BF16)
    rstd = p.sbuf("rstd", [128, T], F32)
    sq = p.sbuf("sq", [128, DT * 512], F32)
    ps0, ps1 = p.psum("ps0", [128, 512]), p.psum("ps1", [128, 512])
    pj = p.psum("pj", [128, 512])
    pu = p.psum("pu", [128, 512])
    py = p.psum("py", [128, 512])
    pw = p.psum("pw", [128, 512])
    ln_mean = emit_ln(p, res, xc_b, rstd, ps0, ps1, sq, ones_f, tt, sv, ss)

    l2 = p.sem("l2")
    stage = p.sbuf("stage", [128, DT * 128], F32)
    stage2 = p.sbuf("stage2", [128, DP * 128], F32)
    wcb = p.sbuf("wcb", [128, DT * 128], BF16)
    wcb2 = p.sbuf("wcb2", [128, DP * 128], BF16)
    xpad = p.sbuf("xpad", [128, DP * (TC + 4)], F32)
    convs = Carve(res, 0, DP * TC, F32)
    convsb = Carve(res, 6144, DP * TC, BF16)
    zs = Carve(res, 4096, DP * TC, BF16)
    dtb = Carve(res, 8192, DP * TC, BF16)
    ub = Carve(res, 10240, DP * TC, BF16)
    dbls = p.sbuf("dbls", [128, TC], F32)
    brep = p.sbuf("brep", [128, TC], F32)
    crep = p.sbuf("crep", [128, TC], F32)
    dA = p.sbuf("dA", [128, TC], F32)
    uBs = p.sbuf("uBs", [128, TC], F32)
    hbuf = p.sbuf("hbuf", [128, TC], F32)
    hCb = p.sbuf("hCb", [128, TC], BF16)
    states = p.sbuf("states", [128, 256], F32)
    ytmp = p.sbuf("ytmp", [128, TC], F32)
    yb = Carve(res, 12288, DP * TC, BF16)
    obuf = Carve(res, 14336, DT * TC, F32)
    oms = p.sbuf("oms", [128, 2 * TC], F32)
    sc1 = p.sbuf("sc1", [128, TC], F32)
    sc2 = p.sbuf("sc2", [128, TC], F32)
    rldc = ln_mean
    rin2 = _rin if fused else p.dram_in("rin2", [D, T], F32)
    rout = _rout if fused else p.dram_out("rout", [D, T], F32)

    p.V.wait(ld)
    p.V(lambda e, d=states: e.memset(d[:], 0.0))
    p.V(lambda e, d=xpad: e.memset(d[:], 0.0))
    p.V.fence(inc=(sv, 1))

    XC3 = lambda buf, w: buf[:].rearrange("p (k c) -> p k c", c=w)

    p.V.wait(ss)
    p.T.wait(sv)
    p.T.wait(ss)

    ntc_run = 1 if phase in (1, 2, 3) else NTC
    for tc in range(ntc_run):
        t0 = tc * TC
        for mi in range(2 * DP):
            p.Y.wait(sg, sg.n)
            p.Y(lambda e, d=stage, s=w_in, mi=mi: e.dma_start(
                out=d[:].rearrange("p (k c) -> p k c", c=128),
                in_=s.rearrange("(k p) c -> p k c", p=128)[:, :, mi * 128:(mi + 1) * 128]),
                inc=(l2, 16))
            p.V.wait(l2)
            p.V(lambda e, d=wcb, s=stage: e.tensor_copy(d[:], s[:]))
            p.V.fence(inc=(sg, 1))
            p.T.wait(sg)
            p.T.wait(sv)
            for ki in range(DT):
                p.T(lambda e, o=pw, w=wcb, x=xc_b, ki=ki, t0=t0:
                    e.matmul(o[:, 0:TC], w[:, ki * 128:(ki + 1) * 128],
                             x[:, ki * T + t0: ki * T + t0 + TC],
                             start=(ki == 0), stop=(ki == DT - 1)),
                    inc=(tt, 1) if ki == DT - 1 else None)
            p.V.wait(tt)
            if mi < DP:
                p.V(lambda e, d=xpad, s=pw, r=rstd, mi=mi, t0=t0: e.tensor_mul(
                    d[:, mi * (TC + 4) + 3: mi * (TC + 4) + 3 + TC], s[:, 0:TC],
                    r[:, t0:t0 + TC]))
            else:
                p.V(lambda e, d=zs, s=pw, r=rstd, mz=mi - DP, t0=t0: e.tensor_mul(
                    d[:, mz * TC:mz * TC + TC], s[:, 0:TC],
                    r[:, t0:t0 + TC]))
            p.V.fence(inc=(sv, 1))
        for dp in range(DP):
            p.V(lambda e, d=convs, s=xpad, w=convw, dp=dp: e.tensor_scalar(
                d[:, dp * TC:(dp + 1) * TC],
                s[:, dp * (TC + 4) + 0: dp * (TC + 4) + TC],
                w[:, dp * 4 + 0: dp * 4 + 1], None, ALU.mult))
        p.V.fence()
        for k in range(1, 4):
            for dp in range(DP):
                p.V(lambda e, d=convs, s=xpad, w=convw, dp=dp, k=k:
                    e.scalar_tensor_tensor(
                        d[:, dp * TC:(dp + 1) * TC],
                        s[:, dp * (TC + 4) + k: dp * (TC + 4) + k + TC],
                        w[:, dp * 4 + k: dp * 4 + k + 1],
                        d[:, dp * TC:(dp + 1) * TC], ALU.mult, ALU.add))
            p.V.fence()
        p.V.fence(inc=(sv, 1))
        p.S.wait(sv)
        for dp in range(DP):
            p.S(lambda e, d=convs, b=convb, dp=dp: e.activation(
                d[:, dp * TC:(dp + 1) * TC], d[:, dp * TC:(dp + 1) * TC],
                AF.Silu, bias=b[:, dp:dp + 1]))
        p.S.fence(inc=(ss, 1))
        p.V.wait(ss)
        if phase == 1:
            p.V.fence(inc=(sv, 1))
            p.G.wait(sv)
            p.G(lambda e, d=rout, s=convs: e.dma_start(
                out=d[0:128, 0:2048], in_=s[:, 0:2048]), inc=(st, 16))
            p.G(lambda e, d=rout, s=convs: e.dma_start(
                out=d[128:256, 0:2048], in_=s[:, 2048:4096]), inc=(st, 16))
            p.G.wait(st)
            return _fin(p, fused)
        p.V(lambda e, d=convsb, s=convs: e.tensor_copy(d[:], s[:]))
        p.S(lambda e, d=zs: e.activation(d[:], d[:], AF.Silu))
        p.S.fence(inc=(ss, 1))
        p.V(lambda e, d=xpad: e.tensor_copy(
            XC3(d, TC + 4)[:, :, 0:3], XC3(xpad, TC + 4)[:, :, TC:TC + 3]))
        p.V.fence(inc=(sv, 1))
        p.T.wait(sv)
        for ki in range(DP):
            p.T(lambda e, o=pw, w=wx_b, x=convsb, ki=ki:
                e.matmul(o[:, 0:TC], w[:, ki * 128:(ki + 1) * 128],
                         x[:, ki * TC:(ki + 1) * TC],
                         start=(ki == 0), stop=(ki == DP - 1)),
                inc=(tt, 1) if ki == DP - 1 else None)
        p.V.wait(tt)
        p.V(lambda e, d=dbls, s=pw: e.tensor_copy(d[:], s[:, 0:TC]))
        p.V.fence(inc=(sv, 1))
        p.T.wait(sv)
        p.T(lambda e, o=pj, s=selbc_f, x=dbls: e.matmul(
            o[:, 0:TC], s[64:80, 0:128], x[64:80, :], start=True, stop=True,
            tile_position=(64, 0)), inc=(tt, 1))
        p.T(lambda e, o=pu, s=selbc_f, x=dbls: e.matmul(
            o[:, 0:TC], s[96:112, 128:256], x[96:112, :], start=True, stop=True,
            tile_position=(96, 0)), inc=(tt, 1))
        p.V.wait(tt)
        p.V(lambda e, d=brep, s=pj: e.tensor_copy(d[:], s[:, 0:TC]))
        p.V(lambda e, d=crep, s=pu: e.tensor_copy(d[:], s[:, 0:TC]))
        p.V.fence(inc=(sv, 1))
        p.T.wait(sv)
        for mi in range(DP):
            p.T.wait(ss)
            p.T(lambda e, o=pw, w=wdt_f, x=dbls, mi=mi: e.matmul(
                o[:, 0:TC], w[0:64, mi * 128:(mi + 1) * 128], x[0:64, :],
                start=True, stop=True), inc=(tt, 1))
            p.V.wait(tt)
            p.S.wait(sv)
            p.S.wait(tt)
            p.S(lambda e, d=sc1, s=pw, b=bdt, mi=mi: e.activation(
                d[:], s[:, 0:TC], AF.Abs, bias=b[:, mi:mi + 1]))
            p.S.fence()
            p.S(lambda e, d=sc1: e.activation(d[:], d[:], AF.Exp, scale=-1.0))
            p.S.fence()
            p.S(lambda e, d=sc1: e.activation(d[:], d[:], AF.Ln, bias=1.0))
            p.S.fence()
            p.S(lambda e, d=sc2, s=pw, b=bdt, mi=mi: e.activation(
                d[:], s[:, 0:TC], AF.Relu, bias=b[:, mi:mi + 1]))
            p.S.fence(inc=(ss, 1))
            p.V.wait(ss)
            p.V(lambda e, d=dtb, a=sc1, b=sc2, mi=mi: e.tensor_add(
                d[:, mi * TC:(mi + 1) * TC], a[:], b[:]))
            p.V.fence()
            p.V(lambda e, d=ub, s=dtb, c=convs, mi=mi: e.tensor_mul(
                d[:, mi * TC:(mi + 1) * TC], s[:, mi * TC:(mi + 1) * TC],
                c[:, mi * TC:(mi + 1) * TC]))
            p.V.fence(inc=(sv, 1))
            p.T.wait(sv)
        if phase == 2:
            p.G.wait(sv)
            p.G(lambda e, d=rout, s=res: e.dma_start(
                out=d[0:128, :], in_=s[:, 8192:10240]), inc=(st, 16))
            p.G.wait(st)
            return _fin(p, fused)
        ndp = 1 if phase == 6 else DP
        for dp in range(ndp):
            for j in range(16):
                stc = dp * 16 + j
                p.T.wait(sv)
                p.T(lambda e, o=pj, s=selj_b, x=dtb, j=j, dp=dp: e.matmul(
                    o[:, 0:TC], s[:, j * 128:(j + 1) * 128],
                    x[:, dp * TC:(dp + 1) * TC], start=True, stop=True),
                    inc=(tt, 1))
                p.T(lambda e, o=pu, s=selj_b, x=ub, j=j, dp=dp: e.matmul(
                    o[:, 0:TC], s[:, j * 128:(j + 1) * 128],
                    x[:, dp * TC:(dp + 1) * TC], start=True, stop=True),
                    inc=(tt, 1))
                p.V.wait(tt)
                p.V(lambda e, d=dA, s=pj, a=acol, stc=stc: e.tensor_scalar(
                    d[:], s[:, 0:TC], a[:, stc:stc + 1], None, ALU.mult))
                p.V(lambda e, d=uBs, s=pu, b=brep: e.tensor_mul(d[:], s[:, 0:TC], b[:]))
                p.V.fence(inc=(sv, 1))
                p.S.wait(sv)
                p.S(lambda e, d=dA: e.activation(d[:], d[:], AF.Exp))
                p.S.fence(inc=(ss, 1))
                p.V.wait(ss)
                p.V(lambda e, d=hbuf, a=dA, u=uBs, st=states, stc=stc:
                    e.tensor_tensor_scan(d[:], a[:], u[:], st[:, stc:stc + 1],
                                         ALU.mult, ALU.add))
                p.V.fence()
                p.V(lambda e, d=states, s=hbuf, stc=stc: e.tensor_copy(
                    d[:, stc:stc + 1], s[:, TC - 1:TC]))
                p.V(lambda e, d=hCb, s=hbuf, c=crep: e.tensor_mul(d[:], s[:], c[:]))
                p.V.fence(inc=(sv, 1))
                p.T.wait(sv)
                p.T(lambda e, o=py, s=dmask_b, x=hCb, j=j: e.matmul(
                    o[:, 0:TC], s[:, j * 128:(j + 1) * 128], x[:],
                    start=True, stop=True), inc=(tt, 1))
                p.V.wait(tt)
                if j == 0:
                    p.V(lambda e, d=ytmp, o=py: e.tensor_copy(d[:], o[:, 0:TC]))
                else:
                    p.V(lambda e, d=ytmp, o=py: e.tensor_add(d[:], d[:], o[:, 0:TC]))
                p.V.fence(inc=(sv, 1))
            p.V(lambda e, d=ytmp, c=convs, k=dskip, dp=dp:
                e.scalar_tensor_tensor(d[:], c[:, dp * TC:(dp + 1) * TC],
                                       k[:, dp:dp + 1], d[:], ALU.mult, ALU.add))
            p.V.fence()
            p.V(lambda e, d=yb, s=ytmp, z=zs, dp=dp: e.tensor_mul(
                d[:, dp * TC:(dp + 1) * TC], s[:], z[:, dp * TC:(dp + 1) * TC]))
            p.V.fence(inc=(sv, 1))
            p.T.wait(sv)
        if phase == 6:
            p.G.wait(sv)
            for i2, buf in enumerate(["dA", "uBs", "hbuf", "hCb", "ytmp"]):
                pass
            p.G(lambda e, d=rout, s=dA: e.dma_start(out=d[0:128, 0:256], in_=s[:]), inc=(st, 16))
            p.G(lambda e, d=rout, s=uBs: e.dma_start(out=d[0:128, 256:512], in_=s[:]), inc=(st, 16))
            p.G(lambda e, d=rout, s=hbuf: e.dma_start(out=d[0:128, 512:768], in_=s[:]), inc=(st, 16))
            p.G(lambda e, d=rout, s=ytmp: e.dma_start(out=d[0:128, 768:1024], in_=s[:]), inc=(st, 16))
            p.G(lambda e, d=rout, s=brep: e.dma_start(out=d[0:128, 1024:1280], in_=s[:]), inc=(st, 16))
            p.G(lambda e, d=rout, s=crep: e.dma_start(out=d[0:128, 1280:1536], in_=s[:]), inc=(st, 16))
            p.G(lambda e, d=rout, s=ytmp: e.dma_start(out=d[128:256, 0:256], in_=s[:]), inc=(st, 16))
            p.G(lambda e, d=rout, s=res: e.dma_start(out=d[128:256, 256:384], in_=s[:, 12288:12416]), inc=(st, 16))
            p.G(lambda e, d=rout, s=res: e.dma_start(out=d[128:256, 384:512], in_=s[:, 4096:4224]), inc=(st, 16))
            p.G.wait(st)
            return _fin(p, fused)
        if phase == 4:
            p.G.wait(sv)
            p.G(lambda e, d=rout, s=res: e.dma_start(
                out=d[0:128, :], in_=s[:, 12288:14336]), inc=(st, 16))
            p.G.wait(st)
            return _fin(p, fused)
        for mi in range(DT):
            p.Y.wait(sg, sg.n)
            p.Y(lambda e, d=stage2, s=w_out, mi=mi: e.dma_start(
                out=d[:].rearrange("p (k c) -> p k c", c=128),
                in_=s.rearrange("(k p) c -> p k c", p=128)[:, :, mi * 128:(mi + 1) * 128]),
                inc=(l2, 16))
            p.V.wait(l2)
            p.V(lambda e, d=wcb2, s=stage2: e.tensor_copy(d[:], s[:]))
            p.V.fence(inc=(sg, 1))
            p.T.wait(sg)
            p.T.wait(sv)
            for ki in range(DP):
                p.T(lambda e, o=pw, w=wcb2, x=yb, ki=ki:
                    e.matmul(o[:, 0:TC], w[:, ki * 128:(ki + 1) * 128],
                             x[:, ki * TC:(ki + 1) * TC],
                             start=(ki == 0), stop=(ki == DP - 1)),
                    inc=(tt, 1) if ki == DP - 1 else None)
            p.V.wait(tt)
            p.V(lambda e, d=obuf, s=pw, mi=mi: e.tensor_copy(
                d[:, mi * TC:(mi + 1) * TC], s[:, 0:TC]))
            p.V.fence(inc=(sv, 1))
        if phase == 5:
            p.G.wait(sv)
            p.G(lambda e, d=rout, s=res: e.dma_start(
                out=d[0:128, :], in_=s[:, 14336:16384]), inc=(st, 16))
            p.G.wait(st)
            return _fin(p, fused)
        p.T.wait(sv)
        p.S.wait(sv)
        p.S(lambda e, d=sq, s=obuf: e.activation(
            d[:, 0:DT * TC], s[:], AF.Square))
        p.S.fence(inc=(ss, 1))
        p.T.wait(ss)
        for di in range(DT):
            p.T(lambda e, o=pj, w=ones_f, s=obuf, di=di:
                e.matmul(o[:, 0:TC], w[:], s[:, di * TC:(di + 1) * TC],
                         start=(di == 0), stop=(di == DT - 1)))
        for di in range(DT):
            p.T(lambda e, o=pu, w=ones_f, s=sq, di=di:
                e.matmul(o[:, 0:TC], w[:], s[:, di * TC:(di + 1) * TC],
                         start=(di == 0), stop=(di == DT - 1)),
                inc=(tt, 1) if di == DT - 1 else None)
        p.V.wait(tt)
        p.V(lambda e, d=oms, s=pj: e.tensor_scalar_mul(d[:, 0:TC], s[:, 0:TC], 1.0 / D))
        p.V(lambda e, d=oms, s=pu: e.tensor_scalar_mul(d[:, TC:2 * TC], s[:, 0:TC], 1.0 / D))
        p.V.fence(inc=(sv, 1))
        p.S.wait(sv)
        p.S(lambda e, d=sc1, s=oms: e.activation(d[:], s[:, 0:TC], AF.Square))
        p.S.fence(inc=(ss, 1))
        p.V.wait(ss)
        p.V(lambda e, d=sc2, a=oms, b=sc1: e.tensor_sub(d[:], a[:, TC:2 * TC], b[:]))
        p.V.fence()
        p.V(lambda e, d=sc2: e.tensor_scalar_add(d[:], d[:], EPS))
        p.V.fence()
        p.V(lambda e, d=sc2: e.reciprocal(d[:], d[:]))
        p.V.fence(inc=(sv, 1))
        p.S.wait(sv)
        p.S(lambda e, d=sc2: e.activation(d[:], d[:], AF.Sqrt))
        p.S.fence(inc=(ss, 1))
        p.Y.wait(sg, sg.n)
        p.Y(lambda e, d=rldc, s=rin2, t0=t0: e.dma_start(
            out=d[:].rearrange("p (k c) -> p k c", c=TC),
            in_=s.rearrange("(k p) t -> p k t", p=128)[:, :, t0:t0 + TC]),
            inc=(l2, 16))
        p.V.wait(l2)
        p.V.wait(ss)
        for di in range(DT):
            p.V(lambda e, d=obuf, m=oms, di=di: e.tensor_sub(
                d[:, di * TC:(di + 1) * TC], d[:, di * TC:(di + 1) * TC],
                m[:, 0:TC]))
            p.V.fence()
            p.V(lambda e, d=obuf, r=sc2, di=di: e.tensor_mul(
                d[:, di * TC:(di + 1) * TC], d[:, di * TC:(di + 1) * TC], r[:]))
            p.V.fence()
            p.V(lambda e, d=obuf, g=lng, b2=lnb, di=di: e.tensor_scalar(
                d[:, di * TC:(di + 1) * TC], d[:, di * TC:(di + 1) * TC],
                g[:, di:di + 1], b2[:, di:di + 1], ALU.mult, ALU.add))
            p.V.fence()
            p.V(lambda e, d=rldc, s=obuf, di=di: e.tensor_add(
                d[:, di * TC:(di + 1) * TC], d[:, di * TC:(di + 1) * TC],
                s[:, di * TC:(di + 1) * TC]))
        p.V.fence(inc=(sv, 1))
        p.G.wait(sv)
        p.G(lambda e, d=rout, s=rldc, t0=t0: e.dma_start(
            out=d.rearrange("(k p) t -> p k t", p=128)[:, :, t0:t0 + TC],
            in_=s[:].rearrange("p (k c) -> p k c", c=TC)), inc=(st, 16))
        p.Y.wait(sv, sv.n)
        p.Y.wait(st, st.n)
        if phase == 3:
            break
    p.G.wait(st)
    return _fin(p, fused)


def l3_attn_inputs(inputs, core, rin_fm):
    import ml_dtypes
    DI, DP = 2048, 16
    wx = inputs["L3_attn_W_x"]
    wxp = np.zeros((DI, 128), np.float32)
    wxp[:, 0:64] = wx[:, 0:64]
    wxp[:, 64:80] = wx[:, 64:80]
    wxp[:, 96:112] = wx[:, 80:96]
    wdt = np.zeros((128, DI), np.float32)
    wdt[0:64] = inputs["L3_attn_W_dt"]
    selj = np.zeros((128, 16 * 128), np.float32)
    for j in range(16):
        for pp in range(128):
            selj[j * 8 + pp // 16, j * 128 + pp] = 1.0
    selbc = np.zeros((128, 256), np.float32)
    for pp in range(128):
        selbc[64 + pp % 16, pp] = 1.0
        selbc[96 + pp % 16, 128 + pp] = 1.0
    dmask = np.zeros((128, 16 * 128), np.float32)
    for j in range(16):
        for k in range(128):
            dmask[k, j * 128 + j * 8 + k // 16] = 1.0
    alog = inputs["L3_attn_A_log"]
    acolv = np.zeros((128, 256), np.float32)
    for dp in range(16):
        for j in range(16):
            for pp in range(128):
                acolv[pp, dp * 16 + j] = alog[dp * 128 + j * 8 + pp // 16, pp % 16]
    def cols(v, n=16):
        return np.ascontiguousarray(v.reshape(n, 128).T)
    cw = inputs["L3_attn_conv_w"]
    convw = np.zeros((128, DP * 4), np.float32)
    for dp in range(DP):
        convw[:, dp * 4:(dp + 1) * 4] = cw[dp * 128:(dp + 1) * 128, :]
    return _with_ones({
        "rin": rin_fm, "rin2": rin_fm,
        "WIN": np.ascontiguousarray(inputs["L3_attn_W_in"]),
        "WOUT": np.ascontiguousarray(inputs["L3_attn_W_out"]),
        "WXP": wxp, "WDT": wdt,
        "ones": np.ones((128, 128), np.float32),
        "SELJ": selj.astype(ml_dtypes.bfloat16),
        "SELBC": selbc,
        "DMASK": dmask.astype(ml_dtypes.bfloat16),
        "ACOL": acolv,
        "CONVW": convw,
        "CONVB": cols(inputs["L3_attn_conv_b"]),
        "DSKIP": cols(inputs["L3_attn_D_skip"]),
        "BDT": cols(inputs["L3_attn_b_dt"]),
        "LNG": cols(inputs["L3_attn_ln_g"], 8),
        "LNB": cols(inputs["L3_attn_ln_b"], 8),
    })



def build_l0_attn_wkv(phase=99, fz=None):
    p, _rin, _rout, _rin3, px, fused = _fz_head(fz, "l0_attn_wkv")
    ld, st, sv, ss, tt, sg = std_sems(p)
    res = ln_prologue(p, ld, _rin)
    w_k = p.dram_in(px + "WK", [D, D], F32)
    w_v = p.dram_in(px + "WV", [D, D], F32)
    w_r = p.dram_in(px + "WR", [D, D], F32)
    w_o = p.dram_in(px + "WO", [D, D], F32)
    ones = p.dram_in(px + "ones", [128, 128], F32)
    muk_in = p.dram_in(px + "MUK", [128, DT], F32)
    muv_in = p.dram_in(px + "MUV", [128, DT], F32)
    mur_in = p.dram_in(px + "MUR", [128, DT], F32)
    td_in = p.dram_in(px + "TDEC", [128, DT], F32)
    tf_in = p.dram_in(px + "TFIR", [128, DT], F32)
    g1_in = p.dram_in(px + "LN1G", [128, DT], F32)
    b1_in = p.dram_in(px + "LN1B", [128, DT], F32)
    rin2 = _rin if fused else p.dram_in("rin2", [D, T], F32)
    rout = _rout if fused else p.dram_out("rout", [D, T], F32)

    ones_f = ones_root(p, ld)
    cols = {}
    for nm, src2 in [("muk", muk_in), ("muv", muv_in), ("mur", mur_in),
                     ("tdec", td_in), ("tfir", tf_in), ("g1", g1_in), ("b1", b1_in)]:
        cb = p.sbuf("c_" + nm, [128, DT], F32)
        p.Y(lambda e, d=cb, s=src2: e.dma_start(out=d[:], in_=s), inc=(ld, 16))
        cols[nm] = cb

    rstd = p.sbuf("rstd", [128, T], F32)
    sq = p.sbuf("sq", [128, DT * 512], F32)
    ps0, ps1 = p.psum("ps0", [128, 512]), p.psum("ps1", [128, 512])
    banks = [p.psum(f"pm{i}", [128, 512]) for i in range(4)]
    p.V.wait(ld)
    p.S.wait(ld)
    p.T.wait(ld)
    mean = emit_ln(p, res, None, rstd, ps0, ps1, sq, ones_f, tt, sv, ss)
    p.S(lambda e, d=cols["tdec"]: e.activation(d[:], d[:], AF.Exp))
    p.S.fence()
    p.S(lambda e, d=cols["tdec"]: e.activation(d[:], d[:], AF.Identity, scale=-1.0))
    p.S.fence(inc=(ss, 1))

    p.S.wait(sv)
    p.S(lambda e, d=sq, s=rstd: e.activation(d[:, 0:T], s[:], AF.Square))
    p.S.fence(inc=(ss, 1))
    p.V.wait(ss)
    p.V(lambda e, d=sq: e.tensor_scalar(
        d[:, 0:T], d[:, 0:T], -EPS, 1.0 + EPS, ALU.mult, ALU.add))
    p.V.fence()
    p.V(lambda e, d=sq: e.reciprocal(d[:, 0:T], d[:, 0:T]))
    p.V.fence(inc=(sv, 1))
    p.S.wait(sv)
    p.S(lambda e, d=sq: e.activation(d[:, 0:T], d[:, 0:T], AF.Sqrt))
    p.S.fence(inc=(ss, 1))
    p.V.wait(ss)
    p.V(lambda e, d=sq, r=rstd: e.tensor_mul(d[:, 0:T], d[:, 0:T], r[:]))
    p.V.fence()
    xn_b = p.sbuf("xn_b", [128, DT * T], BF16)
    for di in range(DT):
        p.V(lambda e, d=res, m=mean, di=di: e.tensor_sub(
            d[:, di * T:(di + 1) * T], d[:, di * T:(di + 1) * T], m[:]))
        p.V.fence()
        p.V(lambda e, d=res, r=sq, di=di: e.tensor_mul(
            d[:, di * T:(di + 1) * T], d[:, di * T:(di + 1) * T], r[:, 0:T]))
        p.V.fence()
        p.V(lambda e, d=xn_b, s=res, g=cols["g1"], b=cols["b1"], di=di:
            e.tensor_scalar(
                d[:, di * T:(di + 1) * T], s[:, di * T:(di + 1) * T],
                g[:, di:di + 1], b[:, di:di + 1], ALU.mult, ALU.add))
        p.V.fence()
    p.V.fence(inc=(sv, 1))

    stage = p.sbuf("stagew", [128, 2048], F32)
    wb = Carve(sq, 0, DT * D, BF16)
    l2 = p.sem("l2")

    def load_w(whbm):
        for q4 in range(4):
            p.Y.wait(sg, sg.n)
            p.Y(lambda e, d=stage, s=whbm, q4=q4: e.dma_start(
                out=d[:].rearrange("p (k c) -> p k c", c=D),
                in_=s.rearrange("(k p) c -> p k c", p=128)[:, q4 * 2:(q4 + 1) * 2, :]),
                inc=(l2, 16))
            p.V.wait(l2)
            p.V(lambda e, d=wb, s=stage, q4=q4: e.tensor_copy(
                d[:, q4 * 2 * D:(q4 + 1) * 2 * D], s[:]))
            p.V.fence(inc=(sg, 1))
        p.T.wait(sg)

    xm_b = p.sbuf("xm_b", [128, DT * T], BF16)
    kf = Carve(res, 0, DT * T, BF16)
    vf = Carve(res, 8192, DT * T, BF16)
    rf = p.sbuf("rf", [128, DT * T], BF16)
    ddx = CarveB(rf, 0, T)

    def mix_into(mu_name):
        for di in range(DT):
            p.V(lambda e, d=ddx, s=xn_b, di=di: e.tensor_sub(
                d[:, 0:T - 1], s[:, di * T + 1: di * T + T],
                s[:, di * T: di * T + T - 1]))
            p.V.fence()
            p.V(lambda e, d=xm_b, s=xn_b, dd=ddx, m=cols[mu_name], di=di:
                e.scalar_tensor_tensor(
                    d[:, di * T + 1: di * T + T], dd[:, 0:T - 1], m[:, di:di + 1],
                    s[:, di * T: di * T + T - 1], ALU.mult, ALU.add))
            p.V(lambda e, d=xm_b, s=xn_b, m=cols[mu_name], di=di: e.tensor_scalar(
                d[:, di * T: di * T + 1], s[:, di * T: di * T + 1],
                m[:, di:di + 1], None, ALU.mult))
            p.V.fence()
        p.V.fence(inc=(sv, 1))
        p.T.wait(sv)

    def proj_into(outbuf, sigmoid=False):
        for mi in range(DT):
            for c in range(NCH):
                bank = banks[c]
                for ki in range(DT):
                    p.T(lambda e, o=bank, w=wb, x=xm_b, ki=ki, mi=mi, c=c:
                        e.matmul(o[:], w[:, ki * D + mi * 128: ki * D + mi * 128 + 128],
                                 x[:, ki * T + c * 512: ki * T + c * 512 + 512],
                                 start=(ki == 0), stop=(ki == DT - 1)),
                        inc=(tt, 1) if ki == DT - 1 else None)
            if sigmoid:
                p.S.wait(tt)
                for c in range(NCH):
                    p.S(lambda e, d=outbuf, s=banks[c], mi=mi, c=c: e.activation(
                        d[:, mi * T + c * 512: mi * T + c * 512 + 512], s[:],
                        AF.Sigmoid))
                p.S.fence(inc=(ss, 1))
                p.T.wait(ss)
            else:
                p.V.wait(tt)
                for c in range(NCH):
                    p.V(lambda e, d=outbuf, s=banks[c], mi=mi, c=c: e.tensor_copy(
                        d[:, mi * T + c * 512: mi * T + c * 512 + 512], s[:]))
                p.V.fence(inc=(sv, 1))
                p.T.wait(sv)

    p.Y.wait(sv, sv.n)
    mix_into("muk"); load_w(w_k); proj_into(kf)
    mix_into("muv"); load_w(w_v); proj_into(vf)
    mix_into("mur"); load_w(w_r); proj_into(rf, sigmoid=True)

    tmp = ([CarveB(xn_b, i * T, T) for i in range(4)] +
           [CarveB(xm_b, i * T, T) for i in range(4)])
    PP, XB, YB, EV, AA, BB, UK, WB2 = tmp
    p.V.wait(tt)
    p.V.wait(ss)
    p.S.wait(ss)
    for mi in range(DT):
        wcol = lambda: cols["tdec"][:, mi:mi + 1]
        ucol = lambda: cols["tfir"][:, mi:mi + 1]
        kap = kf[:, mi * T:(mi + 1) * T]
        vap = vf[:, mi * T:(mi + 1) * T]
        p.V(lambda e, d=WB2, w=wcol(): e.tensor_copy(
            d[:], w.to_broadcast([128, T])))
        p.V.fence()
        p.V(lambda e, d=PP: e.memset(d[:][:, 0:1], -1e30))
        p.V.fence()
        p.V(lambda e, d=PP, wbb=WB2, k=kap: e.tensor_tensor_scan(
            d[:][:, 1:T], wbb[:][:, 0:T - 1], k[:, 0:T - 1], -1e30,
            ALU.add, ALU.max))
        p.V.fence()
        p.V(lambda e, d=XB, s=PP, w=wcol(): e.tensor_scalar(
            d[:][:, 0:T - 1], s[:][:, 0:T - 1], w, None, ALU.add))
        p.V.fence()
        p.V(lambda e, d=XB, s=PP: e.tensor_sub(
            d[:][:, 0:T - 1], d[:][:, 0:T - 1], s[:][:, 1:T]))
        p.V(lambda e, d=YB, s=PP, k=kap: e.tensor_sub(
            d[:][:, 0:T - 1], k[:, 0:T - 1], s[:][:, 1:T]))
        p.V.fence(inc=(sv, 1))
        p.S.wait(sv)
        p.S(lambda e, d=XB: e.activation(d[:][:, 0:T - 1], d[:][:, 0:T - 1], AF.Exp))
        p.S(lambda e, d=YB: e.activation(d[:][:, 0:T - 1], d[:][:, 0:T - 1], AF.Exp))
        p.S.fence(inc=(ss, 1))
        p.V.wait(ss)
        p.V(lambda e, d=EV, y=YB, v=vap: e.tensor_mul(
            d[:][:, 0:T - 1], y[:][:, 0:T - 1], v[:, 0:T - 1]))
        p.V.fence()
        p.V(lambda e, d=AA: e.memset(d[:][:, 0:1], 0.0))
        p.V(lambda e, d=BB: e.memset(d[:][:, 0:1], 0.0))
        p.V.fence()
        p.V(lambda e, d=AA, x=XB, ev=EV: e.tensor_tensor_scan(
            d[:][:, 1:T], x[:][:, 0:T - 1], ev[:][:, 0:T - 1], 0.0,
            ALU.mult, ALU.add))
        p.V.fence()
        p.V(lambda e, d=BB, x=XB, y=YB: e.tensor_tensor_scan(
            d[:][:, 1:T], x[:][:, 0:T - 1], y[:][:, 0:T - 1], 0.0,
            ALU.mult, ALU.add))
        p.V.fence()
        p.V(lambda e, d=UK, k=kap, u=ucol(): e.tensor_scalar(
            d[:], k, u, None, ALU.add))
        p.V.fence()
        p.V(lambda e, d=XB, s=PP, uk=UK: e.tensor_tensor(
            d[:], s[:], uk[:], ALU.max))
        p.V.fence()
        p.V(lambda e, d=PP, q=XB: e.tensor_sub(d[:], d[:], q[:]))
        p.V(lambda e, d=UK, q=XB: e.tensor_sub(d[:], d[:], q[:]))
        p.V.fence(inc=(sv, 1))
        p.S.wait(sv)
        p.S(lambda e, d=PP: e.activation(d[:], d[:], AF.Exp))
        p.S(lambda e, d=UK: e.activation(d[:], d[:], AF.Exp))
        p.S.fence(inc=(ss, 1))
        p.V.wait(ss)
        p.V(lambda e, d=AA, e1=PP: e.tensor_mul(d[:], d[:], e1[:]))
        p.V(lambda e, d=BB, e1=PP: e.tensor_mul(d[:], d[:], e1[:]))
        p.V.fence()
        p.V(lambda e, d=EV, e2=UK, v=vap: e.tensor_mul(d[:], e2[:], v))
        p.V.fence()
        p.V(lambda e, d=AA, ev=EV: e.tensor_add(d[:], d[:], ev[:]))
        p.V(lambda e, d=BB, e2=UK: e.tensor_add(d[:], d[:], e2[:]))
        p.V.fence()
        p.V(lambda e, d=BB: e.reciprocal(d[:], d[:]))
        p.V.fence()
        p.V(lambda e, d=AA, b=BB: e.tensor_mul(d[:], d[:], b[:]))
        p.V.fence()
        p.V(lambda e, d=rf, a=AA, mi=mi: e.tensor_mul(
            d[:, mi * T:(mi + 1) * T], d[:, mi * T:(mi + 1) * T], a[:]))
        p.V.fence(inc=(sv, 1))

    load_w(w_o)
    l3 = p.sem("l3")
    p.G.wait(sv)
    for mi in range(DT):
        p.Y.wait(st, st.n)
        p.Y.wait(sg, sg.n)
        p.V.wait(st, st.n)
        p.Y(lambda e, d=stage, s=rin2, mi=mi: e.dma_start(
            out=d[:, 0:T], in_=s.rearrange("(k p) t -> p k t", p=128)[:, mi, :]),
            inc=(l3, 16))
        for c in range(NCH):
            bank = banks[c]
            for ki in range(DT):
                p.T(lambda e, o=bank, w=wb, x=rf, ki=ki, mi=mi, c=c:
                    e.matmul(o[:], w[:, ki * D + mi * 128: ki * D + mi * 128 + 128],
                             x[:, ki * T + c * 512: ki * T + c * 512 + 512],
                             start=(ki == 0), stop=(ki == DT - 1)),
                    inc=(tt, 1) if ki == DT - 1 else None)
        p.V.wait(tt)
        p.V.wait(l3)
        p.V(lambda e, d=stage, m=mean: e.tensor_sub(d[:, 0:T], d[:, 0:T], m[:]))
        p.V.fence()
        p.V(lambda e, d=stage, r=rstd: e.tensor_mul(d[:, 0:T], d[:, 0:T], r[:]))
        p.V.fence()
        for c in range(NCH):
            p.V(lambda e, d=stage, s=banks[c], c=c: e.tensor_add(
                d[:, c * 512:c * 512 + 512], d[:, c * 512:c * 512 + 512], s[:]))
        p.V.fence(inc=(sv, 1))
        p.G.wait(sv)
        p.G(lambda e, d=rout, s=stage, mi=mi: e.dma_start(
            out=d.rearrange("(k p) t -> p k t", p=128)[:, mi, :], in_=s[:, 0:T]),
            inc=(st, 16))
        p.T.wait(sv)
        p.Y.wait(sv, sv.n)
    p.G.wait(st)
    return _fin(p, fused)


def l0_attn_wkv_inputs(inputs, core, rin_fm):
    def cols(v, n=8):
        return np.ascontiguousarray(np.asarray(v).reshape(n, 128).T)
    return _with_ones({
        "rin": rin_fm, "rin2": rin_fm,
        "WK": np.ascontiguousarray(inputs["L0_attn_Wk"]),
        "WV": np.ascontiguousarray(inputs["L0_attn_Wv"]),
        "WR": np.ascontiguousarray(inputs["L0_attn_Wr"]),
        "WO": np.ascontiguousarray(inputs["L0_attn_Wo"]),
        "ones": np.ones((128, 128), np.float32),
        "MUK": cols(inputs["L0_attn_tm_mu_k"]),
        "MUV": cols(inputs["L0_attn_tm_mu_v"]),
        "MUR": cols(inputs["L0_attn_tm_mu_r"]),
        "TDEC": cols(inputs["L0_attn_time_decay"]),
        "TFIR": cols(inputs["L0_attn_time_first"]),
        "LN1G": cols(inputs["L0_attn_ln1_g"]),
        "LN1B": cols(inputs["L0_attn_ln1_b"]),
    })



def build_l0_attn_moe(phase=99, fz=None):
    p, _rin, _rout, _rin3, px, fused = _fz_head(fz, "l0_attn_moe")
    E_ = 8
    TC2 = 512
    NC2 = T // TC2
    ld, st, sv, ss, tt, sg = std_sems(p)
    res = ln_prologue(p, ld, _rin)
    w_rt = p.dram_in(px + "RWT", [D, E_], F32)
    w_ek = p.dram_in(px + "EWK", [E_ * D, 2048], F32)
    w_ev = p.dram_in(px + "EWV", [E_ * 2048, D], F32)
    w_er = p.dram_in(px + "EWR", [E_ * D, D], F32)
    ones = p.dram_in(px + "ones", [128, 128], F32)
    self1 = p.dram_in(px + "SELF1", [E_, E_ * 128], F32)
    cmk_in = p.dram_in(px + "CMK", [128, E_ * DT], F32)
    cmr_in = p.dram_in(px + "CMR", [128, E_ * DT], F32)
    g2_in = p.dram_in(px + "LN2G", [128, DT], F32)
    b2_in = p.dram_in(px + "LN2B", [128, DT], F32)
    rin3 = _rin3 if fused else p.dram_in("rin3", [D, T], F32)
    rout = _rout if fused else p.dram_out("rout", [D, T], F32)

    ones_f = ones_root(p, ld)
    idf = ones_f_id(p, ld)
    rtf, _ = load_fm(p, "rt", w_rt, D, E_, ld, bf16=False)
    self_f = p.sbuf("self_f", [128, E_ * 128], F32)
    p.Y(lambda e, d=self_f, s=self1: e.dma_start(out=d[0:E_, :], in_=s), inc=(ld, 16))
    cmk = p.sbuf("cmk", [128, E_ * DT], F32)
    p.Y(lambda e, d=cmk, s=cmk_in: e.dma_start(out=d[:], in_=s), inc=(ld, 16))
    cmr = p.sbuf("cmr", [128, E_ * DT], F32)
    p.Y(lambda e, d=cmr, s=cmr_in: e.dma_start(out=d[:], in_=s), inc=(ld, 16))
    g2c = p.sbuf("g2c", [128, DT], F32)
    p.Y(lambda e, d=g2c, s=g2_in: e.dma_start(out=d[:], in_=s), inc=(ld, 16))
    b2c = p.sbuf("b2c", [128, DT], F32)
    p.Y(lambda e, d=b2c, s=b2_in: e.dma_start(out=d[:], in_=s), inc=(ld, 16))

    xc_b = p.sbuf("xc_b", [128, DT * T], BF16)
    rstd = p.sbuf("rstd", [128, T], F32)
    sq = p.sbuf("sq", [128, DT * 512], F32)
    ps0, ps1 = p.psum("ps0", [128, 512]), p.psum("ps1", [128, 512])
    banks = [p.psum(f"pm{i}", [128, 512]) for i in range(4)]
    p.V.wait(ld)
    p.S.wait(ld)
    p.T.wait(ld)
    omk = p.sbuf("omk", [128, E_ * DT], F32)
    p.V(lambda e, d=omk, s=cmk: e.tensor_scalar(
        d[:], s[:], -1.0, 1.0, ALU.mult, ALU.add))
    omr = p.sbuf("omr", [128, E_ * DT], F32)
    p.V(lambda e, d=omr, s=cmr: e.tensor_scalar(
        d[:], s[:], -1.0, 1.0, ALU.mult, ALU.add))
    p.V.fence()
    mean = emit_ln(p, res, xc_b, rstd, ps0, ps1, sq, ones_f, tt, sv, ss)

    bias_c = p.sbuf("bias_c", [128, 1], F32)
    p.T.wait(sv)
    for ki in range(DT):
        p.T(lambda e, o=ps0, w=rtf, b=b2c, ki=ki, E_=E_: e.matmul(
            o[0:E_, 0:1], w[:, ki * E_:(ki + 1) * E_], b[:, ki:ki + 1],
            start=(ki == 0), stop=(ki == DT - 1)),
            inc=(tt, 1) if ki == DT - 1 else None)
    p.V.wait(tt)
    p.V(lambda e, d=bias_c, s=ps0, E_=E_: e.tensor_copy(d[0:E_, :], s[0:E_, 0:1]))
    for ki in range(DT):
        p.V(lambda e, d=rtf, g=g2c, ki=ki, E_=E_: e.tensor_scalar(
            d[:, ki * E_:(ki + 1) * E_], d[:, ki * E_:(ki + 1) * E_],
            g[:, ki:ki + 1], None, ALU.mult))
    p.V.fence(inc=(sv, 1))
    p.T.wait(sv)

    eip_b = p.sbuf("eip_b", [128, DT * (T + 1)], BF16)
    p.V.wait(ss)
    for di in range(DT):
        p.V(lambda e, d=sq, s=xc_b, r=rstd, di=di: e.tensor_mul(
            d[:, 0:T], s[:, di * T:(di + 1) * T], r[:]))
        p.V.fence()
        p.V(lambda e, d=eip_b, s=sq, g=g2c, b=b2c, di=di: e.tensor_scalar(
            d[:, di * (T + 1) + 1: di * (T + 1) + 1 + T], s[:, 0:T],
            g[:, di:di + 1], b[:, di:di + 1], ALU.mult, ALU.add))
        p.V(lambda e, d=eip_b, di=di: e.memset(
            d[:, di * (T + 1): di * (T + 1) + 1], 0.0))
        p.V.fence()
    p.V.fence(inc=(sv, 1))

    lg_fm = p.sbuf("lg_fm", [128, T], F32)
    prob_tm = p.sbuf("prob_tm", [128, 16 * E_], F32)
    p.T.wait(sv)
    emit_router_tm(p, res, mean, rstd, rtf, E_, ps0, ps1, lg_fm, prob_tm,
                   idf, ones_f, tt, sv, ss, softmax=True, bias_col=bias_c)
    comb_tm = p.sbuf("comb_tm", [128, 16 * E_], F32)
    emit_top2_combine(p, prob_tm, comb_tm, E_, sv)
    comb_fm = lg_fm
    emit_comb_fm(p, comb_tm, comb_fm, E_, idf, ps1, tt, sv)
    if phase == 1:
        p.G.wait(sv)
        p.G(lambda e, d=rout, s=comb_fm: e.dma_start(out=d[0:128, :], in_=s[:]),
            inc=(st, 16))
        p.G.wait(st)
        return _fin(p, fused)

    kbuf = Carve(xc_b, 0, 16 * T, BF16)
    rbuf = Carve(rstd, 0, 2 * T, BF16)
    twb = Carve(mean, 0, TC2, BF16)
    tbuf = SqView(mean, 1)
    tbuf = None
    stage = sq
    ewk2 = p.sbuf("ewk2", [128, DT * 1024], BF16)
    kvh = p.sbuf("kvh", [128, DT * 512], BF16)
    relu_s = Carve(mean, 512, 512, F32)
    tb2 = Carve(mean, 1024, 512, F32)
    l2 = p.sem("l2")
    ewkA = Carve(ewk2, 0, DT * 512, BF16)
    ewkB = Carve(ewk2, 4096 if False else 2048, DT * 512, BF16)

    def scaled_cast(mu, om, e_i):
        p.V.wait(tt, tt.n)
        for ki in range(DT):
            p.V(lambda e, d=ewkA, s=stage, m=mu, ki=ki, e_i=e_i:
                e.tensor_scalar(d[:, ki * 512:(ki + 1) * 512],
                                s[:, ki * 512:(ki + 1) * 512],
                                m[:, e_i * DT + ki: e_i * DT + ki + 1],
                                None, ALU.mult))
            p.V(lambda e, d=ewkB, s=stage, m=om, ki=ki, e_i=e_i:
                e.tensor_scalar(d[:, ki * 512:(ki + 1) * 512],
                                s[:, ki * 512:(ki + 1) * 512],
                                m[:, e_i * DT + ki: e_i * DT + ki + 1],
                                None, ALU.mult))
        p.V.fence(inc=(sg, 1))
        p.T.wait(sg)

    def mix_mm(bank, fm, t0, cc):
        for ki in range(DT):
            p.T(lambda e, o=bank, w=ewkA, x=eip_b, ki=ki, fm=fm, t0=t0, cc=cc:
                e.matmul(o[:], w[:, ki * 512 + fm * 128: ki * 512 + fm * 128 + 128],
                         x[:, ki * (T + 1) + 1 + t0 + cc * 512:
                            ki * (T + 1) + 1 + t0 + cc * 512 + 512],
                         start=(ki == 0), stop=False))
        for ki in range(DT):
            p.T(lambda e, o=bank, w=ewkB, x=eip_b, ki=ki, fm=fm, t0=t0, cc=cc:
                e.matmul(o[:], w[:, ki * 512 + fm * 128: ki * 512 + fm * 128 + 128],
                         x[:, ki * (T + 1) + t0 + cc * 512:
                            ki * (T + 1) + t0 + cc * 512 + 512],
                         start=False, stop=(ki == DT - 1)),
                inc=(tt, 1) if ki == DT - 1 else None)

    p.Y.wait(sv, sv.n)
    for c2 in range(NC2):
        t0 = c2 * TC2
        for e_i in range(E_):
            p.T.wait(sv)
            for cc in range(TC2 // 512):
                p.T(lambda e, o=ps0, s=self_f, cf=comb_fm, e_i=e_i, t0=t0, cc=cc:
                    e.matmul(o[:], s[0:E_, e_i * 128:(e_i + 1) * 128],
                             cf[0:E_, t0 + cc * 512: t0 + cc * 512 + 512],
                             start=True, stop=True), inc=(tt, 1))
                p.V.wait(tt)
                p.V(lambda e, d=twb, s=ps0, cc=cc: e.tensor_copy(
                    d[:, cc * 512:cc * 512 + 512], s[:]))
                p.V.fence(inc=(sv, 1))
                p.T.wait(sv)
            for fc in range(4):
                p.Y.wait(sg, sg.n)
                p.Y(lambda e, d=stage, s=w_ek, e_i=e_i, fc=fc: e.dma_start(
                    out=d[:].rearrange("p (k c) -> p k c", c=512),
                    in_=s.rearrange("(E k p) f -> p E k f", p=128, E=E_)[
                        :, e_i, :, fc * 512:(fc + 1) * 512]), inc=(l2, 16))
                p.V.wait(l2)
                scaled_cast(cmk, omk, e_i)
                for fm in range(4):
                    for cc in range(TC2 // 512):
                        bank = banks[cc]
                        mix_mm(bank, fm, t0, cc)
                        p.S.wait(tt)
                        p.S(lambda e, d=relu_s, s=bank: e.activation(
                            d[:], s[:], AF.Relu))
                        p.S.fence()
                        p.S(lambda e, d=kbuf, s=relu_s, fc=fc, fm=fm, cc=cc:
                            e.activation(d[:, (fc * 4 + fm) * TC2 + cc * 512:
                                           (fc * 4 + fm) * TC2 + cc * 512 + 512],
                                         s[:], AF.Square))
                        p.S.fence(inc=(ss, 1))
                        p.T.wait(ss)
            for rh in range(2):
                p.Y.wait(sg, sg.n)
                p.Y(lambda e, d=stage, s=w_er, e_i=e_i, rh=rh: e.dma_start(
                    out=d[:].rearrange("p (k c) -> p k c", c=512),
                    in_=s.rearrange("(E k p) f -> p E k f", p=128, E=E_)[
                        :, e_i, :, rh * 512:(rh + 1) * 512]), inc=(l2, 16))
                p.V.wait(l2)
                scaled_cast(cmr, omr, e_i)
                for fm in range(4):
                    mi = rh * 4 + fm
                    for cc in range(TC2 // 512):
                        bank = banks[cc]
                        mix_mm(bank, fm, t0, cc)
                        p.S.wait(tt)
                        p.S(lambda e, d=rbuf, s=bank, mi=mi, cc=cc: e.activation(
                            d[:, mi * TC2 + cc * 512: mi * TC2 + cc * 512 + 512],
                            s[:], AF.Sigmoid))
                        p.S.fence(inc=(ss, 1))
                        p.T.wait(ss)
            for h2 in range(2):
                for q in range(2):
                    p.Y.wait(sg, sg.n)
                    p.Y(lambda e, d=stage, s=w_ev, e_i=e_i, q=2 * h2 + q:
                        e.dma_start(
                            out=d[:].rearrange("p (k c) -> p k c", c=D),
                            in_=s.rearrange("(E k p) f -> p E k f", p=128,
                                            E=E_, k=16)[
                                :, e_i, q * 4:(q + 1) * 4, :]), inc=(l2, 16))
                    p.V.wait(l2)
                    p.V.wait(tt, tt.n)
                    p.V(lambda e, d=ewk2, s=stage, qo=(q % 2) * 4 * D:
                        e.tensor_copy(d[:, qo:qo + 4 * D], s[:, 0:4 * D]))
                    p.V.fence(inc=(sg, 1))
                p.T.wait(sg)
                for mi in range(DT):
                    for cc in range(TC2 // 512):
                        bank = banks[2 + (cc % 2)]
                        for ki in range(8):
                            kt = h2 * 8 + ki
                            p.T(lambda e, o=bank, w=ewk2, x=kbuf, ki=ki, kt=kt,
                                mi=mi, t0=t0, cc=cc: e.matmul(
                                    o[:], w[:, ki * D + mi * 128: ki * D + mi * 128 + 128],
                                    x[:, kt * TC2 + cc * 512: kt * TC2 + cc * 512 + 512],
                                    start=(ki == 0), stop=(ki == 7)),
                                inc=(tt, 1) if ki == 7 else None)
                        p.V.wait(tt)
                        if h2 == 0:
                            p.V(lambda e, d=kvh, s=bank, mi=mi, cc=cc:
                                e.tensor_copy(
                                    d[:, mi * TC2 + cc * 512: mi * TC2 + cc * 512 + 512],
                                    s[:]))
                            p.V.fence(inc=(sv, 1))
                            p.T.wait(sv)
                            continue
                        p.V(lambda e, d=tb2, s=bank, k=kvh, mi=mi, cc=cc:
                            e.tensor_add(d[:], s[:],
                                         k[:, mi * TC2 + cc * 512: mi * TC2 + cc * 512 + 512]))
                        p.V.fence()
                        p.V(lambda e, d=tb2, r=rbuf, mi=mi, cc=cc: e.tensor_mul(
                            d[:], d[:],
                            r[:, mi * TC2 + cc * 512: mi * TC2 + cc * 512 + 512]))
                        p.V.fence()
                        p.V(lambda e, d=tb2, w=twb, cc=cc: e.tensor_mul(
                            d[:], d[:], w[:, cc * 512:cc * 512 + 512]))
                        p.V.fence()
                        p.V(lambda e, d=res, s=tb2, mi=mi, t0=t0, cc=cc:
                            e.tensor_add(
                                d[:, mi * T + t0 + cc * 512: mi * T + t0 + cc * 512 + 512],
                                d[:, mi * T + t0 + cc * 512: mi * T + t0 + cc * 512 + 512],
                                s[:]))
                        p.V.fence(inc=(sv, 1))
                        p.T.wait(sv)
    l3 = p.sem("l3")
    rl3 = sq
    p.Y.wait(sv, sv.n)
    p.Y.wait(sg, sg.n)
    for mi in range(DT):
        p.Y(lambda e, d=rl3, s=rin3, mi=mi: e.dma_start(
            out=d[:, 0:T], in_=s.rearrange("(k p) t -> p k t", p=128)[:, mi, :]),
            inc=(l3, 16))
        p.V.wait(l3)
        p.V(lambda e, d=res, s=rl3, mi=mi: e.tensor_add(
            d[:, mi * T:(mi + 1) * T], d[:, mi * T:(mi + 1) * T], s[:, 0:T]))
        p.V.fence(inc=(sv, 1))
        p.G.wait(sv)
        p.G(lambda e, d=rout, s=res, mi=mi: e.dma_start(
            out=d.rearrange("(k p) t -> p k t", p=128)[:, mi, :],
            in_=s[:, mi * T:(mi + 1) * T]), inc=(st, 16))
        p.Y.wait(sv, sv.n)
        p.Y.wait(st, st.n)
    p.G.wait(st)
    return _fin(p, fused)


def l0_attn_moe_inputs(inputs, core, rin_fm, rin3_fm=None):
    E_ = 8
    def cols(v, n=8):
        return np.ascontiguousarray(np.asarray(v).reshape(n, 128).T)
    cmk = np.zeros((128, E_ * DT), np.float32)
    cmr = np.zeros((128, E_ * DT), np.float32)
    for e in range(E_):
        cmk[:, e * DT:(e + 1) * DT] = cols(inputs["L0_attn_cm_mu_k"][e])
        cmr[:, e * DT:(e + 1) * DT] = cols(inputs["L0_attn_cm_mu_r"][e])
    self1 = np.zeros((E_, E_ * 128), np.float32)
    for e in range(E_):
        self1[e, e * 128:(e + 1) * 128] = 1.0
    return _with_ones({
        "rin": rin_fm,
        "RWT": np.ascontiguousarray(inputs["L0_attn_router_w"]),
        "EWK": np.ascontiguousarray(inputs["L0_attn_eWk"].reshape(E_ * D, 2048)),
        "EWV": np.ascontiguousarray(inputs["L0_attn_eWv"].reshape(E_ * 2048, D)),
        "EWR": np.ascontiguousarray(inputs["L0_attn_eWr"].reshape(E_ * D, D)),
        "ones": np.ones((128, 128), np.float32),
        "ident": np.eye(128, dtype=np.float32),
        "SELF1": self1,
        "CMK": cmk, "CMR": cmr,
        "LN2G": cols(inputs["L0_attn_ln2_g"]),
        "LN2B": cols(inputs["L0_attn_ln2_b"]),
        "rin3": rin3_fm,
    })



def build_l3_ffn(phase=99, fz=None):
    p, _rin, _rout, _rin3, px, fused = _fz_head(fz, "l3_ffn")
    E_ = 8
    CAP = 1024
    CH = 128
    TC3 = 256
    ld, st, sv, ss, tt, sg = std_sems(p)
    res = ln_prologue(p, ld, _rin)
    w_g = p.dram_in(px + "WG", [D, E_], F32)
    w_q = p.dram_in(px + "WQ", [D, D], F32)
    w_k = p.dram_in(px + "WKE", [E_ * D, D], F32)
    w_v = p.dram_in(px + "WVE", [E_ * D, D], F32)
    w_f = p.dram_in(px + "WFE", [E_ * D, 256], F32)
    w_o = p.dram_in(px + "WOT", [D, D], F32)
    ones = p.dram_in(px + "ones", [128, 128], F32)
    self1 = p.dram_in(px + "SELF1", [E_, E_ * 128], F32)
    nw_in = p.dram_in(px + "NRMW", [128, DT], F32)
    gw_in = p.dram_in(px + "GNRMW", [128, DT], F32)
    triu_in = p.dram_in(px + "TRIU01", [128, 128], BF16)
    rin2 = _rin if fused else p.dram_in("rin2", [D, T], F32)
    rout = _rout if fused else p.dram_out("rout", [D, T], F32)

    ones_f = ones_root(p, ld)
    idf = ones_f_id(p, ld)
    idb = idf_b(p, ld)
    rtf, _ = load_fm(p, "rt", w_g, D, E_, ld, bf16=False)
    self_f = p.sbuf("self_f", [128, E_ * 128], F32)
    p.Y(lambda e, d=self_f, s=self1: e.dma_start(out=d[0:E_, :], in_=s), inc=(ld, 16))
    nwc = p.sbuf("nwc", [128, DT], F32)
    p.Y(lambda e, d=nwc, s=nw_in: e.dma_start(out=d[:], in_=s), inc=(ld, 16))
    gwc = p.sbuf("gwc", [128, DT], F32)
    p.Y(lambda e, d=gwc, s=gw_in: e.dma_start(out=d[:], in_=s), inc=(ld, 16))
    triub = p.sbuf("triub", [128, 128], BF16)
    p.Y(lambda e, d=triub, s=triu_in: e.dma_start(out=d[:], in_=s), inc=(ld, 16))

    xc_b = p.sbuf("xc_b", [128, DT * T], BF16)
    rstd = p.sbuf("rstd", [128, T], F32)
    sq = p.sbuf("sq", [128, DT * 512], F32)
    ps0, ps1 = p.psum("ps0", [128, 512]), p.psum("ps1", [128, 512])
    pm0, pm1 = p.psum("pm0", [128, 512]), p.psum("pm1", [128, 512])
    pm2, pm3 = p.psum("pm2", [128, 512]), p.psum("pm3", [128, 512])
    ptb3 = p.psum("ptb3", [128, 1024], BF16)
    banks = [pm0, pm1, pm2, pm3]
    p.V.wait(ld)
    p.S.wait(ld)
    p.T.wait(ld)
    mean = emit_ln(p, res, xc_b, rstd, ps0, ps1, sq, ones_f, tt, sv, ss)
    p.S.wait(sv)
    p.S(lambda e, d=sq, s=rstd: e.activation(d[:, 0:T], s[:], AF.Square))
    p.S.fence(inc=(ss, 1))
    p.V.wait(ss)
    p.V(lambda e, d=sq: e.tensor_scalar(
        d[:, 0:T], d[:, 0:T], -EPS, 1.0 + EPS, ALU.mult, ALU.add))
    p.V.fence()
    p.V(lambda e, d=sq: e.reciprocal(d[:, 0:T], d[:, 0:T]))
    p.V.fence(inc=(sv, 1))
    p.S.wait(sv)
    p.S(lambda e, d=sq: e.activation(d[:, 0:T], d[:, 0:T], AF.Sqrt))
    p.S.fence(inc=(ss, 1))
    p.V.wait(ss)
    p.V(lambda e, d=rstd, s=sq: e.tensor_mul(d[:], d[:], s[:, 0:T]))
    p.V.fence(inc=(sv, 1))
    xg_b = xc_b
    for di in range(DT):
        p.V(lambda e, d=sq, s=xc_b, r=rstd, di=di: e.tensor_mul(
            d[:, 0:T], s[:, di * T:(di + 1) * T], r[:]))
        p.V.fence()
        p.V(lambda e, d=xg_b, s=sq, g=nwc, di=di: e.tensor_scalar(
            d[:, di * T:(di + 1) * T], s[:, 0:T], g[:, di:di + 1], None, ALU.mult))
        p.V.fence()
    p.V.fence(inc=(sv, 1))

    p.V.wait(ss)
    for ki in range(DT):
        p.V(lambda e, d=rtf, g=nwc, ki=ki, E_=E_: e.tensor_scalar(
            d[:, ki * E_:(ki + 1) * E_], d[:, ki * E_:(ki + 1) * E_],
            g[:, ki:ki + 1], None, ALU.mult))
    p.V.fence(inc=(sv, 1))
    p.T.wait(sv)
    lg_fm = p.sbuf("lg_fm", [128, T], F32)
    prob_tm = p.sbuf("prob_tm", [128, 16 * E_], F32)
    emit_router_tm(p, res, mean, rstd, rtf, E_, ps0, ps1, lg_fm, prob_tm,
                   idf, ones_f, tt, sv, ss, softmax=True)
    comb_tm = p.sbuf("comb_tm", [128, 16 * E_], F32)
    emit_top2_combine(p, prob_tm, comb_tm, E_, sv)
    p.V(lambda e, d=prob_tm, s=comb_tm: e.tensor_scalar(
        d[:], s[:], 0.0, None, ALU.is_gt))
    p.V.fence(inc=(sv, 1))
    wv_fm = lg_fm
    emit_comb_fm(p, comb_tm, wv_fm, E_, idf, ps1, tt, sv)
    km_fm = p.sbuf("km_fm", [128, T], F32)
    emit_comb_fm(p, prob_tm, km_fm, E_, idf, ps1, tt, sv)
    cum = p.sbuf("cum", [128, T], F32)
    thr = p.sbuf("thr", [128, 1], F32)
    p.V(lambda e, d=cum, s=km_fm, E_=E_: e.tensor_tensor_scan(
        d[0:E_, :], s[0:E_, :], s[0:E_, :], 0.0, ALU.add, ALU.bypass))
    p.V.fence()
    p.V(lambda e, d=thr, s=cum, E_=E_: e.tensor_scalar(
        d[0:E_, :], s[0:E_, T - 1:T], float(-CAP), None, ALU.add))
    p.V.fence()
    p.V(lambda e, d=cum, m=thr, E_=E_: e.tensor_scalar(
        d[0:E_, :], d[0:E_, :], m[0:E_, 0:1], None, ALU.is_gt))
    p.V.fence()
    p.V(lambda e, d=km_fm, s=cum, E_=E_: e.tensor_mul(
        d[0:E_, :], d[0:E_, :], s[0:E_, :]))
    p.V.fence()
    p.V(lambda e, d=wv_fm, s=km_fm, E_=E_: e.tensor_mul(
        d[0:E_, :], d[0:E_, :], s[0:E_, :]))
    p.V.fence(inc=(sv, 1))
    if phase == 1:
        p.G.wait(sv)
        p.G(lambda e, d=rout, s=km_fm: e.dma_start(out=d[0:128, :], in_=s[:]),
            inc=(st, 16))
        p.G(lambda e, d=rout, s=wv_fm: e.dma_start(out=d[128:256, :], in_=s[:]),
            inc=(st, 16))
        p.G.wait(st)
        return _fin(p, fused)

    y_acc = Carve(res, 0, DT * T, BF16)
    xe_b = Carve(res, 8192, DT * TC3, BF16)
    q_fm = Carve(res, 9216, DT * TC3, BF16)
    k_fm = Carve(res, 10240, DT * TC3, BF16)
    mb_b = Carve(res, 11264, TC3, BF16)
    wb_b = Carve(res, 11392, TC3, BF16)
    hk_sb = Carve(res, 11520, 512, F32)
    hv_sb = Carve(res, 12032, 4 * 256, F32)
    sinT_b = Carve(res, 13056, 4 * CH, BF16)
    srem_tm = Carve(res, 13312, 256, BF16)
    hk_b = Carve(res, 13760, 512, BF16)
    hv_b = Carve(res, 14016, 1024, BF16)
    qvg_b = Carve(res, 13440, 256, BF16)
    qvgT = Carve(res, 13568, CH, BF16)
    AT_b = Carve(res, 13632, CH, BF16)
    TmT_b = Carve(res, 13696, CH, BF16)
    k_tm = Carve(rstd, 0, (TC3 // 128) * D, BF16)
    v_tm = Carve(sq, 2048, (TC3 // 128) * D, BF16)
    g_tm = Carve(cum, 0, (TC3 // 128) * 256, F32)
    st_tm = Carve(cum, 512, (TC3 // 128) * 256, F32)
    sin_tm = Carve(cum, 1024, 256, BF16)
    dec_tm = Carve(cum, 1280, 256, F32)
    okb = Carve(cum, 1536, 256, F32)
    glb = Carve(mean, 1552, 64 + 256, F32)
    gsm = Carve(mean, 0, 4 * CH, F32)
    Gsm = Carve(mean, 512, 4 * CH, F32)
    srem_sm = Carve(mean, 1024, 4 * CH, F32)
    glsm = Carve(mean, 1536, 8, F32)
    nmv = Carve(mean, 1544, 1, F32)
    denv = Carve(mean, 1545, 1, F32)

    wqb = p.sbuf("wqb", [128, DT * D], BF16)
    wkb = p.sbuf("wkb", [128, DT * D], BF16)
    wvb = p.sbuf("wvb", [128, DT * D], BF16)
    wfb = Carve(rstd, 1024, DT * 256, BF16)
    stage = sq
    l2 = p.sem("l2")

    def loadw(whbm, dst, cols2, e_i=None):
        for hq in range(4):
            p.Y.wait(sg, sg.n)
            if e_i is None:
                src3 = whbm.rearrange("(k p) c -> p k c", p=128)[
                    :, hq * 2:(hq + 1) * 2, :]
            else:
                src3 = whbm.rearrange("(E k p) c -> p E k c", p=128, E=E_)[
                    :, e_i, hq * 2:(hq + 1) * 2, :]
            p.Y(lambda e, d=stage, s=src3, cols2=cols2: e.dma_start(
                out=d[:, 0:2 * cols2].rearrange("p (k c) -> p k c", c=cols2),
                in_=s), inc=(l2, 16))
            p.V.wait(l2)
            p.V.wait(tt, tt.n)
            p.V(lambda e, d=dst, s=stage, hq=hq, cols2=cols2: e.tensor_copy(
                d[:, hq * 2 * cols2:(hq + 1) * 2 * cols2], s[:, 0:2 * cols2]))
            p.V.fence(inc=(sg, 1))
        p.T.wait(sg)

    p.Y.wait(sv, sv.n)
    p.V.wait(tt, tt.n)
    p.V(lambda e, d=y_acc: e.memset(d[:], 0.0))
    p.V.fence(inc=(sv, 1))
    loadw(w_q, wqb, D)

    for e_i in range(E_):
        loadw(w_k, wkb, D, e_i)
        loadw(w_v, wvb, D, e_i)
        loadw(w_f, wfb, 256, e_i)
        p.V.wait(tt, tt.n)
        p.V(lambda e, d=hk_sb: e.memset(d[:], 0.0))
        p.V(lambda e, d=hv_sb: e.memset(d[:], 0.0))
        p.V.fence(inc=(sv, 1))
        for tc in range(T // TC3):
            t0 = tc * TC3
            p.T.wait(sv)
            p.T(lambda e, o=ps0, s=self_f, cf=km_fm, e_i=e_i, t0=t0:
                e.matmul(o[:, 0:TC3], s[0:E_, e_i * 128:(e_i + 1) * 128],
                         cf[0:E_, t0:t0 + TC3], start=True, stop=True),
                inc=(tt, 1))
            p.T(lambda e, o=ps1, s=self_f, cf=wv_fm, e_i=e_i, t0=t0:
                e.matmul(o[:, 0:TC3], s[0:E_, e_i * 128:(e_i + 1) * 128],
                         cf[0:E_, t0:t0 + TC3], start=True, stop=True),
                inc=(tt, 1))
            p.V.wait(tt)
            p.V(lambda e, d=mb_b, s=ps0: e.tensor_copy(d[:], s[:, 0:TC3]))
            p.V(lambda e, d=wb_b, s=ps1: e.tensor_copy(d[:], s[:, 0:TC3]))
            p.V.fence()
            for di in range(DT):
                p.V(lambda e, d=xe_b, s=xg_b, m2=mb_b, di=di, t0=t0:
                    e.tensor_mul(d[:, di * TC3:(di + 1) * TC3],
                                 s[:, di * T + t0: di * T + t0 + TC3], m2[:]))
            p.V.fence(inc=(sv, 1))
            p.T.wait(sv)
            for wsrc, dstf in ((wqb, q_fm), (wkb, k_fm)):
                for mi in range(DT):
                    bank = banks[mi % 2]
                    for ki in range(DT):
                        p.T(lambda e, o=bank, w=wsrc, x=xe_b, ki=ki, mi=mi:
                            e.matmul(o[:, 0:TC3],
                                     w[:, ki * D + mi * 128: ki * D + mi * 128 + 128],
                                     x[:, ki * TC3:(ki + 1) * TC3],
                                     start=(ki == 0), stop=(ki == DT - 1)),
                            inc=(tt, 1) if ki == DT - 1 else None)
                    p.S.wait(tt)
                    p.S(lambda e, d=dstf, s=bank, mi=mi: e.activation(
                        d[:, mi * TC3:(mi + 1) * TC3], s[:, 0:TC3], AF.Silu))
                    p.S.fence(inc=(ss, 1))
                    p.T.wait(ss)
            if phase == 7:
                break
            for tt_i in range(TC3 // 128):
                p.T.wait(ss)
                p.T.wait(sv)
                for jc in range(2):
                    bank = banks[jc]
                    for ki in range(DT):
                        p.T(lambda e, o=bank, x=xe_b, w=wkb, ki=ki, tt_i=tt_i, jc=jc:
                            e.matmul(o[:], x[:, ki * TC3 + tt_i * 128: ki * TC3 + tt_i * 128 + 128],
                                     w[:, ki * D + jc * 512: ki * D + jc * 512 + 512],
                                     start=(ki == 0), stop=(ki == DT - 1)),
                            inc=(tt, 1) if ki == DT - 1 else None)
                p.S.wait(tt)
                for jc in range(2):
                    p.S(lambda e, d=k_tm, s=banks[jc], tt_i=tt_i, jc=jc: e.activation(
                        d[:, tt_i * D + jc * 512: tt_i * D + jc * 512 + 512],
                        s[:], AF.Silu))
                p.S.fence(inc=(ss, 1))
                for jc in range(2):
                    bank = banks[2 + jc]
                    for ki in range(DT):
                        p.T(lambda e, o=bank, x=xe_b, w=wvb, ki=ki, tt_i=tt_i, jc=jc:
                            e.matmul(o[:], x[:, ki * TC3 + tt_i * 128: ki * TC3 + tt_i * 128 + 128],
                                     w[:, ki * D + jc * 512: ki * D + jc * 512 + 512],
                                     start=(ki == 0), stop=(ki == DT - 1)),
                            inc=(tt, 1) if ki == DT - 1 else None)
                p.V.wait(tt)
                for jc in range(2):
                    p.V(lambda e, d=v_tm, s=banks[2 + jc], tt_i=tt_i, jc=jc:
                        e.tensor_copy(d[:, tt_i * D + jc * 512: tt_i * D + jc * 512 + 512],
                                      s[:]))
                p.V.fence(inc=(sv, 1))
                p.T.wait(sv)
                p.T.wait(ss)
                bank = banks[0]
                for ki in range(DT):
                    p.T(lambda e, o=bank, x=xe_b, w=wfb, ki=ki, tt_i=tt_i:
                        e.matmul(o[:, 0:256],
                                 x[:, ki * TC3 + tt_i * 128: ki * TC3 + tt_i * 128 + 128],
                                 w[:, ki * 256:(ki + 1) * 256],
                                 start=(ki == 0), stop=(ki == DT - 1)),
                        inc=(tt, 1) if ki == DT - 1 else None)
                p.S.wait(tt)
                p.S(lambda e, d=okb, s=bank: e.activation(
                    d[:], s[:, 0:256], AF.Abs))
                p.S.fence()
                p.S(lambda e, d=okb: e.activation(d[:], d[:], AF.Exp, scale=-1.0))
                p.S.fence()
                p.S(lambda e, d=okb: e.activation(d[:], d[:], AF.Ln, bias=1.0))
                p.S.fence(inc=(ss, 1))
                p.V.wait(ss)
                p.V(lambda e, d=dec_tm, s=bank: e.tensor_scalar_min(
                    d[:], s[:, 0:256], 0.0))
                p.V.fence()
                p.V(lambda e, d=dec_tm, s=okb: e.tensor_sub(d[:], d[:], s[:]))
                p.V.fence()
                p.T.wait(sv, sv.n)
                p.T(lambda e, o=banks[1], s=km_fm, idd=self_f, e_i=e_i, t0=t0,
                    tt_i=tt_i: e.matmul(
                        o[:, 0:1], s[0:E_, t0 + tt_i * 128: t0 + tt_i * 128 + 128],
                        idd[0:E_, e_i * 128 + 0: e_i * 128 + 1],
                        start=True, stop=True), inc=(tt, 1))
                p.V.wait(tt)
                p.V(lambda e, d=g_tm, s=dec_tm, o=banks[1], tt_i=tt_i:
                    e.tensor_scalar(d[:, tt_i * 256:(tt_i + 1) * 256], s[:],
                                    0.125, None, ALU.mult))
                p.V.fence()
                p.V(lambda e, d=g_tm, o=banks[1], tt_i=tt_i: e.tensor_scalar(
                    d[:, tt_i * 256:(tt_i + 1) * 256],
                    d[:, tt_i * 256:(tt_i + 1) * 256], o[:, 0:1], None, ALU.mult))
                p.V.fence(inc=(sv, 1))
                p.S.wait(sv)
                p.S(lambda e, d=st_tm, s=g_tm, tt_i=tt_i: e.activation(
                    d[:, tt_i * 256:(tt_i + 1) * 256],
                    s[:, tt_i * 256:(tt_i + 1) * 256], AF.Exp))
                p.S.fence(inc=(ss, 1))
                p.V.wait(ss)
                p.V(lambda e, d=st_tm, tt_i=tt_i: e.tensor_scalar(
                    d[:, tt_i * 256:(tt_i + 1) * 256],
                    d[:, tt_i * 256:(tt_i + 1) * 256], -1.0, 1.0,
                    ALU.mult, ALU.add))
                p.V.fence(inc=(sv, 1))
            if phase == 5:
                break
            for ci in range(TC3 // CH):
                tloc = ci * CH
                p.T.wait(sv)
                for h in range(4):
                    p.T(lambda e, o=ps0, s=g_tm, ci=ci, h=h, idf=idf:
                        e.transpose(o[0:64, h * 128:(h + 1) * 128],
                                    s[:, ci * 256 + h * 64: ci * 256 + h * 64 + 64],
                                    idf[:]), inc=(tt, 1) if h == 3 else None)
                p.V.wait(tt)
                p.V(lambda e, d=gsm, s=ps0: e.tensor_copy(
                    d[0:64, :], s[0:64, 0:512]))
                p.V.fence()
                for h in range(4):
                    p.V(lambda e, d=Gsm, s=gsm, h=h: e.tensor_tensor_scan(
                        d[0:64, h * 128:(h + 1) * 128],
                        s[0:64, h * 128:(h + 1) * 128],
                        s[0:64, h * 128:(h + 1) * 128], 0.0, ALU.add, ALU.bypass))
                p.V.fence(inc=(sv, 1))
                p.V(lambda e, d=glsm, s=Gsm: e.tensor_copy(
                    d[0:64, 0:4], s[0:64, :].rearrange(
                        "p (a b) -> p a b", b=CH)[:, :, CH - 1]))
                p.V.fence(inc=(sv, 1))
                p.S.wait(sv)
                p.S(lambda e, d=glsm: e.activation(d[0:64, 0:4], d[0:64, 0:4],
                                                   AF.Exp))
                p.S.fence(inc=(ss, 1))
                p.T.wait(sv)
                for h in range(4):
                    p.T(lambda e, o=ps1, s=Gsm, h=h, idf=idf:
                        e.transpose(o[:, h * 64:(h + 1) * 64],
                                    s[0:64, h * 128:(h + 1) * 128],
                                    idf[0:64, 0:64]),
                        inc=(tt, 1) if h == 3 else None)
                p.S.wait(tt)
                p.S(lambda e, d=dec_tm, s=ps1: e.activation(
                    d[:], s[:, 0:256], AF.Exp))
                p.S.fence()
                p.S(lambda e, d=okb, s=ps1: e.activation(
                    d[:], s[:, 0:256], AF.Exp, scale=-1.0))
                p.S.fence(inc=(ss, 1))
                p.V.wait(ss)
                p.V(lambda e, d=sin_tm, s=st_tm, o=okb, ci=ci: e.tensor_mul(
                    d[:, 0:256], s[:, ci * 256:(ci + 1) * 256], o[:]))
                p.V.fence(inc=(sv, 1))
                p.T.wait(sv)
                for h in range(4):
                    p.T(lambda e, o=ptb3, s=sin_tm, h=h, idb=idb:
                        e.transpose(o[0:64, h * 128:(h + 1) * 128],
                                    s[:, h * 64:(h + 1) * 64], idb[:]),
                        inc=(tt, 1) if h == 3 else None)
                p.T(lambda e, o=ps1, s=glsm, idf=idf: e.transpose(
                    o[0:4, 0:64], s[0:64, 0:4], idf[0:64, 0:64]), inc=(tt, 1))
                p.V.wait(tt)
                p.V(lambda e, d=sinT_b, s=ptb3: e.tensor_copy(
                    d[0:64, :], s[0:64, 0:512]))
                p.V(lambda e, d=glb, s=ps1: e.tensor_copy(
                    d[0:4, 0:64], s[0:4, 0:64]))
                p.V.fence(inc=(sv, 1))
                p.T.wait(sv)
                for h in range(4):
                    p.T(lambda e, o=ps1, w=self_f, s=glb, h=h: e.matmul(
                        o[:, 128 + h * 64: 128 + h * 64 + 64],
                        w[0:4, h * 128:(h + 1) * 128],
                        s[0:4, 0:64], start=True, stop=True),
                        inc=(tt, 1) if h == 3 else None)
                p.V.wait(tt)
                p.V(lambda e, d=glb, s=ps1: e.tensor_copy(
                    d[:, 64:64 + 256], s[:, 128:384]))
                p.V.fence(inc=(sv, 1))
                p.V(lambda e, d=srem_sm, s=sinT_b, g=glsm: e.tensor_tensor(
                    d[0:64, :].rearrange("p (a b) -> p a b", b=CH),
                    s[0:64, :].rearrange("p (a b) -> p a b", b=CH),
                    g[0:64, 0:4].unsqueeze(2).to_broadcast([64, 4, CH]),
                    ALU.mult))
                p.V.fence(inc=(sv, 1))
                p.T.wait(sv)
                for h in range(4):
                    p.T(lambda e, o=ps1, s=srem_sm, h=h, idf=idf:
                        e.transpose(o[:, h * 64:(h + 1) * 64],
                                    s[0:64, h * 128:(h + 1) * 128],
                                    idf[0:64, 0:64]),
                        inc=(tt, 1) if h == 3 else None)
                p.V.wait(tt)
                p.V(lambda e, d=srem_tm, s=ps1: e.tensor_copy(d[:], s[:, 0:256]))
                p.V(lambda e, d=hk_b, s=hk_sb: e.tensor_copy(d[:], s[:]))
                p.V(lambda e, d=hv_b, s=hv_sb: e.tensor_copy(
                    d[0:64, :], s[0:64, :]))
                p.V.fence(inc=(sv, 1))
                p.T.wait(sv)
                for h in range(4):
                    for s2 in range(2):
                        p.T(lambda e, o=ps0, kf2=k_fm, qf2=q_fm, h=h, s2=s2:
                            e.matmul(
                                o[:, 0:CH],
                                kf2[:, (2 * h + s2) * TC3 + tloc: (2 * h + s2) * TC3 + tloc + CH],
                                qf2[:, (2 * h + s2) * TC3 + tloc: (2 * h + s2) * TC3 + tloc + CH],
                                start=(s2 == 0), stop=(s2 == 1)),
                            inc=(tt, 1) if s2 == 1 else None)
                    p.V.wait(tt)
                    p.V(lambda e, d=AT_b, s=ps0, m2=triub: e.tensor_tensor(
                        d[:], s[:, 0:CH], m2[:], ALU.mult))
                    p.V.fence(inc=(sv, 1))
                    p.T.wait(sv)
                    bank = pm0
                    p.T(lambda e, o=bank, a=AT_b, s=sin_tm, h=h: e.matmul(
                        o[:, 0:64], a[:], s[:, h * 64:(h + 1) * 64],
                        start=True, stop=False))
                    for s2 in range(2):
                        p.T(lambda e, o=bank, q=q_fm, hk=hk_b, h=h, s2=s2:
                            e.matmul(o[:, 0:64],
                                     q[:, (2 * h + s2) * TC3 + tloc: (2 * h + s2) * TC3 + tloc + CH],
                                     hk[:, (h * 2 + s2) * 64: (h * 2 + s2) * 64 + 64],
                                     start=False, stop=(s2 == 1)),
                            inc=(tt, 1) if s2 == 1 else None)
                    p.V.wait(tt)
                    p.V(lambda e, d=okb, s=bank, dc=dec_tm, h=h: e.tensor_mul(
                        d[:, 0:64], s[:, 0:64], dc[:, h * 64:(h + 1) * 64]))
                    p.V.fence()
                    p.V(lambda e, d=nmv, s=okb: e.reduce_max(
                        d[:], s[:, 0:64], AX.X, negate=True))
                    p.V.fence(inc=(sv, 1))
                    p.S.wait(sv)
                    p.S(lambda e, d=okb, m2=nmv, a=denv: e.activation(
                        d[:, 0:64], d[:, 0:64], AF.Exp, bias=m2[:, 0:1],
                        accum_out=a[:, 0:1]))
                    p.S.fence(inc=(ss, 1))
                    p.V.wait(ss)
                    p.V(lambda e, d=denv: e.reciprocal(d[:], d[:]))
                    p.V.fence()
                    p.V(lambda e, d=okb, r=denv: e.tensor_scalar(
                        d[:, 0:64], d[:, 0:64], r[:, 0:1], None, ALU.mult))
                    p.V.fence()
                    p.V(lambda e, d=qvg_b, s=okb, dc=dec_tm, h=h: e.tensor_mul(
                        d[:, 0:64], s[:, 0:64], dc[:, h * 64:(h + 1) * 64]))
                    p.V.fence(inc=(sv, 1))
                    p.T.wait(sv)
                    p.T(lambda e, o=ptb3, s=qvg_b: e.transpose(
                        o[0:64, 0:CH], s[:, 0:64], idb[:]), inc=(tt, 1))
                    p.V.wait(tt)
                    p.V(lambda e, d=qvgT, s=ptb3: e.tensor_copy(
                        d[0:64, 0:CH], s[0:64, 0:CH]))
                    p.V.fence(inc=(sv, 1))
                    p.T.wait(sv)
                    p.T(lambda e, o=ps0, s=sinT_b, qg=qvgT, h=h: e.matmul(
                        o[:, 0:CH], s[0:64, h * 128:(h + 1) * 128],
                        qg[0:64, 0:CH], start=True, stop=True), inc=(tt, 1))
                    p.V.wait(tt)
                    p.V(lambda e, d=TmT_b, s=ps0, m2=triub: e.tensor_tensor(
                        d[:], s[:, 0:CH], m2[:], ALU.mult))
                    p.V.fence(inc=(sv, 1))
                    p.T.wait(sv)
                    for s2 in range(2):
                        p.T(lambda e, o=pm1, v=v_tm, tm=TmT_b, h=h, s2=s2, ci=ci:
                            e.matmul(o[:, s2 * CH:(s2 + 1) * CH],
                                     v[:, ci * D + h * 256 + s2 * 128: ci * D + h * 256 + s2 * 128 + 128],
                                     tm[:], start=True, stop=True), inc=(tt, 1))
                        p.T(lambda e, o=pm2, hv=hv_b, qg=qvgT, h=h, s2=s2:
                            e.matmul(o[:, s2 * CH:(s2 + 1) * CH],
                                     hv[0:64, h * 256 + s2 * 128: h * 256 + s2 * 128 + 128],
                                     qg[0:64, 0:CH], start=True, stop=True),
                            inc=(tt, 1))
                    p.V.wait(tt)
                    for s2 in range(2):
                        p.V(lambda e, d=okb, s=pm2, s2=s2: e.tensor_copy(
                            d[:, s2 * 128:(s2 + 1) * 128],
                            s[:, s2 * CH:(s2 + 1) * CH]))
                    p.V.fence()
                    for s2 in range(2):
                        p.V(lambda e, d=okb, s=pm1, s2=s2: e.tensor_add(
                            d[:, s2 * 128:(s2 + 1) * 128],
                            d[:, s2 * 128:(s2 + 1) * 128],
                            s[:, s2 * CH:(s2 + 1) * CH]))
                    p.V.fence()
                    p.V(lambda e, d=okb, w=wb_b, tloc=tloc: e.tensor_tensor(
                        d[:, 0:256].rearrange("p (a b) -> p a b", b=CH),
                        d[:, 0:256].rearrange("p (a b) -> p a b", b=CH),
                        w[:, tloc:tloc + CH].unsqueeze(1).to_broadcast(
                            [128, 2, CH]), ALU.mult))
                    p.V.fence()
                    for s2 in range(2):
                        p.V(lambda e, d=y_acc, s=okb, h=h, s2=s2, t0=t0,
                            tloc=tloc: e.tensor_add(
                                d[:, (2 * h + s2) * T + t0 + tloc:
                                   (2 * h + s2) * T + t0 + tloc + CH],
                                d[:, (2 * h + s2) * T + t0 + tloc:
                                   (2 * h + s2) * T + t0 + tloc + CH],
                                s[:, s2 * 128:(s2 + 1) * 128]))
                    p.V.fence(inc=(sv, 1))
                    p.T.wait(sv)
                    for s2 in range(2):
                        p.T(lambda e, o=pm3, kt2=k_tm, sr=srem_tm, h=h, s2=s2,
                            ci=ci: e.matmul(
                                o[:, s2 * 64:(s2 + 1) * 64],
                                kt2[:, ci * D + h * 256 + s2 * 128: ci * D + h * 256 + s2 * 128 + 128],
                                sr[:, h * 64:(h + 1) * 64],
                                start=True, stop=True), inc=(tt, 1))
                    p.T(lambda e, o=pm0, sr=srem_tm, v=v_tm, h=h, ci=ci:
                        e.matmul(o[0:64, 0:256], sr[:, h * 64:(h + 1) * 64],
                                 v[:, ci * D + h * 256: ci * D + h * 256 + 256],
                                 start=True, stop=True), inc=(tt, 1))
                    p.V.wait(tt)
                    for s2 in range(2):
                        p.V(lambda e, d=hk_sb, g=glb, h=h, s2=s2: e.tensor_mul(
                            d[:, (h * 2 + s2) * 64: (h * 2 + s2) * 64 + 64],
                            d[:, (h * 2 + s2) * 64: (h * 2 + s2) * 64 + 64],
                            g[:, 64 + h * 64: 64 + h * 64 + 64]))
                    p.V(lambda e, d=hv_sb, g=glsm, h=h: e.tensor_scalar(
                        d[0:64, h * 256:(h + 1) * 256],
                        d[0:64, h * 256:(h + 1) * 256],
                        g[0:64, h:h + 1], None, ALU.mult))
                    p.V.fence()
                    for s2 in range(2):
                        p.V(lambda e, d=hk_sb, s=pm3, h=h, s2=s2: e.tensor_add(
                            d[:, (h * 2 + s2) * 64: (h * 2 + s2) * 64 + 64],
                            d[:, (h * 2 + s2) * 64: (h * 2 + s2) * 64 + 64],
                            s[:, s2 * 64:(s2 + 1) * 64]))
                    p.V(lambda e, d=hv_sb, s=pm0, h=h: e.tensor_add(
                        d[0:64, h * 256:(h + 1) * 256],
                        d[0:64, h * 256:(h + 1) * 256], s[0:64, 0:256]))
                    p.V.fence(inc=(sv, 1))
                    p.T.wait(sv)
                    if phase == 6:
                        break
                if phase == 6:
                    break
            if phase in (5, 6, 7):
                break
        if phase in (5, 6, 7):
            break
    if phase in (5, 6, 7):
        p.G.wait(sv)
        p.G(lambda e, d=rout, s=km_fm: e.dma_start(out=d[0:128, :], in_=s[:]),
            inc=(st, 16))
        p.G.wait(st)
        return _fin(p, fused)

    loadw(w_o, wqb, D)
    ysq = sq
    ystat = km_fm
    p.S.wait(sg, sg.n)
    for c in range(NCH):
        p.S.wait(sv)
        p.S(lambda e, d=ysq, s=y_acc, c=c: e.activation(
            d[:].rearrange("p (k n) -> p k n", n=512),
            y_acc[:].rearrange("p (k t) -> p k t", t=T)[:, :, c * 512:(c + 1) * 512]
            if False else _carve3d(y_acc, c), AF.Square))
        p.S.fence(inc=(ss, 1))
        p.T.wait(ss)
        for di in range(DT):
            p.T(lambda e, o=ps0, w=ones_f, s=ysq, di=di:
                e.matmul(o[:], w[:], s[:, di * 512:(di + 1) * 512],
                         start=(di == 0), stop=(di == DT - 1)),
                inc=(tt, 1) if di == DT - 1 else None)
        p.V.wait(tt)
        p.V(lambda e, d=ystat, s=ps0, c=c: e.tensor_scalar(
            d[:, c * 512:c * 512 + 512], s[:], 1.0 / D, EPS, ALU.mult, ALU.add))
        p.V.fence(inc=(sv, 1))
        p.T.wait(sv)
        p.S.wait(sv)
    p.V(lambda e, d=ystat: e.reciprocal(d[:], d[:]))
    p.V.fence(inc=(sv, 1))
    p.S.wait(sv)
    p.S(lambda e, d=ystat: e.activation(d[:], d[:], AF.Sqrt))
    p.S.fence(inc=(ss, 1))
    p.V.wait(ss)
    for di in range(DT):
        p.V(lambda e, d=ysq, s=y_acc, r=ystat, di=di: e.tensor_mul(
            d[:, 0:T], s[:, di * T:(di + 1) * T], r[:]))
        p.V.fence()
        p.V(lambda e, d=y_acc, s=ysq, g=gwc, di=di: e.tensor_scalar(
            d[:, di * T:(di + 1) * T], s[:, 0:T], g[:, di:di + 1], None,
            ALU.mult))
        p.V.fence()
    p.V.fence(inc=(sv, 1))
    p.T.wait(sv)
    l3 = p.sem("l3")
    rl3 = Carve(res, 8192, T, F32)
    obf = Carve(res, 10240, T, F32)
    p.Y.wait(sg, sg.n)
    p.Y.wait(sv, sv.n)
    for mi in range(DT):
        p.V.wait(st, st.n)
        p.Y(lambda e, d=rl3, s=rin2, mi=mi: e.dma_start(
            out=d[:], in_=s.rearrange("(k p) t -> p k t", p=128)[:, mi, :]),
            inc=(l3, 16))
        for c in range(NCH):
            bank = banks[c]
            for ki in range(DT):
                p.T(lambda e, o=bank, w=wqb, x=y_acc, ki=ki, mi=mi, c=c:
                    e.matmul(o[:], w[:, ki * D + mi * 128: ki * D + mi * 128 + 128],
                             x[:, ki * T + c * 512: ki * T + c * 512 + 512],
                             start=(ki == 0), stop=(ki == DT - 1)),
                    inc=(tt, 1) if ki == DT - 1 else None)
        p.V.wait(tt)
        p.V.wait(l3)
        for c in range(NCH):
            p.V(lambda e, d=obf, s=banks[c], r=rl3, c=c: e.scalar_tensor_tensor(
                d[:, c * 512:c * 512 + 512], s[:], 0.5,
                r[:, c * 512:c * 512 + 512], ALU.mult, ALU.add))
        p.V.fence(inc=(sv, 1))
        p.G.wait(sv)
        p.G(lambda e, d=rout, s=obf, mi=mi: e.dma_start(
            out=d.rearrange("(k p) t -> p k t", p=128)[:, mi, :], in_=s[:]),
            inc=(st, 16))
        p.T.wait(sv)
        p.Y.wait(sv, sv.n)
        p.Y.wait(st, st.n)
    p.G.wait(st)
    return _fin(p, fused)


def _carve3d(cv, c):
    ap = cv[:]
    return ap.rearrange("p (k t) -> p k t", t=T)[:, :, c * 512:(c + 1) * 512]


def l3_ffn_inputs(inputs, core, rin_fm):
    import ml_dtypes
    E_ = 8
    self1 = np.zeros((E_, E_ * 128), np.float32)
    for e in range(E_):
        self1[e, e * 128:(e + 1) * 128] = 1.0
    def cols(v, n=8):
        return np.ascontiguousarray(np.asarray(v).reshape(n, 128).T)
    triu = (np.arange(128)[:, None] <= np.arange(128)[None, :]).astype(np.float32)
    return _with_ones({
        "rin": rin_fm, "rin2": rin_fm,
        "WG": np.ascontiguousarray(inputs["L3_ffn_Wg"].T),
        "WQ": np.ascontiguousarray(inputs["L3_ffn_Wq"].T),
        "WKE": np.ascontiguousarray(
            inputs["L3_ffn_Wk"].transpose(0, 2, 1).reshape(E_ * D, D)),
        "WVE": np.ascontiguousarray(
            inputs["L3_ffn_Wv"].transpose(0, 2, 1).reshape(E_ * D, D)),
        "WFE": np.ascontiguousarray(
            inputs["L3_ffn_Wf"].transpose(0, 2, 1).reshape(E_ * D, 256)),
        "WOT": np.ascontiguousarray(inputs["L3_ffn_Wo"].T),
        "ones": np.ones((128, 128), np.float32),
        "ident": np.eye(128, dtype=np.float32),
        "identb": np.eye(128, dtype=np.float32).astype(ml_dtypes.bfloat16),
        "SELF1": self1,
        "NRMW": cols(inputs["L3_ffn_norm_w"]),
        "GNRMW": cols(inputs["L3_ffn_gnorm_w"]),
        "TRIU01": triu.astype(ml_dtypes.bfloat16),
    })


STAGES = [
    ("l2_ffn", build_l2_ffn, l2_ffn_inputs),
    ("l2_attn", build_l2_attn, l2_attn_inputs),
    ("l1_ffn", build_l1_ffn, l1_ffn_inputs),
    ("l0_ffn", build_l0_ffn, l0_ffn_inputs),
    ("l1_attn", build_l1_attn, l1_attn_inputs),
    ("l3_attn", build_l3_attn, l3_attn_inputs),
    ("l0_attn_wkv", build_l0_attn_wkv, l0_attn_wkv_inputs),
    ("l0_attn_moe", build_l0_attn_moe, l0_attn_moe_inputs),
    ("l3_ffn", build_l3_ffn, l3_ffn_inputs),
]

SEQ = ["l0_ffn", "l0_attn_wkv", "l0_attn_moe", "l0_ffn",
       "l1_ffn", "l1_attn", "l1_ffn",
       "l2_ffn", "l2_attn", "l2_ffn",
       "l3_ffn", "l3_attn", "l3_ffn"]


def build_fused():
    p = Prog("mega")
    ones_f_id(p, p.sem("ld"))
    idf_b(p, p.sem("ld"))
    ones_root(p, p.sem("ld"))
    bld = {k: b for k, b, _ in STAGES}
    rin0 = p.dram_in("rin0", [D, T], F32)
    rout_f = p.dram_out("rout", [D, T], F32)
    scr = [p.nc.dram_tensor(f"scr{i}", [D, T], F32, kind="Internal").ap()
           for i in range(3)]
    chain = [(-1, 0, None), (0, 1, None), (1, 2, 0), (2, 0, None),
             (0, 1, None), (1, 0, None), (0, 1, None),
             (1, 0, None), (0, 1, None), (1, 0, None),
             (0, 1, None), (1, 0, None), (0, -2, None)]
    def ap(i):
        if i == -1:
            return rin0
        if i == -2:
            return rout_f
        return scr[i]
    for key, (ri, ro, r3) in zip(SEQ, chain):
        fz = {"p": p, "rin": ap(ri), "rout": ap(ro), "px": key + "_"}
        if r3 is not None:
            fz["rin3"] = ap(r3)
        p.push_scope()
        bld[key](99, fz)
        p.barrier()
        p.pop_scope()
    return p.finalize()




# version 03
# speedup vs baseline: 1.2161x; 1.2161x over previous
import math
import numpy as np
from contextlib import ExitStack

import concourse.bass as bass
import concourse.mybir as mybir
from concourse.bass_utils import run_bass_kernel_spmd

F32 = mybir.dt.float32
BF16 = mybir.dt.bfloat16
AF = mybir.ActivationFunctionType
ALU = mybir.AluOpType
AX = mybir.AxisListType

D = 1024
T = 2048
DT = D // 128
NCH = T // 512
EPS = 1e-5

ENGS = ("sync", "vector", "scalar", "tensor", "gpsimd")


class Sem:
    __slots__ = ("h", "n")
    def __init__(self, h):
        self.h = h
        self.n = 0


class EngQ:
    def __init__(self, prog, name):
        self.p = prog
        self.name = name

    def __call__(self, fn, inc=None):
        if inc is not None:
            sem, k = inc
            sem.n += k
            def wrapped(e, fn=fn, semh=sem.h, k=k):
                fn(e).then_inc(semh, k)
            self.p.ops[self.name].append(wrapped)
        else:
            self.p.ops[self.name].append(fn)

    def wait(self, sem, val=None):
        v = sem.n if val is None else val
        self.p.ops[self.name].append(lambda e, h=sem.h, v=v: e.wait_ge(h, v))

    def fence(self, inc=None):
        self(lambda e: e.drain(), inc=inc)


class ScalarQ(EngQ):
    def fence(self, inc=None):
        self(lambda e: e.drain())
        if inc is not None:
            scr = self.p._s_scr
            self(lambda e, s=scr: e.activation(s[0:1, 0:1], s[0:1, 0:1],
                                               AF.Identity), inc=inc)


class Prog:
    def __init__(self, name="prog"):
        self.nc = bass.Bass("TRN2", target_bir_lowering=False, debug=False)
        self.ops = {e: [] for e in ENGS}
        self.es = ExitStack()
        self.es_stack = [self.es]
        self._sems = {}
        self._drams = {}
        self.Y = EngQ(self, "sync")
        self.V = EngQ(self, "vector")
        self.S = ScalarQ(self, "scalar")
        self.T = EngQ(self, "tensor")
        self.G = EngQ(self, "gpsimd")
        self._names = set()
        self._bar = None
        self._barn = 0
        self._s_scr = self.sbuf("s_scr11", [1, 8], F32)
        self.S(lambda e, s=self._s_scr: e.memzero(s[:]))

    def push_scope(self):
        es = ExitStack()
        self.es_stack.append(es)
        return es

    def pop_scope(self):
        self.es_stack.pop().close()

    def barrier(self):
        b = self.sem("bar")
        for q in (self.Y, self.V, self.S, self.T, self.G):
            if q is self.S:
                q(lambda e, s=self._s_scr: e.activation(
                    s[0:1, 1:2], s[0:1, 1:2], AF.Identity), inc=(b, 1))
            else:
                q(lambda e: e.drain(), inc=(b, 1))
        for q in (self.Y, self.V, self.S, self.T, self.G):
            q.wait(b)

    def _nm(self, base):
        i, nm = 0, base
        while nm in self._names:
            i += 1
            nm = f"{base}_{i}"
        self._names.add(nm)
        return nm

    def sem(self, name):
        if name not in self._sems:
            self._sems[name] = Sem(
                self.es.enter_context(self.nc.semaphore(self._nm(name))))
        return self._sems[name]

    def sbuf(self, name, shape, dt=F32):
        return self.es_stack[-1].enter_context(
            self.nc.sbuf_tensor(self._nm(name), shape, dt))

    def sbuf_root(self, name, shape, dt=F32):
        return self.es.enter_context(
            self.nc.sbuf_tensor(self._nm(name), shape, dt))

    def psum(self, name, shape, dt=F32):
        return self.es_stack[-1].enter_context(
            self.nc.psum_tensor(self._nm(name), shape, dt))

    def dram_in(self, name, shape, dt=F32):
        if name not in self._drams:
            self._names.add(name)
            self._drams[name] = self.nc.dram_tensor(
                name, shape, dt, kind="ExternalInput").ap()
        return self._drams[name]

    def dram_out(self, name, shape, dt=F32):
        self._names.add(name)
        return self.nc.dram_tensor(name, shape, dt, kind="ExternalOutput").ap()

    def finalize(self):
        with self.nc.Block() as block:
            for eng in ENGS:
                ops = self.ops[eng]
                if not ops:
                    continue
                def runner(e, ops=ops):
                    for f in ops:
                        f(e)
                getattr(block, eng)(runner)
        self.es.close()
        return self.nc



def load_fm(p, name, hbm_ap, rows, cols, ld, bf16=True, queue="Y"):
    kt = (rows + 127) // 128
    f32 = p.sbuf(name + "_f", [128, kt * cols], F32)
    src = hbm_ap.rearrange("(k p) c -> p k c", p=128)
    Q = getattr(p, queue)
    Q(lambda e, d=f32, s=src, cols=cols: e.dma_start(
        out=d[:].rearrange("p (k c) -> p k c", c=cols), in_=s), inc=(ld, 16))
    b16 = p.sbuf(name + "_b", [128, kt * cols], BF16) if bf16 else None
    return f32, b16


def cast_w(p, pairs, eng="V"):
    E = getattr(p, eng)
    for f32, b16 in pairs:
        if b16 is None:
            continue
        if eng == "S":
            E(lambda e, d=b16, s=f32: e.copy(d[:], s[:]))
        else:
            E(lambda e, d=b16, s=f32: e.tensor_copy(d[:], s[:]))


def emit_ln(p, res, xc_b, rstd, ps0, ps1, sq, ones_f, tt, sv, ss,
            out_f32=None, tag="ln", res_cols=None):
    mean = p.sbuf(tag + "_mean", [128, T], F32)
    ex2 = rstd
    for c in range(NCH):
        p.S(lambda e, d=sq, s=res, c=c: e.activation(
            d[:].rearrange("p (k n) -> p k n", n=512), _chunk3d(s, c), AF.Square))
        p.S.fence(inc=(ss, 1))
        p.T.wait(ss)
        for di in range(DT):
            p.T(lambda e, o=ps0, w=ones_f, s=res, di=di, c=c:
                e.matmul(o[:], w[:], s[:, di * T + c * 512: di * T + c * 512 + 512],
                         start=(di == 0), stop=(di == DT - 1)))
        for di in range(DT):
            p.T(lambda e, o=ps1, w=ones_f, s=sq, di=di:
                e.matmul(o[:], w[:], s[:, di * 512: di * 512 + 512],
                         start=(di == 0), stop=(di == DT - 1)),
                inc=(tt, 1) if di == DT - 1 else None)
        p.V.wait(tt)
        p.V(lambda e, d=mean, s=ps0, c=c: e.tensor_scalar_mul(
            d[:, c * 512:c * 512 + 512], s[:], 1.0 / D))
        p.V(lambda e, d=ex2, s=ps1, c=c: e.tensor_scalar_mul(
            d[:, c * 512:c * 512 + 512], s[:], 1.0 / D))
        p.V.fence(inc=(sv, 1))
        p.T.wait(sv)
        p.S.wait(sv)
    p.S(lambda e, d=sq, s=mean: e.activation(d[:, 0:T], s[:], AF.Square))
    p.S.fence(inc=(ss, 1))
    p.V.wait(ss)
    p.V(lambda e, d=rstd, b=sq: e.tensor_sub(d[:], d[:], b[:, 0:T]))
    p.V.fence()
    p.V(lambda e, d=rstd: e.tensor_scalar_add(d[:], d[:], EPS))
    p.V.fence()
    p.V(lambda e, d=rstd: e.reciprocal(d[:], d[:]))
    p.V.fence(inc=(sv, 1))
    p.S.wait(sv)
    p.S(lambda e, d=rstd: e.activation(d[:], d[:], AF.Sqrt))
    p.S.fence(inc=(ss, 1))
    if xc_b is not None:
        for di in range(DT):
            p.V(lambda e, d=xc_b, s=res, m=mean, di=di: e.tensor_sub(
                d[:, di * T:(di + 1) * T], s[:, di * T:(di + 1) * T], m[:]))
    p.V.fence(inc=(sv, 1))
    if out_f32 is not None:
        p.V.wait(ss)
        for di in range(DT):
            p.V(lambda e, d=out_f32, s=xc_b, r=rstd, di=di: e.tensor_mul(
                d[:, di * T:(di + 1) * T], s[:, di * T:(di + 1) * T], r[:]))
        p.V.fence(inc=(sv, 1))
    return mean


def _chunk3d(res_sb, c):
    return res_sb[:].rearrange("p (k t) -> p k t", t=T)[:, :, c * 512:(c + 1) * 512]


def ln_prologue(p, ld, rin_ap=None):
    rin = rin_ap if rin_ap is not None else p.dram_in("rin", [D, T], F32)
    res = p.sbuf("res", [128, DT * T], F32)
    src = rin.rearrange("(k p) t -> p k t", p=128)
    p.Y(lambda e, d=res, s=src: e.dma_start(
        out=d[:].rearrange("p (k t) -> p k t", t=T), in_=s), inc=(ld, 16))
    return res


def store_rout(p, res, st, wait_sems=(), rout_ap=None):
    rout = rout_ap if rout_ap is not None else p.dram_out("rout", [D, T], F32)
    dst = rout.rearrange("(k p) t -> p k t", p=128)
    for s in wait_sems:
        p.G.wait(s)
    p.G(lambda e, d=dst, s=res: e.dma_start(
        out=d, in_=s[:].rearrange("p (k t) -> p k t", t=T)), inc=(st, 16))
    p.G.wait(st)


def std_sems(p):
    return (p.sem("ld"), p.sem("st"), p.sem("sv"), p.sem("ss"), p.sem("tt"),
            p.sem("sg"))


def _with_ones(d):
    d["onesg"] = np.ones((128, 128), np.float32)
    return d


def _fin(p, fused):
    return None if fused else p.finalize()


def _fz_head(fz, name):
    if fz is None:
        return Prog(name), None, None, None, "", False
    return fz["p"], fz["rin"], fz["rout"], fz.get("rin3"), fz["px"], True



def build_l2_ffn(phase=99, fz=None):
    p, _rin, _rout, _rin3, px, fused = _fz_head(fz, "l2_ffn")
    ld, st, sv, ss, tt, sg = std_sems(p)
    res = ln_prologue(p, ld, _rin)
    w_wt = p.dram_in(px + "WT", [D, D], F32)
    w_vr = p.dram_in(px + "VR", [D, 32], F32)
    w_ct = p.dram_in(px + "CTP", [128, D], F32)
    w_b = p.dram_in(px + "bcol", [128, DT], F32)
    ones = p.dram_in(px + "ones", [128, 128], F32)

    ones_f = ones_root(p, ld)
    bcol = p.sbuf("bcol_sb", [128, DT], F32)
    p.Y(lambda e, d=bcol, s=w_b: e.dma_start(out=d[:], in_=s), inc=(ld, 16))
    wtf, wtb = load_fm(p, "wt", w_wt, D, D, ld)
    vrf, vrb = load_fm(p, "vr", w_vr, D, 32, ld)
    ctf, ctb = load_fm(p, "ct", w_ct, 128, D, ld)

    p.V.wait(ld)
    p.S.wait(ld)
    p.T.wait(ld)
    cast_w(p, [(wtf, wtb), (vrf, vrb)], "V")
    cast_w(p, [(ctf, ctb)], "S")
    halfb = p.sbuf("halfb", [128, DT], F32)
    p.V(lambda e, d=halfb, s=bcol: e.tensor_scalar_mul(d[:], s[:], 0.5))
    r32b = p.sbuf("r32b", [128, T], BF16)
    p.V(lambda e, d=r32b: e.memset(d[32:64, :], 0.0))
    p.V(lambda e, d=r32b: e.memset(d[64:128, :], 0.0))
    p.V.fence(inc=(sv, 1))
    p.S.fence(inc=(ss, 1))

    xc_b = p.sbuf("xc_b", [128, DT * T], BF16)
    rstd = p.sbuf("rstd", [128, T], F32)
    sq = p.sbuf("sq", [128, DT * 512], F32)
    ps0, ps1 = p.psum("ps0", [128, 512]), p.psum("ps1", [128, 512])
    emit_ln(p, res, xc_b, rstd, ps0, ps1, sq, ones_f, tt, sv, ss)

    ps_r = p.psum("ps_r", [128, 512])
    p.T.wait(sv)
    p.T.wait(ss)
    p.V.wait(ss)
    for c in range(NCH):
        for ki in range(DT):
            p.T(lambda e, o=ps_r, w=vrb, x=xc_b, ki=ki, c=c:
                e.matmul(o[0:32, 0:512], w[:, ki * 32: ki * 32 + 32],
                         x[:, ki * T + c * 512: ki * T + c * 512 + 512],
                         start=(ki == 0), stop=(ki == DT - 1)),
                inc=(tt, 1) if ki == DT - 1 else None)
        p.V.wait(tt)
        p.V(lambda e, d=r32b, s=ps_r, c=c: e.tensor_copy(
            d[0:32, c * 512:c * 512 + 512], s[0:32, 0:512]))
        p.V.fence(inc=(sv, 1))
        p.T.wait(sv)

    banks = [p.psum(f"pm{i}", [128, 512]) for i in range(4)]
    for mi in range(DT):
        for c in range(NCH):
            bank = banks[c]
            for ki in range(DT):
                p.T(lambda e, o=bank, w=wtb, x=xc_b, ki=ki, mi=mi, c=c:
                    e.matmul(o[:], w[:, ki * D + mi * 128: ki * D + mi * 128 + 128],
                             x[:, ki * T + c * 512: ki * T + c * 512 + 512],
                             start=(ki == 0), stop=False))
            p.T(lambda e, o=bank, w=ctb, r=r32b, mi=mi, c=c:
                e.matmul(o[:], w[:, mi * 128: mi * 128 + 128],
                         r[:, c * 512:c * 512 + 512], start=False, stop=True),
                inc=(tt, 1))
        p.V.wait(tt)
        for c in range(NCH):
            bank = banks[c]
            p.V(lambda e, o=bank, r=rstd, c=c: e.tensor_mul(
                o[:], o[:], r[:, c * 512:c * 512 + 512]))
            p.V.fence()
            p.V(lambda e, o=bank, d=res, mi=mi, c=c: e.scalar_tensor_tensor(
                d[:, mi * T + c * 512: mi * T + c * 512 + 512], o[:], 0.5,
                d[:, mi * T + c * 512: mi * T + c * 512 + 512],
                ALU.mult, ALU.add))
            p.V.fence()
        p.V(lambda e, d=res, b=halfb, mi=mi: e.tensor_scalar_add(
            d[:, mi * T:(mi + 1) * T], d[:, mi * T:(mi + 1) * T], b[:, mi:mi + 1]))
        p.V.fence(inc=(sv, 1))
        p.T.wait(sv)
    store_rout(p, res, st, wait_sems=[sv], rout_ap=_rout)
    return _fin(p, fused)


def l2_ffn_inputs(inputs, core, rin_fm):
    W = inputs["L2_ffn_W"]
    return _with_ones({
        "rin": rin_fm,
        "WT": np.ascontiguousarray(W.T),
        "VR": np.ascontiguousarray(inputs["L2_ffn_V_r"]),
        "CTP": np.ascontiguousarray(
            np.concatenate([inputs["L2_ffn_C"].T,
                            np.zeros((96, D), np.float32)], 0)),
        "bcol": np.ascontiguousarray(inputs["L2_ffn_b"].reshape(DT, 128).T),
        "ones": np.ones((128, 128), np.float32),
    })



def _fm(x):
    return np.ascontiguousarray(np.asarray(x).T)


_PROGS = {}


def _prog(name, builder):
    if name not in _PROGS:
        _PROGS[name] = builder()
    return _PROGS[name]


def kernel(**inputs):
    n = 8
    x = np.asarray(inputs["L0_attn_x"])
    nc = _prog("mega", build_fused)
    dummy = np.zeros((2, 2), np.float32)
    in_maps = []
    for b in range(n):
        d = {"rin0": np.ascontiguousarray(x[b].T),
             "onesg": np.ones((128, 128), np.float32)}
        for key, _, mk in STAGES:
            sub = mk(inputs, b, dummy) if mk is not l0_attn_moe_inputs else \
                mk(inputs, b, dummy, dummy)
            for k2, v in sub.items():
                if k2 in ("rin", "rin2", "rin3"):
                    continue
                if k2 in ("ident", "identb", "onesg"):
                    d[k2] = v
                else:
                    d[key + "_" + k2] = v
        in_maps.append(d)
    rr = run_bass_kernel_spmd(nc, in_maps, core_ids=list(range(n)))
    return np.stack([rr.results[b]["rout"].T for b in range(n)]).astype(np.float32)


def kernel_multi(**inputs):
    n = 8
    x = np.asarray(inputs["L0_attn_x"])
    resid = [np.ascontiguousarray(x[b].T) for b in range(n)]

    def launch(name, builder, mk, rin, extra=None):
        nc = _prog(name, builder)
        in_maps = []
        for b in range(n):
            if extra is None:
                m = mk(inputs, b, rin[b])
            else:
                m = mk(inputs, b, rin[b], extra[b])
            m["onesg"] = np.ones((128, 128), np.float32)
            in_maps.append(m)
        rr = run_bass_kernel_spmd(nc, in_maps, core_ids=list(range(n)))
        return [np.ascontiguousarray(rr.results[b]["rout"]) for b in range(n)]

    resid = launch("l0ffn", build_l0_ffn, l0_ffn_inputs, resid)
    inner = launch("l0wkv", build_l0_attn_wkv, l0_attn_wkv_inputs, resid)
    resid = launch("l0moe", build_l0_attn_moe, l0_attn_moe_inputs, inner,
                   extra=resid)
    resid = launch("l0ffn", build_l0_ffn, l0_ffn_inputs, resid)
    resid = launch("l1ffn", build_l1_ffn, l1_ffn_inputs, resid)
    resid = launch("l1attn", build_l1_attn, l1_attn_inputs, resid)
    resid = launch("l1ffn", build_l1_ffn, l1_ffn_inputs, resid)
    resid = launch("l2ffn", build_l2_ffn, l2_ffn_inputs, resid)
    resid = launch("l2attn", build_l2_attn, l2_attn_inputs, resid)
    resid = launch("l2ffn", build_l2_ffn, l2_ffn_inputs, resid)
    resid = launch("l3gsa", build_l3_ffn, l3_ffn_inputs, resid)
    resid = launch("l3attn", build_l3_attn, l3_attn_inputs, resid)
    resid = launch("l3gsa", build_l3_ffn, l3_ffn_inputs, resid)
    return np.stack([r.T for r in resid]).astype(np.float32)



CS = 1920

def build_l2_attn(phase=99, fz=None):
    p, _rin, _rout, _rin3, px, fused = _fz_head(fz, "l2_attn")
    ld, st, sv, ss, tt, sg = std_sems(p)
    res = ln_prologue(p, ld, _rin)
    w_wv = p.dram_in(px + "WVT", [D, D], F32)
    w_wo = p.dram_in(px + "WOT", [D, D], F32)
    w_unused = [p.dram_in(px + nm, shp, F32) for nm, shp in
                [("UWQ", [D, D]), ("UWK", [D, D]), ("UCW", [D, 256]),
                 ("UCB", [1, D])]]
    ones = p.dram_in(px + "ones", [128, 128], F32)
    ones_f = ones_root(p, ld)
    utouch = p.sbuf("utouch", [1, 1024], F32)
    for iu, u in enumerate(w_unused):
        p.Y(lambda e, d=utouch, s=u, iu=iu: e.dma_start(
            out=d[0:1, iu * 256:(iu + 1) * 256], in_=s[0:1, 0:256]),
            inc=(ld, 16))
    stage = p.sbuf("stage", [128, DT * D], F32)
    wvb = p.sbuf("wvb", [128, DT * D], BF16)
    wob = p.sbuf("wob", [128, DT * D], BF16)
    p.Y(lambda e, d=stage, s=w_wv.rearrange("(k p) c -> p k c", p=128):
        e.dma_start(out=d[:].rearrange("p (k c) -> p k c", c=D), in_=s),
        inc=(ld, 16))
    p.V.wait(ld)
    p.V(lambda e, d=wvb, s=stage: e.tensor_copy(d[:], s[:]))
    p.V.fence(inc=(sv, 1))
    p.Y.wait(sv)
    p.Y(lambda e, d=stage, s=w_wo.rearrange("(k p) c -> p k c", p=128):
        e.dma_start(out=d[:].rearrange("p (k c) -> p k c", c=D), in_=s),
        inc=(ld, 16))
    p.V.wait(ld)
    p.V(lambda e, d=wob, s=stage: e.tensor_copy(d[:], s[:]))
    p.V.fence(inc=(sv, 1))
    p.S.wait(ld)
    p.T.wait(ld)

    xc_b = p.sbuf("xc_b", [128, DT * T], BF16)
    rstd = p.sbuf("rstd", [128, T], F32)
    sq = p.sbuf("sq", [128, DT * 512], F32)
    ps0, ps1 = p.psum("ps0", [128, 512]), p.psum("ps1", [128, 512])
    emit_ln(p, res, xc_b, rstd, ps0, ps1, sq, ones_f, tt, sv, ss)

    psA = p.psum("psA", [128, DT])
    psB = p.psum("psB", [128, DT])
    v0b = p.sbuf("v0b", [128, DT], BF16)
    rcol = p.sbuf("rcol", [128, DT], F32)
    p.T.wait(sv)
    p.T.wait(ss)
    for mi in range(DT):
        for ki in range(DT):
            p.T(lambda e, o=psA, w=wvb, x=xc_b, ki=ki, mi=mi:
                e.matmul(o[:, mi:mi + 1], w[:, ki * D + mi * 128: ki * D + mi * 128 + 128],
                         x[:, ki * T + CS: ki * T + CS + 1],
                         start=(ki == 0), stop=(ki == DT - 1)),
                inc=(tt, 1) if ki == DT - 1 else None)
    p.V.wait(tt)
    p.V(lambda e, d=v0b, s=psA: e.tensor_copy(d[:], s[:]))
    p.V.fence(inc=(sv, 1))
    p.T.wait(sv)
    for mi in range(DT):
        for ki in range(DT):
            p.T(lambda e, o=psB, w=wob, x=v0b, ki=ki, mi=mi:
                e.matmul(o[:, mi:mi + 1], w[:, ki * D + mi * 128: ki * D + mi * 128 + 128],
                         x[:, ki:ki + 1],
                         start=(ki == 0), stop=(ki == DT - 1)),
                inc=(tt, 1) if ki == DT - 1 else None)
    p.V.wait(tt)
    p.V(lambda e, d=rcol, s=psB, r=rstd: e.tensor_scalar(
        d[:], s[:], r[:, CS:CS + 1], None, ALU.mult))
    p.V.fence()
    tmp = sq
    for di in range(DT):
        p.V(lambda e, d=tmp, s=xc_b, r=rstd, di=di: e.tensor_mul(
            d[:, 0:CS], s[:, di * T: di * T + CS], r[:, 0:CS]))
        p.V.fence()
        p.V(lambda e, d=res, s=tmp, di=di: e.tensor_add(
            d[:, di * T: di * T + CS], d[:, di * T: di * T + CS], s[:, 0:CS]))
        p.V(lambda e, d=res, r=rcol, di=di: e.tensor_scalar(
            d[:, di * T + CS: di * T + T], d[:, di * T + CS: di * T + T],
            r[:, di:di + 1], None, ALU.add))
        p.V.fence()
    p.V.fence(inc=(sv, 1))
    store_rout(p, res, st, wait_sems=[sv], rout_ap=_rout)
    return _fin(p, fused)


def l2_attn_inputs(inputs, core, rin_fm):
    return _with_ones({
        "rin": rin_fm,
        "WVT": np.ascontiguousarray(inputs["L2_attn_Wv"].T),
        "WOT": np.ascontiguousarray(inputs["L2_attn_Wo"].T),
        "UWQ": np.asarray(inputs["L2_attn_Wq"]),
        "UWK": np.asarray(inputs["L2_attn_Wk"]),
        "UCW": np.asarray(inputs["L2_attn_conv_w"]).reshape(D, 256),
        "UCB": np.asarray(inputs["L2_attn_conv_b"]).reshape(1, D),
        "ones": np.ones((128, 128), np.float32),
    })



def emit_router_tm(p, res, mean, rstd, rtf, E_, ps_lg, ps_tr, lg_fm, prob_tm,
                   ident_f, ones_f, tt, sv, ss, softmax=True, bias_col=None):
    swc = p.sbuf("rt_swc", [128, 1], F32)
    for ki in range(DT):
        p.T(lambda e, o=ps_lg, w=rtf, x=ones_f, ki=ki, E_=E_:
            e.matmul(o[0:E_, 0:1], w[:, ki * E_: (ki + 1) * E_], x[:, 0:1],
                     start=(ki == 0), stop=(ki == DT - 1)),
            inc=(tt, 1) if ki == DT - 1 else None)
    p.V.wait(tt)
    p.V(lambda e, d=swc, s=ps_lg, E_=E_: e.tensor_copy(d[0:E_, :], s[0:E_, 0:1]))
    p.V.fence(inc=(sv, 1))
    p.T.wait(sv)
    for c in range(NCH):
        for ki in range(DT):
            p.T(lambda e, o=ps_lg, w=rtf, x=res, ki=ki, c=c, E_=E_:
                e.matmul(o[0:E_, 0:512], w[:, ki * E_: (ki + 1) * E_],
                         x[:, ki * T + c * 512: ki * T + c * 512 + 512],
                         start=(ki == 0), stop=(ki == DT - 1)),
                inc=(tt, 1) if ki == DT - 1 else None)
        p.V.wait(tt)
        p.V(lambda e, d=lg_fm, m=mean, w=swc, c=c, E_=E_: e.tensor_scalar(
            d[0:E_, c * 512:c * 512 + 512], m[0:E_, c * 512:c * 512 + 512],
            w[0:E_, 0:1], None, ALU.mult))
        p.V.fence()
        p.V(lambda e, d=lg_fm, s=ps_lg, c=c, E_=E_: e.tensor_sub(
            d[0:E_, c * 512:c * 512 + 512], s[0:E_, 0:512],
            d[0:E_, c * 512:c * 512 + 512]))
        p.V.fence()
        p.V(lambda e, d=lg_fm, r=rstd, c=c, E_=E_: e.tensor_mul(
            d[0:E_, c * 512:c * 512 + 512], d[0:E_, c * 512:c * 512 + 512],
            r[0:E_, c * 512:c * 512 + 512]))
        if bias_col is not None:
            p.V.fence()
            p.V(lambda e, d=lg_fm, b=bias_col, c=c, E_=E_: e.tensor_scalar(
                d[0:E_, c * 512:c * 512 + 512], d[0:E_, c * 512:c * 512 + 512],
                b[0:E_, 0:1], None, ALU.add))
        p.V.fence(inc=(sv, 1))
        p.T.wait(sv)
    for b2 in range(T // 256):
        for half in range(2):
            tt_i = b2 * 2 + half
            p.T(lambda e, o=ps_tr, s=lg_fm, tt_i=tt_i, half=half, idf=ident_f:
                e.transpose(o[:, half * 128:half * 128 + 128],
                            s[:, tt_i * 128:tt_i * 128 + 128], idf[:]),
                inc=(tt, 1) if half == 1 else None)
        p.V.wait(tt)
        for half in range(2):
            tt_i = b2 * 2 + half
            p.V(lambda e, d=prob_tm, s=ps_tr, tt_i=tt_i, half=half, E_=E_:
                e.tensor_copy(d[:, tt_i * E_:(tt_i + 1) * E_],
                              s[:, half * 128:half * 128 + E_]))
        p.V.fence(inc=(sv, 1))
        p.T.wait(sv)
    if softmax:
        nmx = p.sbuf("rt_nmx", [128, 16], F32)
        den = p.sbuf("rt_den", [128, 16], F32)
        p.V(lambda e, d=nmx, s=prob_tm, E_=E_: e.tensor_reduce(
            d[:].rearrange("p (a b) -> p a b", b=1),
            s[:].rearrange("p (a b) -> p a b", b=E_), AX.X, ALU.max, negate=True))
        p.V.fence(inc=(sv, 1))
        p.S.wait(sv)
        for ti in range(16):
            p.S(lambda e, d=prob_tm, s=prob_tm, m=nmx, ti=ti, E_=E_: e.activation(
                d[:, ti * E_:(ti + 1) * E_], s[:, ti * E_:(ti + 1) * E_],
                AF.Exp, bias=m[:, ti:ti + 1]))
        p.S.fence(inc=(ss, 1))
        p.V.wait(ss)
        p.V(lambda e, d=den, s=prob_tm, E_=E_: e.tensor_reduce(
            d[:].rearrange("p (a b) -> p a b", b=1),
            s[:].rearrange("p (a b) -> p a b", b=E_), AX.X, ALU.add))
        p.V.fence()
        p.V(lambda e, d=den: e.reciprocal(d[:], d[:]))
        p.V.fence()
        for ti in range(16):
            p.V(lambda e, d=prob_tm, m=den, ti=ti, E_=E_: e.tensor_scalar(
                d[:, ti * E_:(ti + 1) * E_], d[:, ti * E_:(ti + 1) * E_],
                m[:, ti:ti + 1], None, ALU.mult))
        p.V.fence(inc=(sv, 1))


def emit_top2_combine(p, prob_tm, comb_tm, E_, sv):
    mx8 = p.sbuf("t2_mx8", [128, 8], F32)
    s2 = p.sbuf("t2_s2", [128, 1], F32)
    zap = p.sbuf("t2_zap", [128, E_], F32)
    for ti in range(16):
        sl = (ti * E_, (ti + 1) * E_)
        p.V(lambda e, d=mx8, s=prob_tm, sl=sl: e.max(out=d[:], in_=s[:, sl[0]:sl[1]]))
        p.V.fence()
        p.V(lambda e, d=s2, s=mx8: e.tensor_add(d[:], s[:, 0:1], s[:, 1:2]))
        p.V(lambda e, d=mx8: e.memset(d[:, 2:8], -1e30))
        p.V.fence()
        p.V(lambda e, d=zap, s=prob_tm, m=mx8, sl=sl: e.match_replace(
            out=d[:], in_to_replace=m[:], in_values=s[:, sl[0]:sl[1]], imm_value=0.0))
        p.V(lambda e, d=s2: e.reciprocal(d[:], d[:]))
        p.V.fence()
        p.V(lambda e, d=comb_tm, s=prob_tm, z=zap, sl=sl: e.tensor_sub(
            d[:, sl[0]:sl[1]], s[:, sl[0]:sl[1]], z[:]))
        p.V.fence()
        p.V(lambda e, d=comb_tm, m=s2, sl=sl: e.tensor_scalar(
            d[:, sl[0]:sl[1]], d[:, sl[0]:sl[1]], m[:, 0:1], None, ALU.mult))
        p.V.fence()
    p.V.fence(inc=(sv, 1))


def emit_comb_fm(p, comb_tm, comb_fm, E_, ident_f, ps_tr, tt, sv):
    p.T.wait(sv)
    for b2 in range(T // 256):
        for half in range(2):
            ti = b2 * 2 + half
            p.T(lambda e, o=ps_tr, s=comb_tm, ti=ti, half=half, idf=ident_f, E_=E_:
                e.transpose(o[0:E_, half * 128:half * 128 + 128],
                            s[:, ti * E_:(ti + 1) * E_], idf[:]),
                inc=(tt, 1) if half == 1 else None)
        p.V.wait(tt)
        for half in range(2):
            ti = b2 * 2 + half
            p.V(lambda e, d=comb_fm, s=ps_tr, ti=ti, half=half, E_=E_:
                e.tensor_copy(d[0:E_, ti * 128:(ti + 1) * 128],
                              s[0:E_, half * 128:half * 128 + 128]))
        p.V.fence(inc=(sv, 1))
        p.T.wait(sv)


def build_l1_ffn(phase=99, fz=None):
    p, _rin, _rout, _rin3, px, fused = _fz_head(fz, "l1_ffn")
    E_, H_ = 8, 4096
    FH = 256
    NFH = H_ // FH
    p0 = p
    ld, st, sv, ss, tt, sg = std_sems(p)
    res = ln_prologue(p, ld, _rin)
    w_rt = p.dram_in(px + "RWT", [D, E_], F32)
    w_fc = p.dram_in(px + "WFC", [E_ * D, H_], F32)
    w_pj = p.dram_in(px + "WPJ", [E_ * H_, D], F32)
    w_bf = p.dram_in(px + "BFC", [128, E_ * (H_ // 128)], F32)
    w_bp = p.dram_in(px + "BPJ", [128, D], BF16)
    ones = p.dram_in(px + "ones", [128, 128], F32)
    selh = p.dram_in(px + "SELH", [E_, E_ * 128], F32)

    ones_f = ones_root(p, ld)
    idf = ones_f_id(p, ld)
    rtf, _ = load_fm(p, "rt", w_rt, D, E_, ld, bf16=False)
    bfc = p.sbuf("bfc", [128, E_ * (H_ // 128)], F32)
    p.Y(lambda e, d=bfc, s=w_bf: e.dma_start(out=d[:], in_=s), inc=(ld, 16))
    bpj_b = p.sbuf("bpj_b", [128, D], BF16)
    p.Y(lambda e, d=bpj_b, s=w_bp: e.dma_start(out=d[:], in_=s), inc=(ld, 16))
    self_f = p.sbuf("self_f", [128, E_ * 128], F32)
    p.Y(lambda e, d=self_f, s=selh: e.dma_start(out=d[0:E_, :], in_=s), inc=(ld, 16))

    p.V.wait(ld)
    p.S.wait(ld)
    p.T.wait(ld)
    p.V.fence(inc=(sv, 1))

    xc_b = p.sbuf("xc_b", [128, DT * T], BF16)
    rstd = p.sbuf("rstd", [128, T], F32)
    sq = p.sbuf("sq", [128, DT * 512], F32)
    ps0, ps1 = p.psum("ps0", [128, 512]), p.psum("ps1", [128, 512])
    mean = emit_ln(p, res, xc_b, rstd, ps0, ps1, sq, ones_f, tt, sv, ss)

    lg_fm = p.sbuf("lg_fm", [128, T], F32)
    prob_tm = p.sbuf("prob_tm", [128, 16 * E_], F32)
    p.T.wait(sv)
    p.T.wait(ss)
    p.V.wait(ss)
    emit_router_tm(p, res, mean, rstd, rtf, E_, ps0, ps1, lg_fm, prob_tm,
                   idf, ones_f, tt, sv, ss)
    comb_tm = p.sbuf("comb_tm", [128, 16 * E_], F32)
    emit_top2_combine(p, prob_tm, comb_tm, E_, sv)
    comb_fm = lg_fm
    emit_comb_fm(p, comb_tm, comb_fm, E_, idf, ps1, tt, sv)
    if phase == 1:
        rout1 = _rout if fused else p.dram_out("rout", [D, T], F32)
        p.G.wait(sv)
        p.G(lambda e, d=rout1, s=comb_fm: e.dma_start(out=d[0:128, :], in_=s[:]),
            inc=(st, 16))
        p.G.wait(st)
        return _fin(p, fused)

    y = res
    combh_b = p.sbuf("combh_b", [128, T], BF16)
    p.V(lambda e, d=combh_b, s=comb_fm: e.tensor_scalar_mul(
        d[0:E_, :], s[0:E_, :], 0.5))
    p.V.fence(inc=(sv, 1))
    p.T.wait(sv)
    banks = [p.psum(f"pm{i}", [128, 512]) for i in range(4)]
    for mi in range(DT):
        for c in range(NCH):
            p.T(lambda e, o=banks[c], w=bpj_b, x=combh_b, mi=mi, c=c, E_=E_:
                e.matmul(o[:], w[0:E_, mi * 128 + 0: mi * 128 + 128] if False else
                         w[0:E_, mi * 128: mi * 128 + 128],
                         x[0:E_, c * 512: c * 512 + 512], start=True, stop=True),
                inc=(tt, 1))
        p.V.wait(tt)
        for c in range(NCH):
            p.V(lambda e, d=y, s=banks[c], mi=mi, c=c: e.tensor_copy(
                d[:, mi * T + c * 512: mi * T + c * 512 + 512], s[:]))
        p.V.fence(inc=(sv, 1))
        p.T.wait(sv)

    if phase == 2:
        rout2 = p.dram_out("rout", [D, T], F32)
        p.G.wait(sv)
        p.G(lambda e, d=rout2, s=y: e.dma_start(
            out=d.rearrange("(k p) t -> p k t", p=128),
            in_=s[:].rearrange("p (k t) -> p k t", t=T)), inc=(st, 16))
        p.G.wait(st)
        return _fin(p, fused)

    l2 = p.sem("l2")
    p.V.wait(ss)
    for di in range(DT):
        p.V(lambda e, d=xc_b, r=rstd, di=di: e.tensor_mul(
            d[:, di * T:(di + 1) * T], d[:, di * T:(di + 1) * T], r[:]))
    p.V.fence(inc=(sv, 1))
    stage_fc = p.sbuf("stage_fc", [128, DT * FH], F32)
    stage_pj = p.sbuf("stage_pj", [128, (FH // 128) * D], F32)
    wfc_b = [p.sbuf("wfc_bA", [128, 2 * DT * FH], BF16),
             Carve(rstd, 0, 2 * DT * FH, BF16)]
    wpj_b = [p.sbuf("wpjA", [128, 2 * (FH // 128) * D], BF16),
             p.sbuf("wpjB", [128, 2 * (FH // 128) * D], BF16)]
    hb = [Carve(mean, 0, (FH // 128) * 2 * 512, BF16),
          Carve(mean, 1024, (FH // 128) * 2 * 512, BF16)]
    cbr_b = p.sbuf("cbr_b", [128, T], BF16)
    pc = ps0
    HB = [p.psum("ph0", [128, 512]), ps1, banks[0], banks[1]]
    PJ = [banks[2], banks[3]]
    sgb = sg.n
    tmark = {}
    wmark = {0: 0, 1: 0}
    hmark = {0: 0, 1: 0}
    pmark = {0: 0, 1: 0}
    cbrmark = {0: 0, 1: 0}
    n_exp = 1 if phase == 3 else E_
    for e_i in range(n_exp):
        p.T.wait(sv)
        for c in range(NCH):
            p.T(lambda e, o=pc, s=self_f, cf=comb_fm, e_i=e_i, c=c, E_=E_:
                e.matmul(o[:], s[0:E_, e_i * 128:(e_i + 1) * 128],
                         cf[0:E_, c * 512:c * 512 + 512], start=True, stop=True),
                inc=(tt, 1))
            p.V.wait(tt)
            p.V(lambda e, d=cbr_b, s=pc, c=c: e.tensor_copy(
                d[:, c * 512:c * 512 + 512], s[:]))
            p.V.fence(inc=(sv, 1))
            p.T.wait(sv)
        for pr in range(NFH // 2):
            it2 = e_i * (NFH // 2) + pr
            par = it2 % 2
            if it2 > 0:
                p.Y.wait(sg, sgb + 2 * it2)
                p.G.wait(sg, sgb + 2 * it2)
            for half in range(2):
                fh = 2 * pr + half
                if half == 1:
                    p.Y.wait(sg, sgb + 2 * it2 + 1)
                    p.G.wait(sg, sgb + 2 * it2 + 1)
                p.Y(lambda e, d=stage_fc, s=w_fc, e_i=e_i, fh=fh: e.dma_start(
                    out=d[:].rearrange("p (k c) -> p k c", c=FH),
                    in_=s.rearrange("(E k p) h -> p E k h", p=128, E=E_)[
                        :, e_i, :, fh * FH:(fh + 1) * FH]), inc=(l2, 16))
                p.G(lambda e, d=stage_pj, s=w_pj, e_i=e_i, fh=fh: e.dma_start(
                    out=d[:].rearrange("p (k c) -> p k c", c=D),
                    in_=s.rearrange("(E k p) h -> p E k h", p=128, E=E_)[
                        :, e_i, fh * (FH // 128):(fh + 1) * (FH // 128), :]),
                    inc=(st, 16))
                p.V.wait(l2)
                p.V.wait(st)
                if half == 0:
                    p.V.wait(tt, wmark[par])
                p.V(lambda e, d=wfc_b[par], s=stage_fc, half=half: e.tensor_copy(
                    d[:, half * DT * FH:(half + 1) * DT * FH], s[:]))
                p.V(lambda e, d=wpj_b[par], s=stage_pj, half=half: e.tensor_copy(
                    d[:, half * (FH // 128) * D:(half + 1) * (FH // 128) * D],
                    s[:]))
                p.V.fence(inc=(sg, 1))
            p.T.wait(sg)
            for c in range(NCH):
                hpar = c % 2
                p.S.wait(tt, hmark[hpar])
                p.S.wait(sv, cbrmark[hpar])
                for half in range(2):
                    fh = 2 * pr + half
                    for hm in range(FH // 128):
                        bank = HB[(half * 2 + hm) % 4]
                        p.T.wait(ss, tmark.get(id(bank), 0))
                        for ki in range(DT):
                            p.T(lambda e, o=bank, w=wfc_b[par], x=xc_b, ki=ki,
                                hm=hm, c=c, half=half:
                                e.matmul(o[:], w[:, half * DT * FH + ki * FH + hm * 128:
                                               half * DT * FH + ki * FH + hm * 128 + 128],
                                         x[:, ki * T + c * 512: ki * T + c * 512 + 512],
                                         start=(ki == 0), stop=(ki == DT - 1)),
                                inc=(tt, 1) if ki == DT - 1 else None)
                        p.S.wait(tt)
                        p.S(lambda e, d=hb[hpar], s=bank, b=bfc, e_i=e_i,
                            fh=fh, hm=hm, half=half:
                            e.activation(d[:, (half * 2 + hm) * 512:
                                           (half * 2 + hm) * 512 + 512], s[:],
                                         AF.Gelu_apprx_tanh,
                                         bias=b[:, e_i * 32 + fh * (FH // 128) + hm:
                                                e_i * 32 + fh * (FH // 128) + hm + 1]))
                        p.S.fence(inc=(ss, 1))
                        tmark[id(bank)] = ss.n
                        p.V.wait(ss)
                        p.V(lambda e, d=hb[hpar], cb=cbr_b, hm=hm,
                            c=c, half=half: e.tensor_mul(
                                d[:, (half * 2 + hm) * 512:
                                   (half * 2 + hm) * 512 + 512],
                                d[:, (half * 2 + hm) * 512:
                                   (half * 2 + hm) * 512 + 512],
                                cb[:, c * 512: c * 512 + 512]))
                        p.V.fence(inc=(sv, 1))
                cbrmark[hpar] = sv.n
                p.T.wait(sv)
                for mi in range(DT):
                    bank = PJ[mi % 2]
                    p.T.wait(sv, tmark.get(id(bank) + 1, 0))
                    for ki in range(2 * (FH // 128)):
                        p.T(lambda e, o=bank, w=wpj_b[par], x=hb[hpar], ki=ki,
                            mi=mi:
                            e.matmul(o[:], w[:, ki * D + mi * 128: ki * D + mi * 128 + 128],
                                     x[:, ki * 512: ki * 512 + 512],
                                     start=(ki == 0),
                                     stop=(ki == 2 * (FH // 128) - 1)),
                            inc=(tt, 1) if ki == 2 * (FH // 128) - 1 else None)
                    p.V.wait(tt)
                    p.V(lambda e, d=y, s=bank, mi=mi, c=c: e.tensor_add(
                        d[:, mi * T + c * 512: mi * T + c * 512 + 512],
                        d[:, mi * T + c * 512: mi * T + c * 512 + 512], s[:]))
                    p.V.fence(inc=(sv, 1))
                    tmark[id(bank) + 1] = sv.n
                hmark[hpar] = tt.n
            wmark[par] = tt.n
    rin2 = _rin if fused else p.dram_in("rin2", [D, T], F32)
    rout = _rout if fused else p.dram_out("rout", [D, T], F32)
    rld = sq
    l3 = p.sem("l3")
    for mi in range(DT):
        p.Y(lambda e, d=rld, s=rin2, mi=mi: e.dma_start(
            out=d[:, 0:T], in_=s.rearrange("(k p) t -> p k t", p=128)[:, mi, :]),
            inc=(l3, 16))
        p.V.wait(l3)
        p.V(lambda e, d=y, s=rld, mi=mi: e.tensor_add(
            d[:, mi * T:(mi + 1) * T], d[:, mi * T:(mi + 1) * T], s[:, 0:T]))
        p.V.fence(inc=(sv, 1))
        p.G.wait(sv)
        p.G(lambda e, d=rout, s=y, mi=mi: e.dma_start(
            out=d.rearrange("(k p) t -> p k t", p=128)[:, mi, :],
            in_=s[:, mi * T:(mi + 1) * T]), inc=(st, 16))
        p.Y.wait(sv, sv.n)
    p.G.wait(st)
    return _fin(p, fused)


def ones_root(p, ld):
    if not hasattr(p, "_onesf"):
        src2 = p.dram_in("onesg", [128, 128], F32)
        of = p.sbuf_root("onesf_g", [128, 128], F32)
        p.Y(lambda e, d=of, s=src2: e.dma_start(out=d[:], in_=s), inc=(ld, 16))
        p._onesf = of
    return p._onesf


def ones_f_id(p, ld):
    if not hasattr(p, "_idf"):
        iden = p.dram_in("ident", [128, 128], F32)
        idf = p.sbuf_root("idf", [128, 128], F32)
        p.Y(lambda e, d=idf, s=iden: e.dma_start(out=d[:], in_=s), inc=(ld, 16))
        p._idf = idf
    return p._idf


def l1_ffn_inputs(inputs, core, rin_fm):
    E_, H_ = 8, 4096
    bfc = inputs["L1_ffn_b_fc"]
    bfc_cols = np.zeros((128, E_ * (H_ // 128)), np.float32)
    for e in range(E_):
        for hm in range(H_ // 128):
            bfc_cols[:, e * 32 + hm] = bfc[e, hm * 128:(hm + 1) * 128]
    bpj = np.zeros((128, D), np.float32)
    bpj[0:E_, :] = inputs["L1_ffn_b_proj"]
    selh = np.zeros((E_, E_ * 128), np.float32)
    for e in range(E_):
        selh[e, e * 128:(e + 1) * 128] = 0.5
    import ml_dtypes
    return _with_ones({
        "rin": rin_fm,
        "rin2": rin_fm,
        "RWT": np.ascontiguousarray(inputs["L1_ffn_router_w"].T),
        "WFC": np.ascontiguousarray(inputs["L1_ffn_w_fc"].reshape(E_ * D, H_)),
        "WPJ": np.ascontiguousarray(inputs["L1_ffn_w_proj"].reshape(E_ * H_, D)),
        "BFC": bfc_cols,
        "BPJ": bpj.astype(ml_dtypes.bfloat16),
        "ones": np.ones((128, 128), np.float32),
        "ident": np.eye(128, dtype=np.float32),
        "SELH": selh,
    })



def emit_group_top4(p, prob_tm, comb_tm, sv):
    E_ = 16
    gmax = p.sbuf("g_m1", [128, 4], F32)
    gmax2 = p.sbuf("g_m2", [128, 4], F32)
    geq = p.sbuf("g_geq", [128, E_], F32)
    gsc = p.sbuf("g_gsc", [128, 4], F32)
    thr = p.sbuf("g_thr", [128, 1], F32)
    m1g = p.sbuf("g_m1g", [128, 1], F32)
    allow = p.sbuf("g_allow", [128, 4], F32)
    m16 = p.sbuf("g_m16", [128, E_], F32)
    mx8 = p.sbuf("g_mx8", [128, 8], F32)
    zap = p.sbuf("g_zap", [128, E_], F32)
    ind = p.sbuf("g_ind", [128, E_], F32)
    wsum = p.sbuf("g_ws", [128, 1], F32)
    for ti in range(16):
        sl = (ti * E_, (ti + 1) * E_)
        sc = lambda s, sl=sl: s[:, sl[0]:sl[1]]
        p.V(lambda e, d=gmax, s=prob_tm, sl=sl: e.tensor_reduce(
            d[:].rearrange("p (a b) -> p a b", b=1),
            s[:, sl[0]:sl[1]].rearrange("p (a b) -> p a b", b=4), AX.X, ALU.max))
        p.V.fence()
        p.V(lambda e, d=geq, s=prob_tm, g=gmax, sl=sl: e.tensor_tensor(
            d[:].rearrange("p (a b) -> p a b", b=4),
            s[:, sl[0]:sl[1]].rearrange("p (a b) -> p a b", b=4),
            g[:].rearrange("p (a b) -> p a b", b=1).to_broadcast([128, 4, 4]),
            ALU.is_ge))
        p.V.fence()
        p.V(lambda e, d=geq, s=prob_tm, sl=sl: e.scalar_tensor_tensor(
            d[:], d[:], -1e30, s[:, sl[0]:sl[1]], ALU.mult, ALU.add))
        p.V.fence()
        p.V(lambda e, d=gmax2, s=geq: e.tensor_reduce(
            d[:].rearrange("p (a b) -> p a b", b=1),
            s[:].rearrange("p (a b) -> p a b", b=4), AX.X, ALU.max))
        p.V.fence()
        p.V(lambda e, d=gsc, a=gmax, b=gmax2: e.tensor_add(d[:], a[:], b[:]))
        p.V.fence()
        p.V(lambda e, d=m1g, s=gsc: e.tensor_reduce(
            d[:].rearrange("p (a b) -> p a b", b=1),
            s[:].rearrange("p (a b) -> p a b", b=4), AX.X, ALU.max))
        p.V.fence()
        p.V(lambda e, d=allow, s=gsc, m=m1g: e.tensor_scalar(
            d[:], s[:], m[:, 0:1], None, ALU.is_ge))
        p.V.fence()
        p.V(lambda e, d=allow, s=gsc: e.scalar_tensor_tensor(
            d[:], d[:], -1e30, s[:], ALU.mult, ALU.add))
        p.V.fence()
        p.V(lambda e, d=thr, s=allow: e.tensor_reduce(
            d[:].rearrange("p (a b) -> p a b", b=1),
            s[:].rearrange("p (a b) -> p a b", b=4), AX.X, ALU.max))
        p.V.fence()
        p.V(lambda e, d=allow, s=gsc, m=thr: e.tensor_scalar(
            d[:], s[:], m[:, 0:1], None, ALU.is_ge))
        p.V.fence()
        p.V(lambda e, d=m16, s=prob_tm, a=allow, sl=sl: e.tensor_tensor(
            d[:].rearrange("p (a b) -> p a b", b=4),
            a[:].rearrange("p (a b) -> p a b", b=1).to_broadcast([128, 4, 4]),
            s[:, sl[0]:sl[1]].rearrange("p (a b) -> p a b", b=4), ALU.bypass)
            if False else
            e.scalar_tensor_tensor(
                d[:].rearrange("p (a b) -> p a b", b=4),
                a[:].rearrange("p (a b) -> p a b", b=1).to_broadcast([128, 4, 4]),
                10.0,
                s[:, sl[0]:sl[1]].rearrange("p (a b) -> p a b", b=4),
                ALU.mult, ALU.add))
        p.V.fence()
        p.V(lambda e, d=mx8, s=m16: e.max(out=d[:], in_=s[:]))
        p.V.fence()
        p.V(lambda e, d=mx8: e.memset(d[:, 4:8], -1e30))
        p.V.fence()
        p.V(lambda e, d=zap, s=m16, m=mx8: e.match_replace(
            out=d[:], in_to_replace=m[:], in_values=s[:], imm_value=0.0))
        p.V.fence()
        p.V(lambda e, d=m16, s=m16, z=zap: e.tensor_sub(d[:], s[:], z[:]))
        p.V.fence()
        p.V(lambda e, d=ind, s=m16: e.tensor_scalar(
            d[:], s[:], 5.0, None, ALU.is_gt))
        p.V.fence()
        p.V(lambda e, d=m16, i=ind, s=m16: e.scalar_tensor_tensor(
            d[:], i[:], -10.0, s[:], ALU.mult, ALU.add))
        p.V.fence()
        p.V(lambda e, d=wsum, s=m16: e.tensor_reduce(
            d[:].rearrange("p (a b) -> p a b", b=1),
            s[:].rearrange("p (a b) -> p a b", b=E_), AX.X, ALU.add))
        p.V.fence()
        p.V(lambda e, d=wsum: e.tensor_scalar_add(d[:], d[:], 1e-6))
        p.V.fence()
        p.V(lambda e, d=wsum: e.reciprocal(d[:], d[:]))
        p.V.fence()
        p.V(lambda e, d=comb_tm, s=m16, m=wsum, sl=sl: e.tensor_scalar(
            d[:, sl[0]:sl[1]], s[:], m[:, 0:1], None, ALU.mult))
        p.V.fence()
    p.V.fence(inc=(sv, 1))


def build_l0_ffn(phase=99, fz=None):
    p, _rin, _rout, _rin3, px, fused = _fz_head(fz, "l0_ffn")
    E_ = 16
    NE = 18
    F_ = 512
    FH = 256
    NFH = F_ // FH
    ld, st, sv, ss, tt, sg = std_sems(p)
    res = ln_prologue(p, ld, _rin)
    w_rt = p.dram_in(px + "RWT", [D, E_], F32)
    w_1 = p.dram_in(px + "W1CAT", [NE * D, F_], F32)
    w_3 = p.dram_in(px + "W3CAT", [NE * D, F_], F32)
    w_2 = p.dram_in(px + "W2CAT", [NE * F_, D], F32)
    ones = p.dram_in(px + "ones", [128, 128], F32)
    selh = p.dram_in(px + "SELH", [E_, E_ * 128], F32)

    ones_f = ones_root(p, ld)
    idf = ones_f_id(p, ld)
    rtf, _ = load_fm(p, "rt", w_rt, D, E_, ld, bf16=False)
    self_f = p.sbuf("self_f", [128, E_ * 128], F32)
    p.Y(lambda e, d=self_f, s=selh: e.dma_start(out=d[0:E_, :], in_=s), inc=(ld, 16))

    p.V.wait(ld)
    p.S.wait(ld)
    p.T.wait(ld)
    p.V.fence(inc=(sv, 1))

    xc_b = p.sbuf("xc_b", [128, DT * T], BF16)
    rstd = p.sbuf("rstd", [128, T], F32)
    sq = p.sbuf("sq", [128, DT * 512], F32)
    ps0, ps1 = p.psum("ps0", [128, 512]), p.psum("ps1", [128, 512])
    mean = emit_ln(p, res, xc_b, rstd, ps0, ps1, sq, ones_f, tt, sv, ss)

    lg_fm = p.sbuf("lg_fm", [128, T], F32)
    prob_tm = p.sbuf("prob_tm", [128, 16 * E_], F32)
    p.T.wait(sv)
    p.T.wait(ss)
    p.V.wait(ss)
    emit_router_tm(p, res, mean, rstd, rtf, E_, ps0, ps1, lg_fm, prob_tm,
                   idf, ones_f, tt, sv, ss, softmax=False)
    p.S.wait(sv)
    p.S(lambda e, d=prob_tm: e.activation(d[:], d[:], AF.Sigmoid))
    p.S.fence(inc=(ss, 1))
    p.V.wait(ss)
    comb_tm = p.sbuf("comb_tm", [128, 16 * E_], F32)
    emit_group_top4(p, prob_tm, comb_tm, sv)
    comb_fm = lg_fm
    emit_comb_fm(p, comb_tm, comb_fm, E_, idf, ps1, tt, sv)
    if phase == 1:
        rout1 = _rout if fused else p.dram_out("rout", [D, T], F32)
        p.G.wait(sv)
        p.G(lambda e, d=rout1, s=comb_fm: e.dma_start(out=d[0:128, :], in_=s[:]),
            inc=(st, 16))
        p.G.wait(st)
        return _fin(p, fused)

    y = res
    p.V.wait(sv)
    p.V(lambda e, d=y: e.memset(d[:], 0.0))
    p.V.fence(inc=(sv, 1))

    l2 = p.sem("l2")
    stage_1 = p.sbuf("stage_1", [128, DT * FH], F32)
    stage_3 = p.sbuf("stage_3", [128, DT * FH], F32)
    stage_2 = p.sbuf("stage_2", [128, (FH // 128) * D], F32)
    w1b = p.sbuf("w1b", [128, DT * FH], BF16)
    w3b = p.sbuf("w3b", [128, DT * FH], BF16)
    w2b = p.sbuf("w2b", [128, (FH // 128) * D], BF16)
    gblk = p.sbuf("gblk", [128, (FH // 128) * 512], F32)
    ublk = p.sbuf("ublk", [128, (FH // 128) * 512], F32)
    gub = p.sbuf("gub", [128, (FH // 128) * 512], BF16)
    cbr_b = p.sbuf("cbr_b", [128, T], BF16)
    pc = ps0
    ph = [p.psum("ph0", [128, 512]), ps1]
    banks = [p.psum(f"pm{i}", [128, 512]) for i in range(4)]
    evh = p.sem("evh")
    sgb = sg.n
    n_exp = 1 if phase == 3 else NE
    for e_i in range(n_exp):
        p.T.wait(sv)
        if e_i < 16:
            for c in range(NCH):
                p.T(lambda e, o=pc, s=self_f, cf=comb_fm, e_i=e_i, c=c, E_=E_:
                    e.matmul(o[:], s[0:E_, e_i * 128:(e_i + 1) * 128],
                             cf[0:E_, c * 512:c * 512 + 512], start=True, stop=True),
                    inc=(tt, 1))
                p.V.wait(tt)
                p.V(lambda e, d=cbr_b, s=pc, c=c: e.tensor_copy(
                    d[:, c * 512:c * 512 + 512], s[:]))
                p.V.fence(inc=(sv, 1))
                p.T.wait(sv)
        elif e_i == 16:
            p.V(lambda e, d=cbr_b: e.memset(d[:], 0.5))
            p.V.fence(inc=(sv, 1))
            p.T.wait(sv)
        for fh in range(NFH):
            it = e_i * NFH + fh
            if it > 0:
                p.Y.wait(sg, sgb + it)
                p.G.wait(sg, sgb + it)
            p.Y(lambda e, d=stage_1, s=w_1, e_i=e_i, fh=fh: e.dma_start(
                out=d[:].rearrange("p (k c) -> p k c", c=FH),
                in_=s.rearrange("(E k p) h -> p E k h", p=128, E=NE)[
                    :, e_i, :, fh * FH:(fh + 1) * FH]), inc=(l2, 16))
            p.Y(lambda e, d=stage_3, s=w_3, e_i=e_i, fh=fh: e.dma_start(
                out=d[:].rearrange("p (k c) -> p k c", c=FH),
                in_=s.rearrange("(E k p) h -> p E k h", p=128, E=NE)[
                    :, e_i, :, fh * FH:(fh + 1) * FH]), inc=(l2, 16))
            p.G(lambda e, d=stage_2, s=w_2, e_i=e_i, fh=fh: e.dma_start(
                out=d[:].rearrange("p (k c) -> p k c", c=D),
                in_=s.rearrange("(E k p) h -> p E k h", p=128, E=NE, k=F_ // 128)[
                    :, e_i, fh * (FH // 128):(fh + 1) * (FH // 128), :]),
                inc=(st, 16))
            p.V.wait(l2)
            p.V.wait(st)
            p.V(lambda e, d=w1b, s=stage_1: e.tensor_copy(d[:], s[:]))
            p.V(lambda e, d=w3b, s=stage_3: e.tensor_copy(d[:], s[:]))
            p.V(lambda e, d=w2b, s=stage_2: e.tensor_copy(d[:], s[:]))
            p.V.fence(inc=(sg, 1))
            p.T.wait(sg)
            for c in range(NCH):
                for hm in range(FH // 128):
                    bankg = ph[0]
                    banku = ph[1]
                    p.T.wait(evh)
                    for ki in range(DT):
                        p.T(lambda e, o=bankg, w=w1b, x=xc_b, ki=ki, hm=hm, c=c:
                            e.matmul(o[:], w[:, ki * FH + hm * 128: ki * FH + hm * 128 + 128],
                                     x[:, ki * T + c * 512: ki * T + c * 512 + 512],
                                     start=(ki == 0), stop=(ki == DT - 1)))
                    for ki in range(DT):
                        p.T(lambda e, o=banku, w=w3b, x=xc_b, ki=ki, hm=hm, c=c:
                            e.matmul(o[:], w[:, ki * FH + hm * 128: ki * FH + hm * 128 + 128],
                                     x[:, ki * T + c * 512: ki * T + c * 512 + 512],
                                     start=(ki == 0), stop=(ki == DT - 1)),
                            inc=(tt, 1) if ki == DT - 1 else None)
                    p.V.wait(tt)
                    p.V(lambda e, d=gblk, s=bankg, r=rstd, hm=hm, c=c: e.tensor_mul(
                        d[:, hm * 512: hm * 512 + 512], s[:],
                        r[:, c * 512: c * 512 + 512]))
                    p.V(lambda e, d=ublk, s=banku, r=rstd, hm=hm, c=c: e.tensor_mul(
                        d[:, hm * 512: hm * 512 + 512], s[:],
                        r[:, c * 512: c * 512 + 512]))
                    p.V.fence(inc=(sv, 1))
                    p.S.wait(sv)
                    p.S(lambda e, d=gblk, hm=hm: e.activation(
                        d[:, hm * 512: hm * 512 + 512], d[:, hm * 512: hm * 512 + 512],
                        AF.Silu))
                    p.S.fence(inc=(ss, 1))
                    p.V.wait(ss)
                    p.V(lambda e, d=gub, g=gblk, u=ublk, hm=hm: e.tensor_mul(
                        d[:, hm * 512: hm * 512 + 512], g[:, hm * 512: hm * 512 + 512],
                        u[:, hm * 512: hm * 512 + 512]))
                    p.V.fence()
                    p.V(lambda e, d=gub, cb=cbr_b, hm=hm, c=c: e.tensor_mul(
                        d[:, hm * 512: hm * 512 + 512], d[:, hm * 512: hm * 512 + 512],
                        cb[:, c * 512: c * 512 + 512]))
                    p.V.fence(inc=(evh, 1))
                    p.T.wait(evh)
                for mi in range(DT):
                    bank = banks[mi % 2]
                    for ki in range(FH // 128):
                        p.T(lambda e, o=bank, w=w2b, x=gub, ki=ki, mi=mi:
                            e.matmul(o[:], w[:, ki * D + mi * 128: ki * D + mi * 128 + 128],
                                     x[:, ki * 512: ki * 512 + 512],
                                     start=(ki == 0), stop=(ki == FH // 128 - 1)),
                            inc=(tt, 1) if ki == FH // 128 - 1 else None)
                    p.V.wait(tt)
                    p.V(lambda e, d=y, s=bank, mi=mi, c=c: e.tensor_add(
                        d[:, mi * T + c * 512: mi * T + c * 512 + 512],
                        d[:, mi * T + c * 512: mi * T + c * 512 + 512], s[:]))
                    p.V.fence(inc=(sv, 1))
                    p.T.wait(sv)
    rin2 = _rin if fused else p.dram_in("rin2", [D, T], F32)
    rout = _rout if fused else p.dram_out("rout", [D, T], F32)
    rld = sq
    l3 = p.sem("l3")
    for mi in range(DT):
        p.Y(lambda e, d=rld, s=rin2, mi=mi: e.dma_start(
            out=d[:, 0:T], in_=s.rearrange("(k p) t -> p k t", p=128)[:, mi, :]),
            inc=(l3, 16))
        p.V.wait(l3)
        p.V(lambda e, d=y, s=rld, mi=mi: e.tensor_add(
            d[:, mi * T:(mi + 1) * T], d[:, mi * T:(mi + 1) * T], s[:, 0:T]))
        p.V.fence(inc=(sv, 1))
        p.G.wait(sv)
        p.G(lambda e, d=rout, s=y, mi=mi: e.dma_start(
            out=d.rearrange("(k p) t -> p k t", p=128)[:, mi, :],
            in_=s[:, mi * T:(mi + 1) * T]), inc=(st, 16))
        p.Y.wait(sv, sv.n)
    p.G.wait(st)
    return _fin(p, fused)


def l0_ffn_inputs(inputs, core, rin_fm):
    E_, F_, NE = 16, 512, 18
    w1 = inputs["L0_ffn_w1_e"]
    w3 = inputs["L0_ffn_w3_e"]
    w2 = inputs["L0_ffn_w2_e"]
    w1s = inputs["L0_ffn_w1_s"]
    w3s = inputs["L0_ffn_w3_s"]
    w2s = inputs["L0_ffn_w2_s"]
    W1 = np.zeros((NE * D, F_), np.float32)
    W3 = np.zeros((NE * D, F_), np.float32)
    W2 = np.zeros((NE * F_, D), np.float32)
    for e in range(16):
        W1[e * D:(e + 1) * D] = w1[e].T
        W3[e * D:(e + 1) * D] = w3[e].T
        W2[e * F_:(e + 1) * F_] = w2[e].T
    w1sT, w3sT, w2sT = w1s.T, w3s.T, w2s.T
    W1[16 * D:17 * D] = w1sT[:, 0:512]
    W1[17 * D:18 * D] = w1sT[:, 512:1024]
    W3[16 * D:17 * D] = w3sT[:, 0:512]
    W3[17 * D:18 * D] = w3sT[:, 512:1024]
    W2[16 * F_:17 * F_] = w2sT[0:512]
    W2[17 * F_:18 * F_] = w2sT[512:1024]
    selh = np.zeros((E_, E_ * 128), np.float32)
    for e in range(E_):
        selh[e, e * 128:(e + 1) * 128] = 0.5
    return _with_ones({
        "rin": rin_fm,
        "rin2": rin_fm,
        "RWT": np.ascontiguousarray(inputs["L0_ffn_w_gate"].T),
        "W1CAT": W1, "W3CAT": W3, "W2CAT": W2,
        "ones": np.ones((128, 128), np.float32),
        "ident": np.eye(128, dtype=np.float32),
        "SELH": selh,
    })



LAMBDA_INIT = 0.8 - 0.6 * math.exp(-0.3 * 12)

def build_l1_attn(phase=99, fz=None):
    p, _rin, _rout, _rin3, px, fused = _fz_head(fz, "l1_attn")
    ld, st, sv, ss, tt, sg = std_sems(p)
    res = ln_prologue(p, ld, _rin)
    w_q = p.dram_in(px + "WQT", [D, D], F32)
    w_k = p.dram_in(px + "WKT", [D, D], F32)
    w_v = p.dram_in(px + "WVT", [D, D], F32)
    w_o = p.dram_in(px + "WOT", [D, D], F32)
    ones = p.dram_in(px + "ones", [128, 128], F32)
    lam_in = p.dram_in(px + "LAM", [1, 128], F32)
    gam_in = p.dram_in(px + "GAM", [1, 64], F32)
    cos_in = p.dram_in(px + "COS", [128, 256], F32)
    sin_in = p.dram_in(px + "SIN", [128, 256], F32)
    tril_in = p.dram_in(px + "TRILN", [128, 128], F32)

    ones_f = ones_root(p, ld)
    idf = ones_f_id(p, ld)
    lam = p.sbuf("lam", [1, 128], F32)
    p.Y(lambda e, d=lam, s=lam_in: e.dma_start(out=d[:], in_=s), inc=(ld, 16))
    gam = p.sbuf("gam", [1, 64], F32)
    p.Y(lambda e, d=gam, s=gam_in: e.dma_start(out=d[:], in_=s), inc=(ld, 16))
    cosb = p.sbuf("cosb", [128, 256], F32)
    p.Y(lambda e, d=cosb, s=cos_in: e.dma_start(out=d[:], in_=s), inc=(ld, 16))
    sinb = p.sbuf("sinb", [128, 256], F32)
    p.Y(lambda e, d=sinb, s=sin_in: e.dma_start(out=d[:], in_=s), inc=(ld, 16))
    trilb = p.sbuf("trilb", [128, 128], F32)
    p.Y(lambda e, d=trilb, s=tril_in: e.dma_start(out=d[:], in_=s), inc=(ld, 16))
    idf_b(p, ld)

    xc_b = p.sbuf("xc_b", [128, DT * T], BF16)
    rstd = p.sbuf("rstd", [128, T], F32)
    sq = p.sbuf("sq", [128, DT * 512], F32)
    ps0, ps1 = p.psum("ps0", [128, 512]), p.psum("ps1", [128, 512])
    pa, pb = p.psum("pa", [128, 512]), p.psum("pb", [128, 512])
    pt = p.psum("pt", [128, 512])
    po = p.psum("po", [128, 512])
    po2 = p.psum("po2", [128, 512])
    ptb = p.psum("ptb", [128, 1024], BF16)
    p.V.wait(ld)
    p.S.wait(ld)
    p.T.wait(ld)
    emit_ln(p, res, xc_b, rstd, ps0, ps1, sq, ones_f, tt, sv, ss)

    rstd_tm = p.sbuf("rstd_tm", [128, 16], F32)
    p.T.wait(ss)
    for tt_i in range(16):
        p.T(lambda e, o=pt, s=rstd, tt_i=tt_i, idf=idf: e.transpose(
            o[:, (tt_i % 4) * 128:(tt_i % 4) * 128 + 128],
            s[:, tt_i * 128:(tt_i + 1) * 128], idf[:]),
            inc=(tt, 1) if tt_i % 4 == 3 else None)
        if tt_i % 4 == 3:
            p.V.wait(tt)
            for j in range(tt_i - 3, tt_i + 1):
                p.V(lambda e, d=rstd_tm, o=pt, j=j: e.tensor_copy(
                    d[:, j:j + 1], o[:, (j % 4) * 128:(j % 4) * 128 + 1]))
            p.V.fence(inc=(sv, 1))
            p.T.wait(sv)

    lsc = p.sbuf("lsc", [1, 64], F32)
    negl = p.sbuf("negl", [128, 1], F32)
    gamb = p.sbuf("gamb", [128, 64], F32)
    p.V(lambda e, d=lsc, s=lam: e.tensor_tensor(
        d[0:1, 0:32], s[0:1, 0:32], s[0:1, 32:64], ALU.mult))
    p.V.fence()
    p.V(lambda e, d=lsc, s=lsc: e.tensor_reduce(
        d[0:1, 36:37], s[0:1, 0:32], AX.X, ALU.add))
    p.V.fence()
    p.V(lambda e, d=lsc, s=lam: e.tensor_tensor(
        d[0:1, 0:32], s[0:1, 64:96], s[0:1, 96:128], ALU.mult))
    p.V.fence()
    p.V(lambda e, d=lsc, s=lsc: e.tensor_reduce(
        d[0:1, 37:38], s[0:1, 0:32], AX.X, ALU.add))
    p.V.fence(inc=(sv, 1))
    p.S.wait(sv)
    p.S(lambda e, d=lsc: e.activation(d[0:1, 36:38], d[0:1, 36:38], AF.Exp))
    p.S.fence(inc=(ss, 1))
    p.V.wait(ss)
    p.V(lambda e, d=lsc: e.tensor_sub(d[0:1, 36:37], d[0:1, 37:38], d[0:1, 36:37]))
    p.V.fence()
    p.V(lambda e, d=lsc: e.tensor_scalar(
        d[0:1, 36:37], d[0:1, 36:37], LAMBDA_INIT, None, ALU.subtract))
    p.V.fence(inc=(sv, 1))
    p.T.wait(sv)
    p.T(lambda e, o=pt, w=ones_f, s=lsc: e.matmul(
        o[:, 0:1], w[0:1, :], s[0:1, 36:37], start=True, stop=True), inc=(tt, 1))
    p.T(lambda e, o=pt, w=ones_f, s=gam: e.matmul(
        o[:, 128:192], w[0:1, :], s[0:1, :], start=True, stop=True), inc=(tt, 1))
    p.V.wait(tt)
    p.V(lambda e, d=negl, s=pt: e.tensor_copy(d[:], s[:, 0:1]))
    p.V(lambda e, d=gamb, s=pt: e.tensor_scalar_mul(
        d[:], s[:, 128:192], 1.0 - LAMBDA_INIT))
    p.V.fence(inc=(sv, 1))

    resb = res[:].bitcast(BF16)
    QOF, KOF = 0, DT * T
    v_tm = p.sbuf("v_tm", [128, 16 * D], BF16)
    stage = p.sbuf("stage", [128, DT * D // 4], F32)
    wb = p.sbuf("wb", [128, DT * D], BF16)
    qsc = p.sbuf("qsc", [128, D], F32)
    t_A, t_B, t_C, t_D2 = (SqView(sq, i) for i in range(4))

    l2 = p.sem("l2")

    def load_w(whbm):
        for q4 in range(4):
            p.Y.wait(sg, sg.n)
            p.Y(lambda e, d=stage, s=whbm, q4=q4: e.dma_start(
                out=d[:].rearrange("p (k c) -> p k c", c=D),
                in_=s.rearrange("(k p) c -> p k c", p=128)[:, q4 * 2:(q4 + 1) * 2, :]),
                inc=(l2, 16))
            p.V.wait(l2)
            p.V(lambda e, d=wb, s=stage, q4=q4: e.tensor_copy(
                d[:, q4 * 2 * D:(q4 + 1) * 2 * D], s[:]))
            p.V.fence(inc=(sg, 1))
        p.T.wait(sg)

    SCL = 1.0 / math.sqrt(32.0)

    def proj_pass(which):
        for tt_i in range(16):
            for jc in range(2):
                bank = ps0 if jc == 0 else ps1
                for ki in range(DT):
                    p.T(lambda e, o=bank, x=xc_b, w=wb, ki=ki, jc=jc, tt_i=tt_i:
                        e.matmul(o[:], x[:, ki * T + tt_i * 128: ki * T + tt_i * 128 + 128],
                                 w[:, ki * D + jc * 512: ki * D + jc * 512 + 512],
                                 start=(ki == 0), stop=(ki == DT - 1)),
                        inc=(tt, 1) if ki == DT - 1 else None)
            p.V.wait(tt)
            if which == "v":
                for jc in range(2):
                    bank = ps0 if jc == 0 else ps1
                    p.V(lambda e, d=v_tm, s=bank, r=rstd_tm, tt_i=tt_i, jc=jc:
                        e.tensor_scalar(d[:, tt_i * D + jc * 512: tt_i * D + jc * 512 + 512],
                                        s[:], r[:, tt_i:tt_i + 1], None, ALU.mult))
                p.V.fence(inc=(sv, 1))
                p.T.wait(sv)
                continue
            for jc in range(2):
                bank = ps0 if jc == 0 else ps1
                p.V(lambda e, d=qsc, s=bank, r=rstd_tm, tt_i=tt_i, jc=jc:
                    e.tensor_scalar(d[:, jc * 512:jc * 512 + 512], s[:],
                                    r[:, tt_i:tt_i + 1], None, ALU.mult))
            p.V.fence(inc=(sv, 1))
            p.T.wait(sv)
            x1 = qsc[:].rearrange("p (a b) -> p a b", b=2)[:, :, 0].rearrange(
                "p (a b) -> p a b", b=16)
            x2 = qsc[:].rearrange("p (a b) -> p a b", b=2)[:, :, 1].rearrange(
                "p (a b) -> p a b", b=16)
            ca = cosb[:, tt_i * 16:(tt_i + 1) * 16].unsqueeze(1).to_broadcast(
                [128, 32, 16])
            sa = sinb[:, tt_i * 16:(tt_i + 1) * 16].unsqueeze(1).to_broadcast(
                [128, 32, 16])
            v3 = lambda t: t[:].rearrange("p (a b) -> p a b", b=16)
            p.V(lambda e, d=v3(t_A), x=x1, c=ca: e.tensor_tensor(d, x, c, ALU.mult))
            p.V(lambda e, d=v3(t_B), x=x1, c=sa: e.tensor_tensor(d, x, c, ALU.mult))
            p.V(lambda e, d=v3(t_C), x=x2, c=sa: e.tensor_tensor(d, x, c, ALU.mult))
            p.V(lambda e, d=v3(t_D2), x=x2, c=ca: e.tensor_tensor(d, x, c, ALU.mult))
            p.V.fence()
            p.V(lambda e, o=x1, a=v3(t_A), c=v3(t_C): e.tensor_tensor(
                o, a, c, ALU.subtract))
            p.V(lambda e, o=x2, b=v3(t_B), d2=v3(t_D2): e.tensor_tensor(
                o, b, d2, ALU.add))
            p.V.fence()
            if which == "q":
                p.V(lambda e, d=qsc: e.tensor_scalar_mul(d[:], d[:], SCL))
                p.V.fence()
            p.V.fence(inc=(sv, 1))
            p.T.wait(sv)
            base = QOF if which == "q" else KOF
            for blk in range(DT):
                p.T(lambda e, o=pt, s=qsc, blk=blk, idf=idf: e.transpose(
                    o[:, (blk % 4) * 128:(blk % 4) * 128 + 128],
                    s[:, blk * 128:(blk + 1) * 128], idf[:]),
                    inc=(tt, 1) if blk % 4 == 3 else None)
                if blk % 4 == 3:
                    p.V.wait(tt)
                    for j in range(blk - 3, blk + 1):
                        p.V(lambda e, d=resb, o=pt, j=j, tt_i=tt_i, base=base:
                            e.tensor_copy(
                                d[:, base + j * T + tt_i * 128: base + j * T + tt_i * 128 + 128],
                                o[:, (j % 4) * 128:(j % 4) * 128 + 128]))
                    p.V.fence(inc=(sv, 1))
                    p.T.wait(sv)

    load_w(w_q)
    proj_pass("q")
    load_w(w_k)
    proj_pass("k")
    load_w(w_v)
    proj_pass("v")

    if phase == 1:
        rout1 = _rout if fused else p.dram_out("rout", [D, T], F32)
        p.G.wait(sv)
        p.G(lambda e, d=rout1, s=res: e.dma_start(
            out=d.rearrange("(k p) t -> p k t", p=128),
            in_=s[:].rearrange("p (k t) -> p k t", t=T)), inc=(st, 16))
        p.G.wait(st)
        return _fin(p, fused)

    wv_tm = xc_b[:].bitcast(BF16)
    P_b = p.sbuf("P_b", [128, 2048], BF16)
    PT_b = p.sbuf("PT_b", [128, 2048], BF16)
    nmx = p.sbuf("nmx", [128, 4], F32)
    nm1 = p.sbuf("nm1", [128, 1], F32)
    den = p.sbuf("den", [128, 4], F32)
    rden = p.sbuf("rden", [128, 2], F32)
    wvt = qsc
    sc1 = p.sbuf("sc1", [128, 64], F32)
    rmsb = p.sbuf("rmsb", [128, 16], F32)
    sbanks = [ps0, ps1, pa, pb]
    for qt in range(16):
        nk = qt // 4 + 1
        klen = (qt + 1) * 128
        for h in range(16):
            for ch in range(2):
                row = (2 * h + ch) * 32
                jt, p0 = row // 128, row % 128
                p.T.wait(ss)
                for kc in range(nk):
                    cl = min(512, klen - kc * 512)
                    bank = sbanks[kc]
                    p.T(lambda e, o=bank, q=res, jt=jt, p0=p0, qt=qt, kc=kc, cl=cl:
                        e.matmul(o[:, 0:cl],
                                 q[:].bitcast(BF16)[p0:p0 + 32, QOF + jt * T + qt * 128: QOF + jt * T + qt * 128 + 128],
                                 q[:].bitcast(BF16)[p0:p0 + 32, KOF + jt * T + kc * 512: KOF + jt * T + kc * 512 + cl],
                                 start=True, stop=True,
                                 tile_position=(p0, 0)),
                        inc=(tt, 1))
                p.V.wait(tt)
                dk = qt * 128 - (nk - 1) * 512
                p.V(lambda e, o=sbanks[nk - 1], m=trilb, dk=dk: e.tensor_add(
                    o[:, dk:dk + 128], o[:, dk:dk + 128], m[:]))
                p.V.fence()
                for kc in range(nk):
                    cl = min(512, klen - kc * 512)
                    p.V(lambda e, d=nmx, s=sbanks[kc], kc=kc, cl=cl: e.reduce_max(
                        d[:, kc:kc + 1], s[:, 0:cl], AX.X, negate=True))
                p.V.fence()
                p.V(lambda e, d=nm1, s=nmx, nk=nk: e.tensor_reduce(
                    d[:], s[:, 0:nk], AX.X, ALU.min))
                p.V.fence(inc=(sv, 1))
                p.S.wait(sv)
                p.S.wait(tt)
                for kc in range(nk):
                    cl = min(512, klen - kc * 512)
                    p.S(lambda e, d=P_b, s=sbanks[kc], m=nm1, a=den, kc=kc, cl=cl:
                        e.activation(d[:, kc * 512:kc * 512 + cl], s[:, 0:cl],
                                     AF.Exp, bias=m[:, 0:1],
                                     accum_out=a[:, kc:kc + 1]))
                p.S.fence(inc=(ss, 1))
                p.V.wait(ss)
                p.V(lambda e, d=rden, s=den, nk=nk, ch=ch: e.tensor_reduce(
                    d[:, ch:ch + 1], s[:, 0:nk], AX.X, ALU.add))
                p.V.fence()
                p.V(lambda e, d=rden, ch=ch: e.reciprocal(
                    d[:, ch:ch + 1], d[:, ch:ch + 1]))
                p.V.fence(inc=(sv, 1))
                nkb = (klen + 127) // 128
                p.T.wait(sv)
                idb = idf_b(p, ld)
                for g4 in range((nkb + 3) // 4):
                    hi = min(g4 * 4 + 4, nkb)
                    for j in range(g4 * 4, hi):
                        p.T(lambda e, o=ptb, s=P_b, j=j, idb=idb: e.transpose(
                            o[:, (j % 4) * 128:(j % 4) * 128 + 128],
                            s[:, j * 128:(j + 1) * 128], idb[:]),
                            inc=(tt, 1) if (j % 4 == 3 or j == nkb - 1) else None)
                    p.V.wait(tt)
                    p.V(lambda e, d=PT_b, o=ptb, g4=g4, w=(hi - g4 * 4) * 128:
                        e.tensor_copy(d[:, g4 * 512:g4 * 512 + w], o[:, 0:w]))
                    p.V.fence(inc=(sv, 1))
                    p.T.wait(sv)
                for j in range(nkb):
                    p.T(lambda e, o=(po if ch == 0 else po2), s=PT_b, v=v_tm,
                        j=j, h=h:
                        e.matmul(o[:, 0:64],
                                 s[:, j * 128:(j + 1) * 128],
                                 v[:, j * D + h * 64: j * D + h * 64 + 64],
                                 start=(j == 0), stop=(j == nkb - 1),
                                 skip_group_check=True),
                        inc=(tt, 1) if j == nkb - 1 else None)
            p.V.wait(tt)
            p.V(lambda e, d=sc1, s=po, r=rden: e.tensor_scalar(
                d[:], s[:, 0:64], r[:, 0:1], None, ALU.mult))
            p.V.fence()
            p.V(lambda e, d=wvt, s=po2, r=rden, nl=negl, h=h: e.tensor_scalar(
                d[:, h * 64:h * 64 + 64], s[:, 0:64], r[:, 1:2], nl[:, 0:1],
                ALU.mult, ALU.mult))
            p.V.fence()
            p.V(lambda e, d=wvt, s=sc1, h=h: e.tensor_add(
                d[:, h * 64:h * 64 + 64], d[:, h * 64:h * 64 + 64], s[:]))
            p.V.fence(inc=(sv, 1))
            p.T.wait(sv)
        p.S.wait(sv)
        p.S(lambda e, d=sq, s=wvt: e.activation(d[:, 0:1024], s[:], AF.Square))
        p.S.fence(inc=(ss, 1))
        p.V.wait(ss)
        p.V(lambda e, d=rmsb, s=sq: e.tensor_reduce(
            d[:].rearrange("p (a b) -> p a b", b=1),
            s[:, 0:1024].rearrange("p (a b) -> p a b", b=64), AX.X, ALU.add))
        p.V.fence()
        p.V(lambda e, d=rmsb: e.tensor_scalar(
            d[:], d[:], 1.0 / 64.0, EPS, ALU.mult, ALU.add))
        p.V.fence()
        p.V(lambda e, d=rmsb: e.reciprocal(d[:], d[:]))
        p.V.fence(inc=(sv, 1))
        p.S.wait(sv)
        p.S(lambda e, d=rmsb: e.activation(d[:], d[:], AF.Sqrt))
        p.S.fence(inc=(ss, 1))
        p.V.wait(ss)
        p.V(lambda e, d=wvt, r=rmsb: e.tensor_tensor(
            d[:].rearrange("p (a b) -> p a b", b=64),
            d[:].rearrange("p (a b) -> p a b", b=64),
            r[:].rearrange("p (a b) -> p a b", b=1).to_broadcast([128, 16, 64]),
            ALU.mult))
        p.V.fence()
        p.V(lambda e, s=wvt, g=gamb: e.tensor_tensor(
            s[:].rearrange("p (a b) -> p a b", b=64),
            s[:].rearrange("p (a b) -> p a b", b=64),
            g[:].unsqueeze(1).to_broadcast([128, 16, 64]), ALU.mult))
        p.V.fence()
        p.V(lambda e, d=xc_b, s=wvt, qt=qt: e.tensor_copy(
            d[:, qt * 1024:(qt + 1) * 1024], s[:]))
        p.V.fence(inc=(sv, 1))

    load_w(w_o)
    wv_fm = resb
    for tt_i in range(16):
        p.T.wait(sv)
        idb = idf_b(p, ld)
        for blk in range(DT):
            p.T(lambda e, o=ptb, s=xc_b, blk=blk, tt_i=tt_i, idb=idb: e.transpose(
                o[:, (blk % 4) * 128:(blk % 4) * 128 + 128],
                s[:, tt_i * 1024 + blk * 128: tt_i * 1024 + blk * 128 + 128],
                idb[:]),
                inc=(tt, 1) if blk % 4 == 3 else None)
            if blk % 4 == 3:
                p.V.wait(tt)
                for j in range(blk - 3, blk + 1):
                    p.V(lambda e, d=wv_fm, o=ptb, j=j, tt_i=tt_i: e.tensor_copy(
                        d[:, j * T + tt_i * 128: j * T + tt_i * 128 + 128],
                        o[:, (j % 4) * 128:(j % 4) * 128 + 128]))
                p.V.fence(inc=(sv, 1))
                p.T.wait(sv)

    rin2 = _rin if fused else p.dram_in("rin2", [D, T], F32)
    rout = _rout if fused else p.dram_out("rout", [D, T], F32)
    rld = sq
    obuf = stage
    l3b = p.sem("l3b")
    for mi in range(DT):
        p.V.wait(st, st.n)
        p.Y(lambda e, d=rld, s=rin2, mi=mi: e.dma_start(
            out=d[:, 0:T], in_=s.rearrange("(k p) t -> p k t", p=128)[:, mi, :]),
            inc=(l3b, 16))
        for c in range(NCH):
            bank = sbanks[c]
            for ki in range(DT):
                p.T(lambda e, o=bank, w=wb, x=wv_fm, ki=ki, mi=mi, c=c:
                    e.matmul(o[:], w[:, ki * D + mi * 128: ki * D + mi * 128 + 128],
                             x[:, ki * T + c * 512: ki * T + c * 512 + 512],
                             start=(ki == 0), stop=(ki == DT - 1)),
                    inc=(tt, 1) if ki == DT - 1 else None)
        p.V.wait(tt)
        p.V.wait(l3b)
        for c in range(NCH):
            p.V(lambda e, d=obuf, s=sbanks[c], r=rld, c=c: e.tensor_add(
                d[:, c * 512:c * 512 + 512], s[:], r[:, c * 512:c * 512 + 512]))
        p.V.fence(inc=(sv, 1))
        p.G.wait(sv)
        p.G(lambda e, d=rout, s=obuf, mi=mi: e.dma_start(
            out=d.rearrange("(k p) t -> p k t", p=128)[:, mi, :], in_=s[:, 0:T]),
            inc=(st, 16))
        p.T.wait(sv)
        p.Y.wait(sv, sv.n)
    p.G.wait(st)
    return _fin(p, fused)


class SqView:
    def __init__(self, sb, i):
        self.sb, self.i = sb, i
    def __getitem__(self, key):
        assert key == slice(None)
        return self.sb[:, self.i * 512:(self.i + 1) * 512]


class Carve:
    def __init__(self, sb, off_f32, n, dt=F32):
        self.sb, self.off, self.n, self.dt = sb, off_f32, n, dt
    def __getitem__(self, key):
        ap = self.sb[:]
        if self.dt == BF16:
            ap = ap.bitcast(BF16)
            off = 2 * self.off
        else:
            off = self.off
        if key == slice(None):
            return ap[:, off:off + self.n]
        ps, cs = key
        start = cs.start or 0
        stop = cs.stop if cs.stop is not None else self.n
        return ap[ps, off + start: off + stop]


class CarveB:
    def __init__(self, sb, off, n):
        self.sb, self.off, self.n = sb, off, n
    def __getitem__(self, key):
        ap = self.sb[:].bitcast(F32)
        if key == slice(None):
            return ap[:, self.off:self.off + self.n]
        ps, cs = key
        start = cs.start or 0
        stop = cs.stop if cs.stop is not None else self.n
        return ap[:, self.off + start: self.off + stop]


def idf_b(p, ld):
    if not hasattr(p, "_idfb"):
        iden = p.dram_in("identb", [128, 128], BF16)
        idb = p.sbuf_root("idb", [128, 128], BF16)
        p.Y(lambda e, d=idb, s=iden: e.dma_start(out=d[:], in_=s), inc=(ld, 16))
        p._idfb = idb
    return p._idfb


def l1_attn_inputs(inputs, core, rin_fm):
    import ml_dtypes
    lam = np.zeros((1, 128), np.float32)
    lam[0, 0:32] = inputs["L1_attn_lambda_q1"]
    lam[0, 32:64] = inputs["L1_attn_lambda_k1"]
    lam[0, 64:96] = inputs["L1_attn_lambda_q2"]
    lam[0, 96:128] = inputs["L1_attn_lambda_k2"]
    inv = 1.0 / 10000.0 ** (np.arange(0, 32, 2, dtype=np.float32) / 32)
    t = np.arange(T, dtype=np.float32)
    fr = np.outer(t, inv)
    cos = np.cos(fr).reshape(16, 128, 16).transpose(1, 0, 2).reshape(128, 256)
    sin = np.sin(fr).reshape(16, 128, 16).transpose(1, 0, 2).reshape(128, 256)
    tril = np.where(np.arange(128)[:, None] >= np.arange(128)[None, :],
                    0.0, -1e30).astype(np.float32)
    return _with_ones({
        "rin": rin_fm, "rin2": rin_fm,
        "WQT": np.ascontiguousarray(inputs["L1_attn_Wq"].T),
        "WKT": np.ascontiguousarray(inputs["L1_attn_Wk"].T),
        "WVT": np.ascontiguousarray(inputs["L1_attn_Wv"].T),
        "WOT": np.ascontiguousarray(inputs["L1_attn_Wo"].T),
        "ones": np.ones((128, 128), np.float32),
        "ident": np.eye(128, dtype=np.float32),
        "identb": np.eye(128, dtype=np.float32).astype(ml_dtypes.bfloat16),
        "LAM": lam,
        "GAM": inputs["L1_attn_gamma"].reshape(1, 64).astype(np.float32),
        "COS": np.ascontiguousarray(cos), "SIN": np.ascontiguousarray(sin),
        "TRILN": tril,
    })



def build_l3_attn(phase=99, fz=None):
    p, _rin, _rout, _rin3, px, fused = _fz_head(fz, "l3_attn")
    TC = 256
    NTC = T // TC
    DI = 2048
    DP = DI // 128
    ld, st, sv, ss, tt, sg = std_sems(p)
    res = ln_prologue(p, ld, _rin)
    w_in = p.dram_in(px + "WIN", [D, 2 * DI], F32)
    w_out = p.dram_in(px + "WOUT", [DI, D], F32)
    w_x = p.dram_in(px + "WXP", [DI, 128], F32)
    w_dt = p.dram_in(px + "WDT", [128, DI], F32)
    ones = p.dram_in(px + "ones", [128, 128], F32)
    selj = p.dram_in(px + "SELJ", [128, 16 * 128], BF16)
    selbc = p.dram_in(px + "SELBC", [128, 2 * 128], F32)
    dmask = p.dram_in(px + "DMASK", [128, 16 * 128], BF16)
    acol_in = p.dram_in(px + "ACOL", [128, 256], F32)
    convw_in = p.dram_in(px + "CONVW", [128, DP * 4], F32)
    convb_in = p.dram_in(px + "CONVB", [128, DP], F32)
    dskip_in = p.dram_in(px + "DSKIP", [128, DP], F32)
    bdt_in = p.dram_in(px + "BDT", [128, DP], F32)
    lng_in = p.dram_in(px + "LNG", [128, DT], F32)
    lnb_in = p.dram_in(px + "LNB", [128, DT], F32)

    ones_f = ones_root(p, ld)
    selj_b = p.sbuf("selj_b", [128, 16 * 128], BF16)
    p.Y(lambda e, d=selj_b, s=selj: e.dma_start(out=d[:], in_=s), inc=(ld, 16))
    selbc_f = p.sbuf("selbc_f", [128, 2 * 128], F32)
    p.Y(lambda e, d=selbc_f, s=selbc: e.dma_start(out=d[:], in_=s), inc=(ld, 16))
    dmask_b = p.sbuf("dmask_b", [128, 16 * 128], BF16)
    p.Y(lambda e, d=dmask_b, s=dmask: e.dma_start(out=d[:], in_=s), inc=(ld, 16))
    acol = p.sbuf("acol", [128, 256], F32)
    p.Y(lambda e, d=acol, s=acol_in: e.dma_start(out=d[:], in_=s), inc=(ld, 16))
    convw = p.sbuf("convw", [128, DP * 4], F32)
    p.Y(lambda e, d=convw, s=convw_in: e.dma_start(out=d[:], in_=s), inc=(ld, 16))
    convb = p.sbuf("convb", [128, DP], F32)
    p.Y(lambda e, d=convb, s=convb_in: e.dma_start(out=d[:], in_=s), inc=(ld, 16))
    dskip = p.sbuf("dskip", [128, DP], F32)
    p.Y(lambda e, d=dskip, s=dskip_in: e.dma_start(out=d[:], in_=s), inc=(ld, 16))
    bdt = p.sbuf("bdt", [128, DP], F32)
    p.Y(lambda e, d=bdt, s=bdt_in: e.dma_start(out=d[:], in_=s), inc=(ld, 16))
    lng = p.sbuf("lng", [128, DT], F32)
    p.Y(lambda e, d=lng, s=lng_in: e.dma_start(out=d[:], in_=s), inc=(ld, 16))
    lnb = p.sbuf("lnb", [128, DT], F32)
    p.Y(lambda e, d=lnb, s=lnb_in: e.dma_start(out=d[:], in_=s), inc=(ld, 16))
    wx_f = p.sbuf("wx_f", [128, DP * 128], F32)
    p.Y(lambda e, d=wx_f, s=w_x.rearrange("(k p) c -> p k c", p=128):
        e.dma_start(out=d[:].rearrange("p (k c) -> p k c", c=128), in_=s),
        inc=(ld, 16))
    wdt_f = p.sbuf("wdt_f", [128, DI], F32)
    p.Y(lambda e, d=wdt_f, s=w_dt: e.dma_start(out=d[:], in_=s), inc=(ld, 16))

    p.V.wait(ld)
    p.S.wait(ld)
    p.T.wait(ld)
    wx_b = p.sbuf("wx_b", [128, DP * 128], BF16)
    cast_w(p, [(wx_f, wx_b)], "V")
    p.S(lambda e, d=acol: e.activation(d[:], d[:], AF.Exp))
    p.S.fence()
    p.V.fence(inc=(sv, 1))
    p.S(lambda e, d=acol: e.activation(d[:], d[:], AF.Identity, scale=-1.0))
    p.S.fence(inc=(ss, 1))

    xc_b = p.sbuf("xc_b", [128, DT * T], BF16)
    rstd = p.sbuf("rstd", [128, T], F32)
    sq = p.sbuf("sq", [128, DT * 512], F32)
    ps0, ps1 = p.psum("ps0", [128, 512]), p.psum("ps1", [128, 512])
    pj = p.psum("pj", [128, 512])
    pu = p.psum("pu", [128, 512])
    py = p.psum("py", [128, 512])
    pw = p.psum("pw", [128, 512])
    ln_mean = emit_ln(p, res, xc_b, rstd, ps0, ps1, sq, ones_f, tt, sv, ss)

    l2 = p.sem("l2")
    stage = p.sbuf("stage", [128, DT * 128], F32)
    stage2 = p.sbuf("stage2", [128, DP * 128], F32)
    wcb = p.sbuf("wcb", [128, DT * 128], BF16)
    wcb2 = p.sbuf("wcb2", [128, DP * 128], BF16)
    xpad = p.sbuf("xpad", [128, DP * (TC + 4)], F32)
    convs = Carve(res, 0, DP * TC, F32)
    convsb = Carve(res, 6144, DP * TC, BF16)
    zs = Carve(res, 4096, DP * TC, BF16)
    dtb = Carve(res, 8192, DP * TC, BF16)
    ub = Carve(res, 10240, DP * TC, BF16)
    dbls = p.sbuf("dbls", [128, TC], F32)
    brep = p.sbuf("brep", [128, TC], F32)
    crep = p.sbuf("crep", [128, TC], F32)
    dA = p.sbuf("dA", [128, TC], F32)
    uBs = p.sbuf("uBs", [128, TC], F32)
    hbuf = p.sbuf("hbuf", [128, TC], F32)
    hCb = p.sbuf("hCb", [128, TC], BF16)
    states = p.sbuf("states", [128, 256], F32)
    ytmp = p.sbuf("ytmp", [128, TC], F32)
    yb = Carve(res, 12288, DP * TC, BF16)
    obuf = Carve(res, 14336, DT * TC, F32)
    oms = p.sbuf("oms", [128, 2 * TC], F32)
    sc1 = p.sbuf("sc1", [128, TC], F32)
    sc2 = p.sbuf("sc2", [128, TC], F32)
    rldc = ln_mean
    rin2 = _rin if fused else p.dram_in("rin2", [D, T], F32)
    rout = _rout if fused else p.dram_out("rout", [D, T], F32)

    p.V.wait(ld)
    p.V(lambda e, d=states: e.memset(d[:], 0.0))
    p.V(lambda e, d=xpad: e.memset(d[:], 0.0))
    p.V.fence(inc=(sv, 1))

    XC3 = lambda buf, w: buf[:].rearrange("p (k c) -> p k c", c=w)

    p.V.wait(ss)
    p.T.wait(sv)
    p.T.wait(ss)

    ntc_run = 1 if phase in (1, 2, 3) else NTC
    for tc in range(ntc_run):
        t0 = tc * TC
        for mi in range(2 * DP):
            p.Y.wait(sg, sg.n)
            p.Y(lambda e, d=stage, s=w_in, mi=mi: e.dma_start(
                out=d[:].rearrange("p (k c) -> p k c", c=128),
                in_=s.rearrange("(k p) c -> p k c", p=128)[:, :, mi * 128:(mi + 1) * 128]),
                inc=(l2, 16))
            p.V.wait(l2)
            p.V(lambda e, d=wcb, s=stage: e.tensor_copy(d[:], s[:]))
            p.V.fence(inc=(sg, 1))
            p.T.wait(sg)
            p.T.wait(sv)
            for ki in range(DT):
                p.T(lambda e, o=pw, w=wcb, x=xc_b, ki=ki, t0=t0:
                    e.matmul(o[:, 0:TC], w[:, ki * 128:(ki + 1) * 128],
                             x[:, ki * T + t0: ki * T + t0 + TC],
                             start=(ki == 0), stop=(ki == DT - 1)),
                    inc=(tt, 1) if ki == DT - 1 else None)
            p.V.wait(tt)
            if mi < DP:
                p.V(lambda e, d=xpad, s=pw, r=rstd, mi=mi, t0=t0: e.tensor_mul(
                    d[:, mi * (TC + 4) + 3: mi * (TC + 4) + 3 + TC], s[:, 0:TC],
                    r[:, t0:t0 + TC]))
            else:
                p.V(lambda e, d=zs, s=pw, r=rstd, mz=mi - DP, t0=t0: e.tensor_mul(
                    d[:, mz * TC:mz * TC + TC], s[:, 0:TC],
                    r[:, t0:t0 + TC]))
            p.V.fence(inc=(sv, 1))
        for dp in range(DP):
            p.V(lambda e, d=convs, s=xpad, w=convw, dp=dp: e.tensor_scalar(
                d[:, dp * TC:(dp + 1) * TC],
                s[:, dp * (TC + 4) + 0: dp * (TC + 4) + TC],
                w[:, dp * 4 + 0: dp * 4 + 1], None, ALU.mult))
        p.V.fence()
        for k in range(1, 4):
            for dp in range(DP):
                p.V(lambda e, d=convs, s=xpad, w=convw, dp=dp, k=k:
                    e.scalar_tensor_tensor(
                        d[:, dp * TC:(dp + 1) * TC],
                        s[:, dp * (TC + 4) + k: dp * (TC + 4) + k + TC],
                        w[:, dp * 4 + k: dp * 4 + k + 1],
                        d[:, dp * TC:(dp + 1) * TC], ALU.mult, ALU.add))
            p.V.fence()
        p.V.fence(inc=(sv, 1))
        p.S.wait(sv)
        for dp in range(DP):
            p.S(lambda e, d=convs, b=convb, dp=dp: e.activation(
                d[:, dp * TC:(dp + 1) * TC], d[:, dp * TC:(dp + 1) * TC],
                AF.Silu, bias=b[:, dp:dp + 1]))
        p.S.fence(inc=(ss, 1))
        p.V.wait(ss)
        if phase == 1:
            p.V.fence(inc=(sv, 1))
            p.G.wait(sv)
            p.G(lambda e, d=rout, s=convs: e.dma_start(
                out=d[0:128, 0:2048], in_=s[:, 0:2048]), inc=(st, 16))
            p.G(lambda e, d=rout, s=convs: e.dma_start(
                out=d[128:256, 0:2048], in_=s[:, 2048:4096]), inc=(st, 16))
            p.G.wait(st)
            return _fin(p, fused)
        p.V(lambda e, d=convsb, s=convs: e.tensor_copy(d[:], s[:]))
        p.S(lambda e, d=zs: e.activation(d[:], d[:], AF.Silu))
        p.S.fence(inc=(ss, 1))
        p.V(lambda e, d=xpad: e.tensor_copy(
            XC3(d, TC + 4)[:, :, 0:3], XC3(xpad, TC + 4)[:, :, TC:TC + 3]))
        p.V.fence(inc=(sv, 1))
        p.T.wait(sv)
        for ki in range(DP):
            p.T(lambda e, o=pw, w=wx_b, x=convsb, ki=ki:
                e.matmul(o[:, 0:TC], w[:, ki * 128:(ki + 1) * 128],
                         x[:, ki * TC:(ki + 1) * TC],
                         start=(ki == 0), stop=(ki == DP - 1)),
                inc=(tt, 1) if ki == DP - 1 else None)
        p.V.wait(tt)
        p.V(lambda e, d=dbls, s=pw: e.tensor_copy(d[:], s[:, 0:TC]))
        p.V.fence(inc=(sv, 1))
        p.T.wait(sv)
        p.T(lambda e, o=pj, s=selbc_f, x=dbls: e.matmul(
            o[:, 0:TC], s[64:80, 0:128], x[64:80, :], start=True, stop=True,
            tile_position=(64, 0)), inc=(tt, 1))
        p.T(lambda e, o=pu, s=selbc_f, x=dbls: e.matmul(
            o[:, 0:TC], s[96:112, 128:256], x[96:112, :], start=True, stop=True,
            tile_position=(96, 0)), inc=(tt, 1))
        p.V.wait(tt)
        p.V(lambda e, d=brep, s=pj: e.tensor_copy(d[:], s[:, 0:TC]))
        p.V(lambda e, d=crep, s=pu: e.tensor_copy(d[:], s[:, 0:TC]))
        p.V.fence(inc=(sv, 1))
        p.T.wait(sv)
        for mi in range(DP):
            p.T.wait(ss)
            p.T(lambda e, o=pw, w=wdt_f, x=dbls, mi=mi: e.matmul(
                o[:, 0:TC], w[0:64, mi * 128:(mi + 1) * 128], x[0:64, :],
                start=True, stop=True), inc=(tt, 1))
            p.V.wait(tt)
            p.S.wait(sv)
            p.S.wait(tt)
            p.S(lambda e, d=sc1, s=pw, b=bdt, mi=mi: e.activation(
                d[:], s[:, 0:TC], AF.Abs, bias=b[:, mi:mi + 1]))
            p.S.fence()
            p.S(lambda e, d=sc1: e.activation(d[:], d[:], AF.Exp, scale=-1.0))
            p.S.fence()
            p.S(lambda e, d=sc1: e.activation(d[:], d[:], AF.Ln, bias=1.0))
            p.S.fence()
            p.S(lambda e, d=sc2, s=pw, b=bdt, mi=mi: e.activation(
                d[:], s[:, 0:TC], AF.Relu, bias=b[:, mi:mi + 1]))
            p.S.fence(inc=(ss, 1))
            p.V.wait(ss)
            p.V(lambda e, d=dtb, a=sc1, b=sc2, mi=mi: e.tensor_add(
                d[:, mi * TC:(mi + 1) * TC], a[:], b[:]))
            p.V.fence()
            p.V(lambda e, d=ub, s=dtb, c=convs, mi=mi: e.tensor_mul(
                d[:, mi * TC:(mi + 1) * TC], s[:, mi * TC:(mi + 1) * TC],
                c[:, mi * TC:(mi + 1) * TC]))
            p.V.fence(inc=(sv, 1))
            p.T.wait(sv)
        if phase == 2:
            p.G.wait(sv)
            p.G(lambda e, d=rout, s=res: e.dma_start(
                out=d[0:128, :], in_=s[:, 8192:10240]), inc=(st, 16))
            p.G.wait(st)
            return _fin(p, fused)
        ndp = 1 if phase == 6 else DP
        for dp in range(ndp):
            for j in range(16):
                stc = dp * 16 + j
                p.T.wait(sv)
                p.T(lambda e, o=pj, s=selj_b, x=dtb, j=j, dp=dp: e.matmul(
                    o[:, 0:TC], s[:, j * 128:(j + 1) * 128],
                    x[:, dp * TC:(dp + 1) * TC], start=True, stop=True),
                    inc=(tt, 1))
                p.T(lambda e, o=pu, s=selj_b, x=ub, j=j, dp=dp: e.matmul(
                    o[:, 0:TC], s[:, j * 128:(j + 1) * 128],
                    x[:, dp * TC:(dp + 1) * TC], start=True, stop=True),
                    inc=(tt, 1))
                p.V.wait(tt)
                p.V(lambda e, d=dA, s=pj, a=acol, stc=stc: e.tensor_scalar(
                    d[:], s[:, 0:TC], a[:, stc:stc + 1], None, ALU.mult))
                p.V(lambda e, d=uBs, s=pu, b=brep: e.tensor_mul(d[:], s[:, 0:TC], b[:]))
                p.V.fence(inc=(sv, 1))
                p.S.wait(sv)
                p.S(lambda e, d=dA: e.activation(d[:], d[:], AF.Exp))
                p.S.fence(inc=(ss, 1))
                p.V.wait(ss)
                p.V(lambda e, d=hbuf, a=dA, u=uBs, st=states, stc=stc:
                    e.tensor_tensor_scan(d[:], a[:], u[:], st[:, stc:stc + 1],
                                         ALU.mult, ALU.add))
                p.V.fence()
                p.V(lambda e, d=states, s=hbuf, stc=stc: e.tensor_copy(
                    d[:, stc:stc + 1], s[:, TC - 1:TC]))
                p.V(lambda e, d=hCb, s=hbuf, c=crep: e.tensor_mul(d[:], s[:], c[:]))
                p.V.fence(inc=(sv, 1))
                p.T.wait(sv)
                p.T(lambda e, o=py, s=dmask_b, x=hCb, j=j: e.matmul(
                    o[:, 0:TC], s[:, j * 128:(j + 1) * 128], x[:],
                    start=True, stop=True), inc=(tt, 1))
                p.V.wait(tt)
                if j == 0:
                    p.V(lambda e, d=ytmp, o=py: e.tensor_copy(d[:], o[:, 0:TC]))
                else:
                    p.V(lambda e, d=ytmp, o=py: e.tensor_add(d[:], d[:], o[:, 0:TC]))
                p.V.fence(inc=(sv, 1))
            p.V(lambda e, d=ytmp, c=convs, k=dskip, dp=dp:
                e.scalar_tensor_tensor(d[:], c[:, dp * TC:(dp + 1) * TC],
                                       k[:, dp:dp + 1], d[:], ALU.mult, ALU.add))
            p.V.fence()
            p.V(lambda e, d=yb, s=ytmp, z=zs, dp=dp: e.tensor_mul(
                d[:, dp * TC:(dp + 1) * TC], s[:], z[:, dp * TC:(dp + 1) * TC]))
            p.V.fence(inc=(sv, 1))
            p.T.wait(sv)
        if phase == 6:
            p.G.wait(sv)
            for i2, buf in enumerate(["dA", "uBs", "hbuf", "hCb", "ytmp"]):
                pass
            p.G(lambda e, d=rout, s=dA: e.dma_start(out=d[0:128, 0:256], in_=s[:]), inc=(st, 16))
            p.G(lambda e, d=rout, s=uBs: e.dma_start(out=d[0:128, 256:512], in_=s[:]), inc=(st, 16))
            p.G(lambda e, d=rout, s=hbuf: e.dma_start(out=d[0:128, 512:768], in_=s[:]), inc=(st, 16))
            p.G(lambda e, d=rout, s=ytmp: e.dma_start(out=d[0:128, 768:1024], in_=s[:]), inc=(st, 16))
            p.G(lambda e, d=rout, s=brep: e.dma_start(out=d[0:128, 1024:1280], in_=s[:]), inc=(st, 16))
            p.G(lambda e, d=rout, s=crep: e.dma_start(out=d[0:128, 1280:1536], in_=s[:]), inc=(st, 16))
            p.G(lambda e, d=rout, s=ytmp: e.dma_start(out=d[128:256, 0:256], in_=s[:]), inc=(st, 16))
            p.G(lambda e, d=rout, s=res: e.dma_start(out=d[128:256, 256:384], in_=s[:, 12288:12416]), inc=(st, 16))
            p.G(lambda e, d=rout, s=res: e.dma_start(out=d[128:256, 384:512], in_=s[:, 4096:4224]), inc=(st, 16))
            p.G.wait(st)
            return _fin(p, fused)
        if phase == 4:
            p.G.wait(sv)
            p.G(lambda e, d=rout, s=res: e.dma_start(
                out=d[0:128, :], in_=s[:, 12288:14336]), inc=(st, 16))
            p.G.wait(st)
            return _fin(p, fused)
        for mi in range(DT):
            p.Y.wait(sg, sg.n)
            p.Y(lambda e, d=stage2, s=w_out, mi=mi: e.dma_start(
                out=d[:].rearrange("p (k c) -> p k c", c=128),
                in_=s.rearrange("(k p) c -> p k c", p=128)[:, :, mi * 128:(mi + 1) * 128]),
                inc=(l2, 16))
            p.V.wait(l2)
            p.V(lambda e, d=wcb2, s=stage2: e.tensor_copy(d[:], s[:]))
            p.V.fence(inc=(sg, 1))
            p.T.wait(sg)
            p.T.wait(sv)
            for ki in range(DP):
                p.T(lambda e, o=pw, w=wcb2, x=yb, ki=ki:
                    e.matmul(o[:, 0:TC], w[:, ki * 128:(ki + 1) * 128],
                             x[:, ki * TC:(ki + 1) * TC],
                             start=(ki == 0), stop=(ki == DP - 1)),
                    inc=(tt, 1) if ki == DP - 1 else None)
            p.V.wait(tt)
            p.V(lambda e, d=obuf, s=pw, mi=mi: e.tensor_copy(
                d[:, mi * TC:(mi + 1) * TC], s[:, 0:TC]))
            p.V.fence(inc=(sv, 1))
        if phase == 5:
            p.G.wait(sv)
            p.G(lambda e, d=rout, s=res: e.dma_start(
                out=d[0:128, :], in_=s[:, 14336:16384]), inc=(st, 16))
            p.G.wait(st)
            return _fin(p, fused)
        p.T.wait(sv)
        p.S.wait(sv)
        p.S(lambda e, d=sq, s=obuf: e.activation(
            d[:, 0:DT * TC], s[:], AF.Square))
        p.S.fence(inc=(ss, 1))
        p.T.wait(ss)
        for di in range(DT):
            p.T(lambda e, o=pj, w=ones_f, s=obuf, di=di:
                e.matmul(o[:, 0:TC], w[:], s[:, di * TC:(di + 1) * TC],
                         start=(di == 0), stop=(di == DT - 1)))
        for di in range(DT):
            p.T(lambda e, o=pu, w=ones_f, s=sq, di=di:
                e.matmul(o[:, 0:TC], w[:], s[:, di * TC:(di + 1) * TC],
                         start=(di == 0), stop=(di == DT - 1)),
                inc=(tt, 1) if di == DT - 1 else None)
        p.V.wait(tt)
        p.V(lambda e, d=oms, s=pj: e.tensor_scalar_mul(d[:, 0:TC], s[:, 0:TC], 1.0 / D))
        p.V(lambda e, d=oms, s=pu: e.tensor_scalar_mul(d[:, TC:2 * TC], s[:, 0:TC], 1.0 / D))
        p.V.fence(inc=(sv, 1))
        p.S.wait(sv)
        p.S(lambda e, d=sc1, s=oms: e.activation(d[:], s[:, 0:TC], AF.Square))
        p.S.fence(inc=(ss, 1))
        p.V.wait(ss)
        p.V(lambda e, d=sc2, a=oms, b=sc1: e.tensor_sub(d[:], a[:, TC:2 * TC], b[:]))
        p.V.fence()
        p.V(lambda e, d=sc2: e.tensor_scalar_add(d[:], d[:], EPS))
        p.V.fence()
        p.V(lambda e, d=sc2: e.reciprocal(d[:], d[:]))
        p.V.fence(inc=(sv, 1))
        p.S.wait(sv)
        p.S(lambda e, d=sc2: e.activation(d[:], d[:], AF.Sqrt))
        p.S.fence(inc=(ss, 1))
        p.Y.wait(sg, sg.n)
        p.Y(lambda e, d=rldc, s=rin2, t0=t0: e.dma_start(
            out=d[:].rearrange("p (k c) -> p k c", c=TC),
            in_=s.rearrange("(k p) t -> p k t", p=128)[:, :, t0:t0 + TC]),
            inc=(l2, 16))
        p.V.wait(l2)
        p.V.wait(ss)
        for di in range(DT):
            p.V(lambda e, d=obuf, m=oms, di=di: e.tensor_sub(
                d[:, di * TC:(di + 1) * TC], d[:, di * TC:(di + 1) * TC],
                m[:, 0:TC]))
            p.V.fence()
            p.V(lambda e, d=obuf, r=sc2, di=di: e.tensor_mul(
                d[:, di * TC:(di + 1) * TC], d[:, di * TC:(di + 1) * TC], r[:]))
            p.V.fence()
            p.V(lambda e, d=obuf, g=lng, b2=lnb, di=di: e.tensor_scalar(
                d[:, di * TC:(di + 1) * TC], d[:, di * TC:(di + 1) * TC],
                g[:, di:di + 1], b2[:, di:di + 1], ALU.mult, ALU.add))
            p.V.fence()
            p.V(lambda e, d=rldc, s=obuf, di=di: e.tensor_add(
                d[:, di * TC:(di + 1) * TC], d[:, di * TC:(di + 1) * TC],
                s[:, di * TC:(di + 1) * TC]))
        p.V.fence(inc=(sv, 1))
        p.G.wait(sv)
        p.G(lambda e, d=rout, s=rldc, t0=t0: e.dma_start(
            out=d.rearrange("(k p) t -> p k t", p=128)[:, :, t0:t0 + TC],
            in_=s[:].rearrange("p (k c) -> p k c", c=TC)), inc=(st, 16))
        p.Y.wait(sv, sv.n)
        p.Y.wait(st, st.n)
        if phase == 3:
            break
    p.G.wait(st)
    return _fin(p, fused)


def l3_attn_inputs(inputs, core, rin_fm):
    import ml_dtypes
    DI, DP = 2048, 16
    wx = inputs["L3_attn_W_x"]
    wxp = np.zeros((DI, 128), np.float32)
    wxp[:, 0:64] = wx[:, 0:64]
    wxp[:, 64:80] = wx[:, 64:80]
    wxp[:, 96:112] = wx[:, 80:96]
    wdt = np.zeros((128, DI), np.float32)
    wdt[0:64] = inputs["L3_attn_W_dt"]
    selj = np.zeros((128, 16 * 128), np.float32)
    for j in range(16):
        for pp in range(128):
            selj[j * 8 + pp // 16, j * 128 + pp] = 1.0
    selbc = np.zeros((128, 256), np.float32)
    for pp in range(128):
        selbc[64 + pp % 16, pp] = 1.0
        selbc[96 + pp % 16, 128 + pp] = 1.0
    dmask = np.zeros((128, 16 * 128), np.float32)
    for j in range(16):
        for k in range(128):
            dmask[k, j * 128 + j * 8 + k // 16] = 1.0
    alog = inputs["L3_attn_A_log"]
    acolv = np.zeros((128, 256), np.float32)
    for dp in range(16):
        for j in range(16):
            for pp in range(128):
                acolv[pp, dp * 16 + j] = alog[dp * 128 + j * 8 + pp // 16, pp % 16]
    def cols(v, n=16):
        return np.ascontiguousarray(v.reshape(n, 128).T)
    cw = inputs["L3_attn_conv_w"]
    convw = np.zeros((128, DP * 4), np.float32)
    for dp in range(DP):
        convw[:, dp * 4:(dp + 1) * 4] = cw[dp * 128:(dp + 1) * 128, :]
    return _with_ones({
        "rin": rin_fm, "rin2": rin_fm,
        "WIN": np.ascontiguousarray(inputs["L3_attn_W_in"]),
        "WOUT": np.ascontiguousarray(inputs["L3_attn_W_out"]),
        "WXP": wxp, "WDT": wdt,
        "ones": np.ones((128, 128), np.float32),
        "SELJ": selj.astype(ml_dtypes.bfloat16),
        "SELBC": selbc,
        "DMASK": dmask.astype(ml_dtypes.bfloat16),
        "ACOL": acolv,
        "CONVW": convw,
        "CONVB": cols(inputs["L3_attn_conv_b"]),
        "DSKIP": cols(inputs["L3_attn_D_skip"]),
        "BDT": cols(inputs["L3_attn_b_dt"]),
        "LNG": cols(inputs["L3_attn_ln_g"], 8),
        "LNB": cols(inputs["L3_attn_ln_b"], 8),
    })



def build_l0_attn_wkv(phase=99, fz=None):
    p, _rin, _rout, _rin3, px, fused = _fz_head(fz, "l0_attn_wkv")
    ld, st, sv, ss, tt, sg = std_sems(p)
    res = ln_prologue(p, ld, _rin)
    w_k = p.dram_in(px + "WK", [D, D], F32)
    w_v = p.dram_in(px + "WV", [D, D], F32)
    w_r = p.dram_in(px + "WR", [D, D], F32)
    w_o = p.dram_in(px + "WO", [D, D], F32)
    ones = p.dram_in(px + "ones", [128, 128], F32)
    muk_in = p.dram_in(px + "MUK", [128, DT], F32)
    muv_in = p.dram_in(px + "MUV", [128, DT], F32)
    mur_in = p.dram_in(px + "MUR", [128, DT], F32)
    td_in = p.dram_in(px + "TDEC", [128, DT], F32)
    tf_in = p.dram_in(px + "TFIR", [128, DT], F32)
    g1_in = p.dram_in(px + "LN1G", [128, DT], F32)
    b1_in = p.dram_in(px + "LN1B", [128, DT], F32)
    rin2 = _rin if fused else p.dram_in("rin2", [D, T], F32)
    rout = _rout if fused else p.dram_out("rout", [D, T], F32)

    ones_f = ones_root(p, ld)
    cols = {}
    for nm, src2 in [("muk", muk_in), ("muv", muv_in), ("mur", mur_in),
                     ("tdec", td_in), ("tfir", tf_in), ("g1", g1_in), ("b1", b1_in)]:
        cb = p.sbuf("c_" + nm, [128, DT], F32)
        p.Y(lambda e, d=cb, s=src2: e.dma_start(out=d[:], in_=s), inc=(ld, 16))
        cols[nm] = cb

    rstd = p.sbuf("rstd", [128, T], F32)
    sq = p.sbuf("sq", [128, DT * 512], F32)
    ps0, ps1 = p.psum("ps0", [128, 512]), p.psum("ps1", [128, 512])
    banks = [p.psum(f"pm{i}", [128, 512]) for i in range(4)]
    p.V.wait(ld)
    p.S.wait(ld)
    p.T.wait(ld)
    mean = emit_ln(p, res, None, rstd, ps0, ps1, sq, ones_f, tt, sv, ss)
    p.S(lambda e, d=cols["tdec"]: e.activation(d[:], d[:], AF.Exp))
    p.S.fence()
    p.S(lambda e, d=cols["tdec"]: e.activation(d[:], d[:], AF.Identity, scale=-1.0))
    p.S.fence(inc=(ss, 1))

    p.S.wait(sv)
    p.S(lambda e, d=sq, s=rstd: e.activation(d[:, 0:T], s[:], AF.Square))
    p.S.fence(inc=(ss, 1))
    p.V.wait(ss)
    p.V(lambda e, d=sq: e.tensor_scalar(
        d[:, 0:T], d[:, 0:T], -EPS, 1.0 + EPS, ALU.mult, ALU.add))
    p.V.fence()
    p.V(lambda e, d=sq: e.reciprocal(d[:, 0:T], d[:, 0:T]))
    p.V.fence(inc=(sv, 1))
    p.S.wait(sv)
    p.S(lambda e, d=sq: e.activation(d[:, 0:T], d[:, 0:T], AF.Sqrt))
    p.S.fence(inc=(ss, 1))
    p.V.wait(ss)
    p.V(lambda e, d=sq, r=rstd: e.tensor_mul(d[:, 0:T], d[:, 0:T], r[:]))
    p.V.fence()
    xn_b = p.sbuf("xn_b", [128, DT * T], BF16)
    for di in range(DT):
        p.V(lambda e, d=res, m=mean, di=di: e.tensor_sub(
            d[:, di * T:(di + 1) * T], d[:, di * T:(di + 1) * T], m[:]))
        p.V.fence()
        p.V(lambda e, d=res, r=sq, di=di: e.tensor_mul(
            d[:, di * T:(di + 1) * T], d[:, di * T:(di + 1) * T], r[:, 0:T]))
        p.V.fence()
        p.V(lambda e, d=xn_b, s=res, g=cols["g1"], b=cols["b1"], di=di:
            e.tensor_scalar(
                d[:, di * T:(di + 1) * T], s[:, di * T:(di + 1) * T],
                g[:, di:di + 1], b[:, di:di + 1], ALU.mult, ALU.add))
        p.V.fence()
    p.V.fence(inc=(sv, 1))

    stage = p.sbuf("stagew", [128, 2048], F32)
    wb = Carve(sq, 0, DT * D, BF16)
    l2 = p.sem("l2")

    def load_w(whbm):
        for q4 in range(4):
            p.Y.wait(sg, sg.n)
            p.Y(lambda e, d=stage, s=whbm, q4=q4: e.dma_start(
                out=d[:].rearrange("p (k c) -> p k c", c=D),
                in_=s.rearrange("(k p) c -> p k c", p=128)[:, q4 * 2:(q4 + 1) * 2, :]),
                inc=(l2, 16))
            p.V.wait(l2)
            p.V(lambda e, d=wb, s=stage, q4=q4: e.tensor_copy(
                d[:, q4 * 2 * D:(q4 + 1) * 2 * D], s[:]))
            p.V.fence(inc=(sg, 1))
        p.T.wait(sg)

    xm_b = p.sbuf("xm_b", [128, DT * T], BF16)
    kf = Carve(res, 0, DT * T, BF16)
    vf = Carve(res, 8192, DT * T, BF16)
    rf = p.sbuf("rf", [128, DT * T], BF16)
    ddx = CarveB(rf, 0, T)

    def mix_into(mu_name):
        for di in range(DT):
            p.V(lambda e, d=ddx, s=xn_b, di=di: e.tensor_sub(
                d[:, 0:T - 1], s[:, di * T + 1: di * T + T],
                s[:, di * T: di * T + T - 1]))
            p.V.fence()
            p.V(lambda e, d=xm_b, s=xn_b, dd=ddx, m=cols[mu_name], di=di:
                e.scalar_tensor_tensor(
                    d[:, di * T + 1: di * T + T], dd[:, 0:T - 1], m[:, di:di + 1],
                    s[:, di * T: di * T + T - 1], ALU.mult, ALU.add))
            p.V(lambda e, d=xm_b, s=xn_b, m=cols[mu_name], di=di: e.tensor_scalar(
                d[:, di * T: di * T + 1], s[:, di * T: di * T + 1],
                m[:, di:di + 1], None, ALU.mult))
            p.V.fence()
        p.V.fence(inc=(sv, 1))
        p.T.wait(sv)

    def proj_into(outbuf, sigmoid=False):
        for mi in range(DT):
            for c in range(NCH):
                bank = banks[c]
                for ki in range(DT):
                    p.T(lambda e, o=bank, w=wb, x=xm_b, ki=ki, mi=mi, c=c:
                        e.matmul(o[:], w[:, ki * D + mi * 128: ki * D + mi * 128 + 128],
                                 x[:, ki * T + c * 512: ki * T + c * 512 + 512],
                                 start=(ki == 0), stop=(ki == DT - 1)),
                        inc=(tt, 1) if ki == DT - 1 else None)
            if sigmoid:
                p.S.wait(tt)
                for c in range(NCH):
                    p.S(lambda e, d=outbuf, s=banks[c], mi=mi, c=c: e.activation(
                        d[:, mi * T + c * 512: mi * T + c * 512 + 512], s[:],
                        AF.Sigmoid))
                p.S.fence(inc=(ss, 1))
                p.T.wait(ss)
            else:
                p.V.wait(tt)
                for c in range(NCH):
                    p.V(lambda e, d=outbuf, s=banks[c], mi=mi, c=c: e.tensor_copy(
                        d[:, mi * T + c * 512: mi * T + c * 512 + 512], s[:]))
                p.V.fence(inc=(sv, 1))
                p.T.wait(sv)

    p.Y.wait(sv, sv.n)
    mix_into("muk"); load_w(w_k); proj_into(kf)
    mix_into("muv"); load_w(w_v); proj_into(vf)
    mix_into("mur"); load_w(w_r); proj_into(rf, sigmoid=True)

    tmp = ([CarveB(xn_b, i * T, T) for i in range(4)] +
           [CarveB(xm_b, i * T, T) for i in range(4)])
    PP, XB, YB, EV, AA, BB, UK, WB2 = tmp
    p.V.wait(tt)
    p.V.wait(ss)
    p.S.wait(ss)
    for mi in range(DT):
        wcol = lambda: cols["tdec"][:, mi:mi + 1]
        ucol = lambda: cols["tfir"][:, mi:mi + 1]
        kap = kf[:, mi * T:(mi + 1) * T]
        vap = vf[:, mi * T:(mi + 1) * T]
        p.V(lambda e, d=WB2, w=wcol(): e.tensor_copy(
            d[:], w.to_broadcast([128, T])))
        p.V.fence()
        p.V(lambda e, d=PP: e.memset(d[:][:, 0:1], -1e30))
        p.V.fence()
        p.V(lambda e, d=PP, wbb=WB2, k=kap: e.tensor_tensor_scan(
            d[:][:, 1:T], wbb[:][:, 0:T - 1], k[:, 0:T - 1], -1e30,
            ALU.add, ALU.max))
        p.V.fence()
        p.V(lambda e, d=XB, s=PP, w=wcol(): e.tensor_scalar(
            d[:][:, 0:T - 1], s[:][:, 0:T - 1], w, None, ALU.add))
        p.V.fence()
        p.V(lambda e, d=XB, s=PP: e.tensor_sub(
            d[:][:, 0:T - 1], d[:][:, 0:T - 1], s[:][:, 1:T]))
        p.V(lambda e, d=YB, s=PP, k=kap: e.tensor_sub(
            d[:][:, 0:T - 1], k[:, 0:T - 1], s[:][:, 1:T]))
        p.V.fence(inc=(sv, 1))
        p.S.wait(sv)
        p.S(lambda e, d=XB: e.activation(d[:][:, 0:T - 1], d[:][:, 0:T - 1], AF.Exp))
        p.S(lambda e, d=YB: e.activation(d[:][:, 0:T - 1], d[:][:, 0:T - 1], AF.Exp))
        p.S.fence(inc=(ss, 1))
        p.V.wait(ss)
        p.V(lambda e, d=EV, y=YB, v=vap: e.tensor_mul(
            d[:][:, 0:T - 1], y[:][:, 0:T - 1], v[:, 0:T - 1]))
        p.V.fence()
        p.V(lambda e, d=AA: e.memset(d[:][:, 0:1], 0.0))
        p.V(lambda e, d=BB: e.memset(d[:][:, 0:1], 0.0))
        p.V.fence()
        p.V(lambda e, d=AA, x=XB, ev=EV: e.tensor_tensor_scan(
            d[:][:, 1:T], x[:][:, 0:T - 1], ev[:][:, 0:T - 1], 0.0,
            ALU.mult, ALU.add))
        p.V.fence()
        p.V(lambda e, d=BB, x=XB, y=YB: e.tensor_tensor_scan(
            d[:][:, 1:T], x[:][:, 0:T - 1], y[:][:, 0:T - 1], 0.0,
            ALU.mult, ALU.add))
        p.V.fence()
        p.V(lambda e, d=UK, k=kap, u=ucol(): e.tensor_scalar(
            d[:], k, u, None, ALU.add))
        p.V.fence()
        p.V(lambda e, d=XB, s=PP, uk=UK: e.tensor_tensor(
            d[:], s[:], uk[:], ALU.max))
        p.V.fence()
        p.V(lambda e, d=PP, q=XB: e.tensor_sub(d[:], d[:], q[:]))
        p.V(lambda e, d=UK, q=XB: e.tensor_sub(d[:], d[:], q[:]))
        p.V.fence(inc=(sv, 1))
        p.S.wait(sv)
        p.S(lambda e, d=PP: e.activation(d[:], d[:], AF.Exp))
        p.S(lambda e, d=UK: e.activation(d[:], d[:], AF.Exp))
        p.S.fence(inc=(ss, 1))
        p.V.wait(ss)
        p.V(lambda e, d=AA, e1=PP: e.tensor_mul(d[:], d[:], e1[:]))
        p.V(lambda e, d=BB, e1=PP: e.tensor_mul(d[:], d[:], e1[:]))
        p.V.fence()
        p.V(lambda e, d=EV, e2=UK, v=vap: e.tensor_mul(d[:], e2[:], v))
        p.V.fence()
        p.V(lambda e, d=AA, ev=EV: e.tensor_add(d[:], d[:], ev[:]))
        p.V(lambda e, d=BB, e2=UK: e.tensor_add(d[:], d[:], e2[:]))
        p.V.fence()
        p.V(lambda e, d=BB: e.reciprocal(d[:], d[:]))
        p.V.fence()
        p.V(lambda e, d=AA, b=BB: e.tensor_mul(d[:], d[:], b[:]))
        p.V.fence()
        p.V(lambda e, d=rf, a=AA, mi=mi: e.tensor_mul(
            d[:, mi * T:(mi + 1) * T], d[:, mi * T:(mi + 1) * T], a[:]))
        p.V.fence(inc=(sv, 1))

    load_w(w_o)
    l3 = p.sem("l3")
    p.G.wait(sv)
    for mi in range(DT):
        p.Y.wait(st, st.n)
        p.Y.wait(sg, sg.n)
        p.V.wait(st, st.n)
        p.Y(lambda e, d=stage, s=rin2, mi=mi: e.dma_start(
            out=d[:, 0:T], in_=s.rearrange("(k p) t -> p k t", p=128)[:, mi, :]),
            inc=(l3, 16))
        for c in range(NCH):
            bank = banks[c]
            for ki in range(DT):
                p.T(lambda e, o=bank, w=wb, x=rf, ki=ki, mi=mi, c=c:
                    e.matmul(o[:], w[:, ki * D + mi * 128: ki * D + mi * 128 + 128],
                             x[:, ki * T + c * 512: ki * T + c * 512 + 512],
                             start=(ki == 0), stop=(ki == DT - 1)),
                    inc=(tt, 1) if ki == DT - 1 else None)
        p.V.wait(tt)
        p.V.wait(l3)
        p.V(lambda e, d=stage, m=mean: e.tensor_sub(d[:, 0:T], d[:, 0:T], m[:]))
        p.V.fence()
        p.V(lambda e, d=stage, r=rstd: e.tensor_mul(d[:, 0:T], d[:, 0:T], r[:]))
        p.V.fence()
        for c in range(NCH):
            p.V(lambda e, d=stage, s=banks[c], c=c: e.tensor_add(
                d[:, c * 512:c * 512 + 512], d[:, c * 512:c * 512 + 512], s[:]))
        p.V.fence(inc=(sv, 1))
        p.G.wait(sv)
        p.G(lambda e, d=rout, s=stage, mi=mi: e.dma_start(
            out=d.rearrange("(k p) t -> p k t", p=128)[:, mi, :], in_=s[:, 0:T]),
            inc=(st, 16))
        p.T.wait(sv)
        p.Y.wait(sv, sv.n)
    p.G.wait(st)
    return _fin(p, fused)


def l0_attn_wkv_inputs(inputs, core, rin_fm):
    def cols(v, n=8):
        return np.ascontiguousarray(np.asarray(v).reshape(n, 128).T)
    return _with_ones({
        "rin": rin_fm, "rin2": rin_fm,
        "WK": np.ascontiguousarray(inputs["L0_attn_Wk"]),
        "WV": np.ascontiguousarray(inputs["L0_attn_Wv"]),
        "WR": np.ascontiguousarray(inputs["L0_attn_Wr"]),
        "WO": np.ascontiguousarray(inputs["L0_attn_Wo"]),
        "ones": np.ones((128, 128), np.float32),
        "MUK": cols(inputs["L0_attn_tm_mu_k"]),
        "MUV": cols(inputs["L0_attn_tm_mu_v"]),
        "MUR": cols(inputs["L0_attn_tm_mu_r"]),
        "TDEC": cols(inputs["L0_attn_time_decay"]),
        "TFIR": cols(inputs["L0_attn_time_first"]),
        "LN1G": cols(inputs["L0_attn_ln1_g"]),
        "LN1B": cols(inputs["L0_attn_ln1_b"]),
    })



def build_l0_attn_moe(phase=99, fz=None):
    p, _rin, _rout, _rin3, px, fused = _fz_head(fz, "l0_attn_moe")
    E_ = 8
    TC2 = 512
    NC2 = T // TC2
    ld, st, sv, ss, tt, sg = std_sems(p)
    res = ln_prologue(p, ld, _rin)
    w_rt = p.dram_in(px + "RWT", [D, E_], F32)
    w_ek = p.dram_in(px + "EWK", [E_ * D, 2048], F32)
    w_ev = p.dram_in(px + "EWV", [E_ * 2048, D], F32)
    w_er = p.dram_in(px + "EWR", [E_ * D, D], F32)
    ones = p.dram_in(px + "ones", [128, 128], F32)
    self1 = p.dram_in(px + "SELF1", [E_, E_ * 128], F32)
    cmk_in = p.dram_in(px + "CMK", [128, E_ * DT], F32)
    cmr_in = p.dram_in(px + "CMR", [128, E_ * DT], F32)
    g2_in = p.dram_in(px + "LN2G", [128, DT], F32)
    b2_in = p.dram_in(px + "LN2B", [128, DT], F32)
    rin3 = _rin3 if fused else p.dram_in("rin3", [D, T], F32)
    rout = _rout if fused else p.dram_out("rout", [D, T], F32)

    ones_f = ones_root(p, ld)
    idf = ones_f_id(p, ld)
    rtf, _ = load_fm(p, "rt", w_rt, D, E_, ld, bf16=False)
    self_f = p.sbuf("self_f", [128, E_ * 128], F32)
    p.Y(lambda e, d=self_f, s=self1: e.dma_start(out=d[0:E_, :], in_=s), inc=(ld, 16))
    cmk = p.sbuf("cmk", [128, E_ * DT], F32)
    p.Y(lambda e, d=cmk, s=cmk_in: e.dma_start(out=d[:], in_=s), inc=(ld, 16))
    cmr = p.sbuf("cmr", [128, E_ * DT], F32)
    p.Y(lambda e, d=cmr, s=cmr_in: e.dma_start(out=d[:], in_=s), inc=(ld, 16))
    g2c = p.sbuf("g2c", [128, DT], F32)
    p.Y(lambda e, d=g2c, s=g2_in: e.dma_start(out=d[:], in_=s), inc=(ld, 16))
    b2c = p.sbuf("b2c", [128, DT], F32)
    p.Y(lambda e, d=b2c, s=b2_in: e.dma_start(out=d[:], in_=s), inc=(ld, 16))

    xc_b = p.sbuf("xc_b", [128, DT * T], BF16)
    rstd = p.sbuf("rstd", [128, T], F32)
    sq = p.sbuf("sq", [128, DT * 512], F32)
    ps0, ps1 = p.psum("ps0", [128, 512]), p.psum("ps1", [128, 512])
    banks = [p.psum(f"pm{i}", [128, 512]) for i in range(4)]
    p.V.wait(ld)
    p.S.wait(ld)
    p.T.wait(ld)
    omk = p.sbuf("omk", [128, E_ * DT], F32)
    p.V(lambda e, d=omk, s=cmk: e.tensor_scalar(
        d[:], s[:], -1.0, 1.0, ALU.mult, ALU.add))
    omr = p.sbuf("omr", [128, E_ * DT], F32)
    p.V(lambda e, d=omr, s=cmr: e.tensor_scalar(
        d[:], s[:], -1.0, 1.0, ALU.mult, ALU.add))
    p.V.fence()
    mean = emit_ln(p, res, xc_b, rstd, ps0, ps1, sq, ones_f, tt, sv, ss)

    bias_c = p.sbuf("bias_c", [128, 1], F32)
    p.T.wait(sv)
    for ki in range(DT):
        p.T(lambda e, o=ps0, w=rtf, b=b2c, ki=ki, E_=E_: e.matmul(
            o[0:E_, 0:1], w[:, ki * E_:(ki + 1) * E_], b[:, ki:ki + 1],
            start=(ki == 0), stop=(ki == DT - 1)),
            inc=(tt, 1) if ki == DT - 1 else None)
    p.V.wait(tt)
    p.V(lambda e, d=bias_c, s=ps0, E_=E_: e.tensor_copy(d[0:E_, :], s[0:E_, 0:1]))
    for ki in range(DT):
        p.V(lambda e, d=rtf, g=g2c, ki=ki, E_=E_: e.tensor_scalar(
            d[:, ki * E_:(ki + 1) * E_], d[:, ki * E_:(ki + 1) * E_],
            g[:, ki:ki + 1], None, ALU.mult))
    p.V.fence(inc=(sv, 1))
    p.T.wait(sv)

    eip_b = p.sbuf("eip_b", [128, DT * (T + 1)], BF16)
    p.V.wait(ss)
    for di in range(DT):
        p.V(lambda e, d=sq, s=xc_b, r=rstd, di=di: e.tensor_mul(
            d[:, 0:T], s[:, di * T:(di + 1) * T], r[:]))
        p.V.fence()
        p.V(lambda e, d=eip_b, s=sq, g=g2c, b=b2c, di=di: e.tensor_scalar(
            d[:, di * (T + 1) + 1: di * (T + 1) + 1 + T], s[:, 0:T],
            g[:, di:di + 1], b[:, di:di + 1], ALU.mult, ALU.add))
        p.V(lambda e, d=eip_b, di=di: e.memset(
            d[:, di * (T + 1): di * (T + 1) + 1], 0.0))
        p.V.fence()
    p.V.fence(inc=(sv, 1))

    lg_fm = p.sbuf("lg_fm", [128, T], F32)
    prob_tm = p.sbuf("prob_tm", [128, 16 * E_], F32)
    p.T.wait(sv)
    emit_router_tm(p, res, mean, rstd, rtf, E_, ps0, ps1, lg_fm, prob_tm,
                   idf, ones_f, tt, sv, ss, softmax=True, bias_col=bias_c)
    comb_tm = p.sbuf("comb_tm", [128, 16 * E_], F32)
    emit_top2_combine(p, prob_tm, comb_tm, E_, sv)
    comb_fm = lg_fm
    emit_comb_fm(p, comb_tm, comb_fm, E_, idf, ps1, tt, sv)
    if phase == 1:
        p.G.wait(sv)
        p.G(lambda e, d=rout, s=comb_fm: e.dma_start(out=d[0:128, :], in_=s[:]),
            inc=(st, 16))
        p.G.wait(st)
        return _fin(p, fused)

    kbuf = Carve(xc_b, 0, 16 * T, BF16)
    rbuf = Carve(rstd, 0, 2 * T, BF16)
    twb = Carve(mean, 0, TC2, BF16)
    tbuf = SqView(mean, 1)
    tbuf = None
    stage = sq
    ewk2 = p.sbuf("ewk2", [128, DT * 1024], BF16)
    kvh = p.sbuf("kvh", [128, DT * 512], BF16)
    relu_s = Carve(mean, 512, 512, F32)
    tb2 = Carve(mean, 1024, 512, F32)
    l2 = p.sem("l2")
    ewkA = Carve(ewk2, 0, DT * 512, BF16)
    ewkB = Carve(ewk2, 4096 if False else 2048, DT * 512, BF16)

    def scaled_cast(mu, om, e_i):
        p.V.wait(tt, tt.n)
        for ki in range(DT):
            p.V(lambda e, d=ewkA, s=stage, m=mu, ki=ki, e_i=e_i:
                e.tensor_scalar(d[:, ki * 512:(ki + 1) * 512],
                                s[:, ki * 512:(ki + 1) * 512],
                                m[:, e_i * DT + ki: e_i * DT + ki + 1],
                                None, ALU.mult))
            p.V(lambda e, d=ewkB, s=stage, m=om, ki=ki, e_i=e_i:
                e.tensor_scalar(d[:, ki * 512:(ki + 1) * 512],
                                s[:, ki * 512:(ki + 1) * 512],
                                m[:, e_i * DT + ki: e_i * DT + ki + 1],
                                None, ALU.mult))
        p.V.fence(inc=(sg, 1))
        p.T.wait(sg)

    def mix_mm(bank, fm, t0, cc):
        for ki in range(DT):
            p.T(lambda e, o=bank, w=ewkA, x=eip_b, ki=ki, fm=fm, t0=t0, cc=cc:
                e.matmul(o[:], w[:, ki * 512 + fm * 128: ki * 512 + fm * 128 + 128],
                         x[:, ki * (T + 1) + 1 + t0 + cc * 512:
                            ki * (T + 1) + 1 + t0 + cc * 512 + 512],
                         start=(ki == 0), stop=False))
        for ki in range(DT):
            p.T(lambda e, o=bank, w=ewkB, x=eip_b, ki=ki, fm=fm, t0=t0, cc=cc:
                e.matmul(o[:], w[:, ki * 512 + fm * 128: ki * 512 + fm * 128 + 128],
                         x[:, ki * (T + 1) + t0 + cc * 512:
                            ki * (T + 1) + t0 + cc * 512 + 512],
                         start=False, stop=(ki == DT - 1)),
                inc=(tt, 1) if ki == DT - 1 else None)

    p.Y.wait(sv, sv.n)
    for c2 in range(NC2):
        t0 = c2 * TC2
        for e_i in range(E_):
            p.T.wait(sv)
            for cc in range(TC2 // 512):
                p.T(lambda e, o=ps0, s=self_f, cf=comb_fm, e_i=e_i, t0=t0, cc=cc:
                    e.matmul(o[:], s[0:E_, e_i * 128:(e_i + 1) * 128],
                             cf[0:E_, t0 + cc * 512: t0 + cc * 512 + 512],
                             start=True, stop=True), inc=(tt, 1))
                p.V.wait(tt)
                p.V(lambda e, d=twb, s=ps0, cc=cc: e.tensor_copy(
                    d[:, cc * 512:cc * 512 + 512], s[:]))
                p.V.fence(inc=(sv, 1))
                p.T.wait(sv)
            for fc in range(4):
                p.Y.wait(sg, sg.n)
                p.Y(lambda e, d=stage, s=w_ek, e_i=e_i, fc=fc: e.dma_start(
                    out=d[:].rearrange("p (k c) -> p k c", c=512),
                    in_=s.rearrange("(E k p) f -> p E k f", p=128, E=E_)[
                        :, e_i, :, fc * 512:(fc + 1) * 512]), inc=(l2, 16))
                p.V.wait(l2)
                scaled_cast(cmk, omk, e_i)
                for fm in range(4):
                    for cc in range(TC2 // 512):
                        bank = banks[cc]
                        mix_mm(bank, fm, t0, cc)
                        p.S.wait(tt)
                        p.S(lambda e, d=relu_s, s=bank: e.activation(
                            d[:], s[:], AF.Relu))
                        p.S.fence()
                        p.S(lambda e, d=kbuf, s=relu_s, fc=fc, fm=fm, cc=cc:
                            e.activation(d[:, (fc * 4 + fm) * TC2 + cc * 512:
                                           (fc * 4 + fm) * TC2 + cc * 512 + 512],
                                         s[:], AF.Square))
                        p.S.fence(inc=(ss, 1))
                        p.T.wait(ss)
            for rh in range(2):
                p.Y.wait(sg, sg.n)
                p.Y(lambda e, d=stage, s=w_er, e_i=e_i, rh=rh: e.dma_start(
                    out=d[:].rearrange("p (k c) -> p k c", c=512),
                    in_=s.rearrange("(E k p) f -> p E k f", p=128, E=E_)[
                        :, e_i, :, rh * 512:(rh + 1) * 512]), inc=(l2, 16))
                p.V.wait(l2)
                scaled_cast(cmr, omr, e_i)
                for fm in range(4):
                    mi = rh * 4 + fm
                    for cc in range(TC2 // 512):
                        bank = banks[cc]
                        mix_mm(bank, fm, t0, cc)
                        p.S.wait(tt)
                        p.S(lambda e, d=rbuf, s=bank, mi=mi, cc=cc: e.activation(
                            d[:, mi * TC2 + cc * 512: mi * TC2 + cc * 512 + 512],
                            s[:], AF.Sigmoid))
                        p.S.fence(inc=(ss, 1))
                        p.T.wait(ss)
            for h2 in range(2):
                for q in range(2):
                    p.Y.wait(sg, sg.n)
                    p.Y(lambda e, d=stage, s=w_ev, e_i=e_i, q=2 * h2 + q:
                        e.dma_start(
                            out=d[:].rearrange("p (k c) -> p k c", c=D),
                            in_=s.rearrange("(E k p) f -> p E k f", p=128,
                                            E=E_, k=16)[
                                :, e_i, q * 4:(q + 1) * 4, :]), inc=(l2, 16))
                    p.V.wait(l2)
                    p.V.wait(tt, tt.n)
                    p.V(lambda e, d=ewk2, s=stage, qo=(q % 2) * 4 * D:
                        e.tensor_copy(d[:, qo:qo + 4 * D], s[:, 0:4 * D]))
                    p.V.fence(inc=(sg, 1))
                p.T.wait(sg)
                for mi in range(DT):
                    for cc in range(TC2 // 512):
                        bank = banks[2 + (cc % 2)]
                        for ki in range(8):
                            kt = h2 * 8 + ki
                            p.T(lambda e, o=bank, w=ewk2, x=kbuf, ki=ki, kt=kt,
                                mi=mi, t0=t0, cc=cc: e.matmul(
                                    o[:], w[:, ki * D + mi * 128: ki * D + mi * 128 + 128],
                                    x[:, kt * TC2 + cc * 512: kt * TC2 + cc * 512 + 512],
                                    start=(ki == 0), stop=(ki == 7)),
                                inc=(tt, 1) if ki == 7 else None)
                        p.V.wait(tt)
                        if h2 == 0:
                            p.V(lambda e, d=kvh, s=bank, mi=mi, cc=cc:
                                e.tensor_copy(
                                    d[:, mi * TC2 + cc * 512: mi * TC2 + cc * 512 + 512],
                                    s[:]))
                            p.V.fence(inc=(sv, 1))
                            p.T.wait(sv)
                            continue
                        p.V(lambda e, d=tb2, s=bank, k=kvh, mi=mi, cc=cc:
                            e.tensor_add(d[:], s[:],
                                         k[:, mi * TC2 + cc * 512: mi * TC2 + cc * 512 + 512]))
                        p.V.fence()
                        p.V(lambda e, d=tb2, r=rbuf, mi=mi, cc=cc: e.tensor_mul(
                            d[:], d[:],
                            r[:, mi * TC2 + cc * 512: mi * TC2 + cc * 512 + 512]))
                        p.V.fence()
                        p.V(lambda e, d=tb2, w=twb, cc=cc: e.tensor_mul(
                            d[:], d[:], w[:, cc * 512:cc * 512 + 512]))
                        p.V.fence()
                        p.V(lambda e, d=res, s=tb2, mi=mi, t0=t0, cc=cc:
                            e.tensor_add(
                                d[:, mi * T + t0 + cc * 512: mi * T + t0 + cc * 512 + 512],
                                d[:, mi * T + t0 + cc * 512: mi * T + t0 + cc * 512 + 512],
                                s[:]))
                        p.V.fence(inc=(sv, 1))
                        p.T.wait(sv)
    l3 = p.sem("l3")
    rl3 = sq
    p.Y.wait(sv, sv.n)
    p.Y.wait(sg, sg.n)
    for mi in range(DT):
        p.Y(lambda e, d=rl3, s=rin3, mi=mi: e.dma_start(
            out=d[:, 0:T], in_=s.rearrange("(k p) t -> p k t", p=128)[:, mi, :]),
            inc=(l3, 16))
        p.V.wait(l3)
        p.V(lambda e, d=res, s=rl3, mi=mi: e.tensor_add(
            d[:, mi * T:(mi + 1) * T], d[:, mi * T:(mi + 1) * T], s[:, 0:T]))
        p.V.fence(inc=(sv, 1))
        p.G.wait(sv)
        p.G(lambda e, d=rout, s=res, mi=mi: e.dma_start(
            out=d.rearrange("(k p) t -> p k t", p=128)[:, mi, :],
            in_=s[:, mi * T:(mi + 1) * T]), inc=(st, 16))
        p.Y.wait(sv, sv.n)
        p.Y.wait(st, st.n)
    p.G.wait(st)
    return _fin(p, fused)


def l0_attn_moe_inputs(inputs, core, rin_fm, rin3_fm=None):
    E_ = 8
    def cols(v, n=8):
        return np.ascontiguousarray(np.asarray(v).reshape(n, 128).T)
    cmk = np.zeros((128, E_ * DT), np.float32)
    cmr = np.zeros((128, E_ * DT), np.float32)
    for e in range(E_):
        cmk[:, e * DT:(e + 1) * DT] = cols(inputs["L0_attn_cm_mu_k"][e])
        cmr[:, e * DT:(e + 1) * DT] = cols(inputs["L0_attn_cm_mu_r"][e])
    self1 = np.zeros((E_, E_ * 128), np.float32)
    for e in range(E_):
        self1[e, e * 128:(e + 1) * 128] = 1.0
    return _with_ones({
        "rin": rin_fm,
        "RWT": np.ascontiguousarray(inputs["L0_attn_router_w"]),
        "EWK": np.ascontiguousarray(inputs["L0_attn_eWk"].reshape(E_ * D, 2048)),
        "EWV": np.ascontiguousarray(inputs["L0_attn_eWv"].reshape(E_ * 2048, D)),
        "EWR": np.ascontiguousarray(inputs["L0_attn_eWr"].reshape(E_ * D, D)),
        "ones": np.ones((128, 128), np.float32),
        "ident": np.eye(128, dtype=np.float32),
        "SELF1": self1,
        "CMK": cmk, "CMR": cmr,
        "LN2G": cols(inputs["L0_attn_ln2_g"]),
        "LN2B": cols(inputs["L0_attn_ln2_b"]),
        "rin3": rin3_fm,
    })



def build_l3_ffn(phase=99, fz=None):
    p, _rin, _rout, _rin3, px, fused = _fz_head(fz, "l3_ffn")
    E_ = 8
    CAP = 1024
    CH = 128
    TC3 = 256
    ld, st, sv, ss, tt, sg = std_sems(p)
    res = ln_prologue(p, ld, _rin)
    w_g = p.dram_in(px + "WG", [D, E_], F32)
    w_q = p.dram_in(px + "WQ", [D, D], F32)
    w_k = p.dram_in(px + "WKE", [E_ * D, D], F32)
    w_v = p.dram_in(px + "WVE", [E_ * D, D], F32)
    w_f = p.dram_in(px + "WFE", [E_ * D, 256], F32)
    w_o = p.dram_in(px + "WOT", [D, D], F32)
    ones = p.dram_in(px + "ones", [128, 128], F32)
    self1 = p.dram_in(px + "SELF1", [E_, E_ * 128], F32)
    nw_in = p.dram_in(px + "NRMW", [128, DT], F32)
    gw_in = p.dram_in(px + "GNRMW", [128, DT], F32)
    triu_in = p.dram_in(px + "TRIU01", [128, 128], BF16)
    rin2 = _rin if fused else p.dram_in("rin2", [D, T], F32)
    rout = _rout if fused else p.dram_out("rout", [D, T], F32)

    ones_f = ones_root(p, ld)
    idf = ones_f_id(p, ld)
    idb = idf_b(p, ld)
    rtf, _ = load_fm(p, "rt", w_g, D, E_, ld, bf16=False)
    self_f = p.sbuf("self_f", [128, E_ * 128], F32)
    p.Y(lambda e, d=self_f, s=self1: e.dma_start(out=d[0:E_, :], in_=s), inc=(ld, 16))
    nwc = p.sbuf("nwc", [128, DT], F32)
    p.Y(lambda e, d=nwc, s=nw_in: e.dma_start(out=d[:], in_=s), inc=(ld, 16))
    gwc = p.sbuf("gwc", [128, DT], F32)
    p.Y(lambda e, d=gwc, s=gw_in: e.dma_start(out=d[:], in_=s), inc=(ld, 16))
    triub = p.sbuf("triub", [128, 128], BF16)
    p.Y(lambda e, d=triub, s=triu_in: e.dma_start(out=d[:], in_=s), inc=(ld, 16))

    xc_b = p.sbuf("xc_b", [128, DT * T], BF16)
    rstd = p.sbuf("rstd", [128, T], F32)
    sq = p.sbuf("sq", [128, DT * 512], F32)
    ps0, ps1 = p.psum("ps0", [128, 512]), p.psum("ps1", [128, 512])
    pm0, pm1 = p.psum("pm0", [128, 512]), p.psum("pm1", [128, 512])
    pm2, pm3 = p.psum("pm2", [128, 512]), p.psum("pm3", [128, 512])
    ptb3 = p.psum("ptb3", [128, 1024], BF16)
    banks = [pm0, pm1, pm2, pm3]
    p.V.wait(ld)
    p.S.wait(ld)
    p.T.wait(ld)
    mean = emit_ln(p, res, xc_b, rstd, ps0, ps1, sq, ones_f, tt, sv, ss)
    p.S.wait(sv)
    p.S(lambda e, d=sq, s=rstd: e.activation(d[:, 0:T], s[:], AF.Square))
    p.S.fence(inc=(ss, 1))
    p.V.wait(ss)
    p.V(lambda e, d=sq: e.tensor_scalar(
        d[:, 0:T], d[:, 0:T], -EPS, 1.0 + EPS, ALU.mult, ALU.add))
    p.V.fence()
    p.V(lambda e, d=sq: e.reciprocal(d[:, 0:T], d[:, 0:T]))
    p.V.fence(inc=(sv, 1))
    p.S.wait(sv)
    p.S(lambda e, d=sq: e.activation(d[:, 0:T], d[:, 0:T], AF.Sqrt))
    p.S.fence(inc=(ss, 1))
    p.V.wait(ss)
    p.V(lambda e, d=rstd, s=sq: e.tensor_mul(d[:], d[:], s[:, 0:T]))
    p.V.fence(inc=(sv, 1))
    xg_b = xc_b
    for di in range(DT):
        p.V(lambda e, d=sq, s=xc_b, r=rstd, di=di: e.tensor_mul(
            d[:, 0:T], s[:, di * T:(di + 1) * T], r[:]))
        p.V.fence()
        p.V(lambda e, d=xg_b, s=sq, g=nwc, di=di: e.tensor_scalar(
            d[:, di * T:(di + 1) * T], s[:, 0:T], g[:, di:di + 1], None, ALU.mult))
        p.V.fence()
    p.V.fence(inc=(sv, 1))

    p.V.wait(ss)
    for ki in range(DT):
        p.V(lambda e, d=rtf, g=nwc, ki=ki, E_=E_: e.tensor_scalar(
            d[:, ki * E_:(ki + 1) * E_], d[:, ki * E_:(ki + 1) * E_],
            g[:, ki:ki + 1], None, ALU.mult))
    p.V.fence(inc=(sv, 1))
    p.T.wait(sv)
    lg_fm = p.sbuf("lg_fm", [128, T], F32)
    prob_tm = p.sbuf("prob_tm", [128, 16 * E_], F32)
    emit_router_tm(p, res, mean, rstd, rtf, E_, ps0, ps1, lg_fm, prob_tm,
                   idf, ones_f, tt, sv, ss, softmax=True)
    comb_tm = p.sbuf("comb_tm", [128, 16 * E_], F32)
    emit_top2_combine(p, prob_tm, comb_tm, E_, sv)
    p.V(lambda e, d=prob_tm, s=comb_tm: e.tensor_scalar(
        d[:], s[:], 0.0, None, ALU.is_gt))
    p.V.fence(inc=(sv, 1))
    wv_fm = lg_fm
    emit_comb_fm(p, comb_tm, wv_fm, E_, idf, ps1, tt, sv)
    km_fm = p.sbuf("km_fm", [128, T], F32)
    emit_comb_fm(p, prob_tm, km_fm, E_, idf, ps1, tt, sv)
    cum = p.sbuf("cum", [128, T], F32)
    thr = p.sbuf("thr", [128, 1], F32)
    p.V(lambda e, d=cum, s=km_fm, E_=E_: e.tensor_tensor_scan(
        d[0:E_, :], s[0:E_, :], s[0:E_, :], 0.0, ALU.add, ALU.bypass))
    p.V.fence()
    p.V(lambda e, d=thr, s=cum, E_=E_: e.tensor_scalar(
        d[0:E_, :], s[0:E_, T - 1:T], float(-CAP), None, ALU.add))
    p.V.fence()
    p.V(lambda e, d=cum, m=thr, E_=E_: e.tensor_scalar(
        d[0:E_, :], d[0:E_, :], m[0:E_, 0:1], None, ALU.is_gt))
    p.V.fence()
    p.V(lambda e, d=km_fm, s=cum, E_=E_: e.tensor_mul(
        d[0:E_, :], d[0:E_, :], s[0:E_, :]))
    p.V.fence()
    p.V(lambda e, d=wv_fm, s=km_fm, E_=E_: e.tensor_mul(
        d[0:E_, :], d[0:E_, :], s[0:E_, :]))
    p.V.fence(inc=(sv, 1))
    if phase == 1:
        p.G.wait(sv)
        p.G(lambda e, d=rout, s=km_fm: e.dma_start(out=d[0:128, :], in_=s[:]),
            inc=(st, 16))
        p.G(lambda e, d=rout, s=wv_fm: e.dma_start(out=d[128:256, :], in_=s[:]),
            inc=(st, 16))
        p.G.wait(st)
        return _fin(p, fused)

    y_acc = Carve(res, 0, DT * T, BF16)
    xe_b = Carve(res, 8192, DT * TC3, BF16)
    q_fm = Carve(res, 9216, DT * TC3, BF16)
    k_fm = Carve(res, 10240, DT * TC3, BF16)
    mb_b = Carve(res, 11264, TC3, BF16)
    wb_b = Carve(res, 11392, TC3, BF16)
    hk_sb = Carve(res, 11520, 512, F32)
    hv_sb = Carve(res, 12032, 4 * 256, F32)
    sinT_b = Carve(res, 13056, 4 * CH, BF16)
    srem_tm = Carve(res, 13312, 256, BF16)
    hk_b = Carve(res, 13760, 512, BF16)
    hv_b = Carve(res, 14016, 1024, BF16)
    qvg_b = Carve(res, 13440, 256, BF16)
    qvgT = Carve(res, 13568, CH, BF16)
    AT_b = Carve(res, 13632, CH, BF16)
    TmT_b = Carve(res, 13696, CH, BF16)
    k_tm = Carve(rstd, 0, (TC3 // 128) * D, BF16)
    v_tm = Carve(sq, 2048, (TC3 // 128) * D, BF16)
    g_tm = Carve(cum, 0, (TC3 // 128) * 256, F32)
    st_tm = Carve(cum, 512, (TC3 // 128) * 256, F32)
    sin_tm = Carve(cum, 1024, 256, BF16)
    dec_tm = Carve(cum, 1280, 256, F32)
    okb = Carve(cum, 1536, 256, F32)
    glb = Carve(mean, 1552, 64 + 256, F32)
    gsm = Carve(mean, 0, 4 * CH, F32)
    Gsm = Carve(mean, 512, 4 * CH, F32)
    srem_sm = Carve(mean, 1024, 4 * CH, F32)
    glsm = Carve(mean, 1536, 8, F32)
    nmv = Carve(mean, 1544, 1, F32)
    denv = Carve(mean, 1545, 1, F32)

    wqb = p.sbuf("wqb", [128, DT * D], BF16)
    wkb = p.sbuf("wkb", [128, DT * D], BF16)
    wvb = p.sbuf("wvb", [128, DT * D], BF16)
    wfb = Carve(rstd, 1024, DT * 256, BF16)
    stage = sq
    l2 = p.sem("l2")

    def loadw(whbm, dst, cols2, e_i=None):
        for hq in range(4):
            p.Y.wait(sg, sg.n)
            if e_i is None:
                src3 = whbm.rearrange("(k p) c -> p k c", p=128)[
                    :, hq * 2:(hq + 1) * 2, :]
            else:
                src3 = whbm.rearrange("(E k p) c -> p E k c", p=128, E=E_)[
                    :, e_i, hq * 2:(hq + 1) * 2, :]
            p.Y(lambda e, d=stage, s=src3, cols2=cols2: e.dma_start(
                out=d[:, 0:2 * cols2].rearrange("p (k c) -> p k c", c=cols2),
                in_=s), inc=(l2, 16))
            p.V.wait(l2)
            p.V.wait(tt, tt.n)
            p.V(lambda e, d=dst, s=stage, hq=hq, cols2=cols2: e.tensor_copy(
                d[:, hq * 2 * cols2:(hq + 1) * 2 * cols2], s[:, 0:2 * cols2]))
            p.V.fence(inc=(sg, 1))
        p.T.wait(sg)

    p.Y.wait(sv, sv.n)
    p.V.wait(tt, tt.n)
    p.V(lambda e, d=y_acc: e.memset(d[:], 0.0))
    p.V.fence(inc=(sv, 1))
    loadw(w_q, wqb, D)

    for e_i in range(E_):
        loadw(w_k, wkb, D, e_i)
        loadw(w_v, wvb, D, e_i)
        loadw(w_f, wfb, 256, e_i)
        p.V.wait(tt, tt.n)
        p.V(lambda e, d=hk_sb: e.memset(d[:], 0.0))
        p.V(lambda e, d=hv_sb: e.memset(d[:], 0.0))
        p.V.fence(inc=(sv, 1))
        for tc in range(T // TC3):
            t0 = tc * TC3
            p.T.wait(sv)
            p.T(lambda e, o=ps0, s=self_f, cf=km_fm, e_i=e_i, t0=t0:
                e.matmul(o[:, 0:TC3], s[0:E_, e_i * 128:(e_i + 1) * 128],
                         cf[0:E_, t0:t0 + TC3], start=True, stop=True),
                inc=(tt, 1))
            p.T(lambda e, o=ps1, s=self_f, cf=wv_fm, e_i=e_i, t0=t0:
                e.matmul(o[:, 0:TC3], s[0:E_, e_i * 128:(e_i + 1) * 128],
                         cf[0:E_, t0:t0 + TC3], start=True, stop=True),
                inc=(tt, 1))
            p.V.wait(tt)
            p.V(lambda e, d=mb_b, s=ps0: e.tensor_copy(d[:], s[:, 0:TC3]))
            p.V(lambda e, d=wb_b, s=ps1: e.tensor_copy(d[:], s[:, 0:TC3]))
            p.V.fence()
            for di in range(DT):
                p.V(lambda e, d=xe_b, s=xg_b, m2=mb_b, di=di, t0=t0:
                    e.tensor_mul(d[:, di * TC3:(di + 1) * TC3],
                                 s[:, di * T + t0: di * T + t0 + TC3], m2[:]))
            p.V.fence(inc=(sv, 1))
            p.T.wait(sv)
            for wsrc, dstf in ((wqb, q_fm), (wkb, k_fm)):
                for mi in range(DT):
                    bank = banks[mi % 2]
                    for ki in range(DT):
                        p.T(lambda e, o=bank, w=wsrc, x=xe_b, ki=ki, mi=mi:
                            e.matmul(o[:, 0:TC3],
                                     w[:, ki * D + mi * 128: ki * D + mi * 128 + 128],
                                     x[:, ki * TC3:(ki + 1) * TC3],
                                     start=(ki == 0), stop=(ki == DT - 1)),
                            inc=(tt, 1) if ki == DT - 1 else None)
                    p.S.wait(tt)
                    p.S(lambda e, d=dstf, s=bank, mi=mi: e.activation(
                        d[:, mi * TC3:(mi + 1) * TC3], s[:, 0:TC3], AF.Silu))
                    p.S.fence(inc=(ss, 1))
                    p.T.wait(ss)
            if phase == 7:
                break
            for tt_i in range(TC3 // 128):
                p.T.wait(ss)
                p.T.wait(sv)
                for jc in range(2):
                    bank = banks[jc]
                    for ki in range(DT):
                        p.T(lambda e, o=bank, x=xe_b, w=wkb, ki=ki, tt_i=tt_i, jc=jc:
                            e.matmul(o[:], x[:, ki * TC3 + tt_i * 128: ki * TC3 + tt_i * 128 + 128],
                                     w[:, ki * D + jc * 512: ki * D + jc * 512 + 512],
                                     start=(ki == 0), stop=(ki == DT - 1)),
                            inc=(tt, 1) if ki == DT - 1 else None)
                p.S.wait(tt)
                for jc in range(2):
                    p.S(lambda e, d=k_tm, s=banks[jc], tt_i=tt_i, jc=jc: e.activation(
                        d[:, tt_i * D + jc * 512: tt_i * D + jc * 512 + 512],
                        s[:], AF.Silu))
                p.S.fence(inc=(ss, 1))
                for jc in range(2):
                    bank = banks[2 + jc]
                    for ki in range(DT):
                        p.T(lambda e, o=bank, x=xe_b, w=wvb, ki=ki, tt_i=tt_i, jc=jc:
                            e.matmul(o[:], x[:, ki * TC3 + tt_i * 128: ki * TC3 + tt_i * 128 + 128],
                                     w[:, ki * D + jc * 512: ki * D + jc * 512 + 512],
                                     start=(ki == 0), stop=(ki == DT - 1)),
                            inc=(tt, 1) if ki == DT - 1 else None)
                p.V.wait(tt)
                for jc in range(2):
                    p.V(lambda e, d=v_tm, s=banks[2 + jc], tt_i=tt_i, jc=jc:
                        e.tensor_copy(d[:, tt_i * D + jc * 512: tt_i * D + jc * 512 + 512],
                                      s[:]))
                p.V.fence(inc=(sv, 1))
                p.T.wait(sv)
                p.T.wait(ss)
                bank = banks[0]
                for ki in range(DT):
                    p.T(lambda e, o=bank, x=xe_b, w=wfb, ki=ki, tt_i=tt_i:
                        e.matmul(o[:, 0:256],
                                 x[:, ki * TC3 + tt_i * 128: ki * TC3 + tt_i * 128 + 128],
                                 w[:, ki * 256:(ki + 1) * 256],
                                 start=(ki == 0), stop=(ki == DT - 1)),
                        inc=(tt, 1) if ki == DT - 1 else None)
                p.S.wait(tt)
                p.S(lambda e, d=okb, s=bank: e.activation(
                    d[:], s[:, 0:256], AF.Abs))
                p.S.fence()
                p.S(lambda e, d=okb: e.activation(d[:], d[:], AF.Exp, scale=-1.0))
                p.S.fence()
                p.S(lambda e, d=okb: e.activation(d[:], d[:], AF.Ln, bias=1.0))
                p.S.fence(inc=(ss, 1))
                p.V.wait(ss)
                p.V(lambda e, d=dec_tm, s=bank: e.tensor_scalar_min(
                    d[:], s[:, 0:256], 0.0))
                p.V.fence()
                p.V(lambda e, d=dec_tm, s=okb: e.tensor_sub(d[:], d[:], s[:]))
                p.V.fence()
                p.T.wait(sv, sv.n)
                p.T(lambda e, o=banks[1], s=km_fm, idd=self_f, e_i=e_i, t0=t0,
                    tt_i=tt_i: e.matmul(
                        o[:, 0:1], s[0:E_, t0 + tt_i * 128: t0 + tt_i * 128 + 128],
                        idd[0:E_, e_i * 128 + 0: e_i * 128 + 1],
                        start=True, stop=True), inc=(tt, 1))
                p.V.wait(tt)
                p.V(lambda e, d=g_tm, s=dec_tm, o=banks[1], tt_i=tt_i:
                    e.tensor_scalar(d[:, tt_i * 256:(tt_i + 1) * 256], s[:],
                                    0.125, None, ALU.mult))
                p.V.fence()
                p.V(lambda e, d=g_tm, o=banks[1], tt_i=tt_i: e.tensor_scalar(
                    d[:, tt_i * 256:(tt_i + 1) * 256],
                    d[:, tt_i * 256:(tt_i + 1) * 256], o[:, 0:1], None, ALU.mult))
                p.V.fence(inc=(sv, 1))
                p.S.wait(sv)
                p.S(lambda e, d=st_tm, s=g_tm, tt_i=tt_i: e.activation(
                    d[:, tt_i * 256:(tt_i + 1) * 256],
                    s[:, tt_i * 256:(tt_i + 1) * 256], AF.Exp))
                p.S.fence(inc=(ss, 1))
                p.V.wait(ss)
                p.V(lambda e, d=st_tm, tt_i=tt_i: e.tensor_scalar(
                    d[:, tt_i * 256:(tt_i + 1) * 256],
                    d[:, tt_i * 256:(tt_i + 1) * 256], -1.0, 1.0,
                    ALU.mult, ALU.add))
                p.V.fence(inc=(sv, 1))
            if phase == 5:
                break
            for ci in range(TC3 // CH):
                tloc = ci * CH
                p.T.wait(sv)
                for h in range(4):
                    p.T(lambda e, o=ps0, s=g_tm, ci=ci, h=h, idf=idf:
                        e.transpose(o[0:64, h * 128:(h + 1) * 128],
                                    s[:, ci * 256 + h * 64: ci * 256 + h * 64 + 64],
                                    idf[:]), inc=(tt, 1) if h == 3 else None)
                p.V.wait(tt)
                p.V(lambda e, d=gsm, s=ps0: e.tensor_copy(
                    d[0:64, :], s[0:64, 0:512]))
                p.V.fence()
                for h in range(4):
                    p.V(lambda e, d=Gsm, s=gsm, h=h: e.tensor_tensor_scan(
                        d[0:64, h * 128:(h + 1) * 128],
                        s[0:64, h * 128:(h + 1) * 128],
                        s[0:64, h * 128:(h + 1) * 128], 0.0, ALU.add, ALU.bypass))
                p.V.fence(inc=(sv, 1))
                p.V(lambda e, d=glsm, s=Gsm: e.tensor_copy(
                    d[0:64, 0:4], s[0:64, :].rearrange(
                        "p (a b) -> p a b", b=CH)[:, :, CH - 1]))
                p.V.fence(inc=(sv, 1))
                p.S.wait(sv)
                p.S(lambda e, d=glsm: e.activation(d[0:64, 0:4], d[0:64, 0:4],
                                                   AF.Exp))
                p.S.fence(inc=(ss, 1))
                p.T.wait(sv)
                for h in range(4):
                    p.T(lambda e, o=ps1, s=Gsm, h=h, idf=idf:
                        e.transpose(o[:, h * 64:(h + 1) * 64],
                                    s[0:64, h * 128:(h + 1) * 128],
                                    idf[0:64, 0:64]),
                        inc=(tt, 1) if h == 3 else None)
                p.S.wait(tt)
                p.S(lambda e, d=dec_tm, s=ps1: e.activation(
                    d[:], s[:, 0:256], AF.Exp))
                p.S.fence()
                p.S(lambda e, d=okb, s=ps1: e.activation(
                    d[:], s[:, 0:256], AF.Exp, scale=-1.0))
                p.S.fence(inc=(ss, 1))
                p.V.wait(ss)
                p.V(lambda e, d=sin_tm, s=st_tm, o=okb, ci=ci: e.tensor_mul(
                    d[:, 0:256], s[:, ci * 256:(ci + 1) * 256], o[:]))
                p.V.fence(inc=(sv, 1))
                p.T.wait(sv)
                for h in range(4):
                    p.T(lambda e, o=ptb3, s=sin_tm, h=h, idb=idb:
                        e.transpose(o[0:64, h * 128:(h + 1) * 128],
                                    s[:, h * 64:(h + 1) * 64], idb[:]),
                        inc=(tt, 1) if h == 3 else None)
                p.T(lambda e, o=ps1, s=glsm, idf=idf: e.transpose(
                    o[0:4, 0:64], s[0:64, 0:4], idf[0:64, 0:64]), inc=(tt, 1))
                p.V.wait(tt)
                p.V(lambda e, d=sinT_b, s=ptb3: e.tensor_copy(
                    d[0:64, :], s[0:64, 0:512]))
                p.V(lambda e, d=glb, s=ps1: e.tensor_copy(
                    d[0:4, 0:64], s[0:4, 0:64]))
                p.V.fence(inc=(sv, 1))
                p.T.wait(sv)
                for h in range(4):
                    p.T(lambda e, o=ps1, w=self_f, s=glb, h=h: e.matmul(
                        o[:, 128 + h * 64: 128 + h * 64 + 64],
                        w[0:4, h * 128:(h + 1) * 128],
                        s[0:4, 0:64], start=True, stop=True),
                        inc=(tt, 1) if h == 3 else None)
                p.V.wait(tt)
                p.V(lambda e, d=glb, s=ps1: e.tensor_copy(
                    d[:, 64:64 + 256], s[:, 128:384]))
                p.V.fence(inc=(sv, 1))
                p.V(lambda e, d=srem_sm, s=sinT_b, g=glsm: e.tensor_tensor(
                    d[0:64, :].rearrange("p (a b) -> p a b", b=CH),
                    s[0:64, :].rearrange("p (a b) -> p a b", b=CH),
                    g[0:64, 0:4].unsqueeze(2).to_broadcast([64, 4, CH]),
                    ALU.mult))
                p.V.fence(inc=(sv, 1))
                p.T.wait(sv)
                for h in range(4):
                    p.T(lambda e, o=ps1, s=srem_sm, h=h, idf=idf:
                        e.transpose(o[:, h * 64:(h + 1) * 64],
                                    s[0:64, h * 128:(h + 1) * 128],
                                    idf[0:64, 0:64]),
                        inc=(tt, 1) if h == 3 else None)
                p.V.wait(tt)
                p.V(lambda e, d=srem_tm, s=ps1: e.tensor_copy(d[:], s[:, 0:256]))
                p.V(lambda e, d=hk_b, s=hk_sb: e.tensor_copy(d[:], s[:]))
                p.V(lambda e, d=hv_b, s=hv_sb: e.tensor_copy(
                    d[0:64, :], s[0:64, :]))
                p.V.fence(inc=(sv, 1))
                p.T.wait(sv)
                for h in range(4):
                    for s2 in range(2):
                        p.T(lambda e, o=ps0, kf2=k_fm, qf2=q_fm, h=h, s2=s2:
                            e.matmul(
                                o[:, 0:CH],
                                kf2[:, (2 * h + s2) * TC3 + tloc: (2 * h + s2) * TC3 + tloc + CH],
                                qf2[:, (2 * h + s2) * TC3 + tloc: (2 * h + s2) * TC3 + tloc + CH],
                                start=(s2 == 0), stop=(s2 == 1)),
                            inc=(tt, 1) if s2 == 1 else None)
                    p.V.wait(tt)
                    p.V(lambda e, d=AT_b, s=ps0, m2=triub: e.tensor_tensor(
                        d[:], s[:, 0:CH], m2[:], ALU.mult))
                    p.V.fence(inc=(sv, 1))
                    p.T.wait(sv)
                    bank = pm0
                    p.T(lambda e, o=bank, a=AT_b, s=sin_tm, h=h: e.matmul(
                        o[:, 0:64], a[:], s[:, h * 64:(h + 1) * 64],
                        start=True, stop=False))
                    for s2 in range(2):
                        p.T(lambda e, o=bank, q=q_fm, hk=hk_b, h=h, s2=s2:
                            e.matmul(o[:, 0:64],
                                     q[:, (2 * h + s2) * TC3 + tloc: (2 * h + s2) * TC3 + tloc + CH],
                                     hk[:, (h * 2 + s2) * 64: (h * 2 + s2) * 64 + 64],
                                     start=False, stop=(s2 == 1)),
                            inc=(tt, 1) if s2 == 1 else None)
                    p.V.wait(tt)
                    p.V(lambda e, d=okb, s=bank, dc=dec_tm, h=h: e.tensor_mul(
                        d[:, 0:64], s[:, 0:64], dc[:, h * 64:(h + 1) * 64]))
                    p.V.fence()
                    p.V(lambda e, d=nmv, s=okb: e.reduce_max(
                        d[:], s[:, 0:64], AX.X, negate=True))
                    p.V.fence(inc=(sv, 1))
                    p.S.wait(sv)
                    p.S(lambda e, d=okb, m2=nmv, a=denv: e.activation(
                        d[:, 0:64], d[:, 0:64], AF.Exp, bias=m2[:, 0:1],
                        accum_out=a[:, 0:1]))
                    p.S.fence(inc=(ss, 1))
                    p.V.wait(ss)
                    p.V(lambda e, d=denv: e.reciprocal(d[:], d[:]))
                    p.V.fence()
                    p.V(lambda e, d=okb, r=denv: e.tensor_scalar(
                        d[:, 0:64], d[:, 0:64], r[:, 0:1], None, ALU.mult))
                    p.V.fence()
                    p.V(lambda e, d=qvg_b, s=okb, dc=dec_tm, h=h: e.tensor_mul(
                        d[:, 0:64], s[:, 0:64], dc[:, h * 64:(h + 1) * 64]))
                    p.V.fence(inc=(sv, 1))
                    p.T.wait(sv)
                    p.T(lambda e, o=ptb3, s=qvg_b: e.transpose(
                        o[0:64, 0:CH], s[:, 0:64], idb[:]), inc=(tt, 1))
                    p.V.wait(tt)
                    p.V(lambda e, d=qvgT, s=ptb3: e.tensor_copy(
                        d[0:64, 0:CH], s[0:64, 0:CH]))
                    p.V.fence(inc=(sv, 1))
                    p.T.wait(sv)
                    p.T(lambda e, o=ps0, s=sinT_b, qg=qvgT, h=h: e.matmul(
                        o[:, 0:CH], s[0:64, h * 128:(h + 1) * 128],
                        qg[0:64, 0:CH], start=True, stop=True), inc=(tt, 1))
                    p.V.wait(tt)
                    p.V(lambda e, d=TmT_b, s=ps0, m2=triub: e.tensor_tensor(
                        d[:], s[:, 0:CH], m2[:], ALU.mult))
                    p.V.fence(inc=(sv, 1))
                    p.T.wait(sv)
                    for s2 in range(2):
                        p.T(lambda e, o=pm1, v=v_tm, tm=TmT_b, h=h, s2=s2, ci=ci:
                            e.matmul(o[:, s2 * CH:(s2 + 1) * CH],
                                     v[:, ci * D + h * 256 + s2 * 128: ci * D + h * 256 + s2 * 128 + 128],
                                     tm[:], start=True, stop=True), inc=(tt, 1))
                        p.T(lambda e, o=pm2, hv=hv_b, qg=qvgT, h=h, s2=s2:
                            e.matmul(o[:, s2 * CH:(s2 + 1) * CH],
                                     hv[0:64, h * 256 + s2 * 128: h * 256 + s2 * 128 + 128],
                                     qg[0:64, 0:CH], start=True, stop=True),
                            inc=(tt, 1))
                    p.V.wait(tt)
                    for s2 in range(2):
                        p.V(lambda e, d=okb, s=pm2, s2=s2: e.tensor_copy(
                            d[:, s2 * 128:(s2 + 1) * 128],
                            s[:, s2 * CH:(s2 + 1) * CH]))
                    p.V.fence()
                    for s2 in range(2):
                        p.V(lambda e, d=okb, s=pm1, s2=s2: e.tensor_add(
                            d[:, s2 * 128:(s2 + 1) * 128],
                            d[:, s2 * 128:(s2 + 1) * 128],
                            s[:, s2 * CH:(s2 + 1) * CH]))
                    p.V.fence()
                    p.V(lambda e, d=okb, w=wb_b, tloc=tloc: e.tensor_tensor(
                        d[:, 0:256].rearrange("p (a b) -> p a b", b=CH),
                        d[:, 0:256].rearrange("p (a b) -> p a b", b=CH),
                        w[:, tloc:tloc + CH].unsqueeze(1).to_broadcast(
                            [128, 2, CH]), ALU.mult))
                    p.V.fence()
                    for s2 in range(2):
                        p.V(lambda e, d=y_acc, s=okb, h=h, s2=s2, t0=t0,
                            tloc=tloc: e.tensor_add(
                                d[:, (2 * h + s2) * T + t0 + tloc:
                                   (2 * h + s2) * T + t0 + tloc + CH],
                                d[:, (2 * h + s2) * T + t0 + tloc:
                                   (2 * h + s2) * T + t0 + tloc + CH],
                                s[:, s2 * 128:(s2 + 1) * 128]))
                    p.V.fence(inc=(sv, 1))
                    p.T.wait(sv)
                    for s2 in range(2):
                        p.T(lambda e, o=pm3, kt2=k_tm, sr=srem_tm, h=h, s2=s2,
                            ci=ci: e.matmul(
                                o[:, s2 * 64:(s2 + 1) * 64],
                                kt2[:, ci * D + h * 256 + s2 * 128: ci * D + h * 256 + s2 * 128 + 128],
                                sr[:, h * 64:(h + 1) * 64],
                                start=True, stop=True), inc=(tt, 1))
                    p.T(lambda e, o=pm0, sr=srem_tm, v=v_tm, h=h, ci=ci:
                        e.matmul(o[0:64, 0:256], sr[:, h * 64:(h + 1) * 64],
                                 v[:, ci * D + h * 256: ci * D + h * 256 + 256],
                                 start=True, stop=True), inc=(tt, 1))
                    p.V.wait(tt)
                    for s2 in range(2):
                        p.V(lambda e, d=hk_sb, g=glb, h=h, s2=s2: e.tensor_mul(
                            d[:, (h * 2 + s2) * 64: (h * 2 + s2) * 64 + 64],
                            d[:, (h * 2 + s2) * 64: (h * 2 + s2) * 64 + 64],
                            g[:, 64 + h * 64: 64 + h * 64 + 64]))
                    p.V(lambda e, d=hv_sb, g=glsm, h=h: e.tensor_scalar(
                        d[0:64, h * 256:(h + 1) * 256],
                        d[0:64, h * 256:(h + 1) * 256],
                        g[0:64, h:h + 1], None, ALU.mult))
                    p.V.fence()
                    for s2 in range(2):
                        p.V(lambda e, d=hk_sb, s=pm3, h=h, s2=s2: e.tensor_add(
                            d[:, (h * 2 + s2) * 64: (h * 2 + s2) * 64 + 64],
                            d[:, (h * 2 + s2) * 64: (h * 2 + s2) * 64 + 64],
                            s[:, s2 * 64:(s2 + 1) * 64]))
                    p.V(lambda e, d=hv_sb, s=pm0, h=h: e.tensor_add(
                        d[0:64, h * 256:(h + 1) * 256],
                        d[0:64, h * 256:(h + 1) * 256], s[0:64, 0:256]))
                    p.V.fence(inc=(sv, 1))
                    p.T.wait(sv)
                    if phase == 6:
                        break
                if phase == 6:
                    break
            if phase in (5, 6, 7):
                break
        if phase in (5, 6, 7):
            break
    if phase in (5, 6, 7):
        p.G.wait(sv)
        p.G(lambda e, d=rout, s=km_fm: e.dma_start(out=d[0:128, :], in_=s[:]),
            inc=(st, 16))
        p.G.wait(st)
        return _fin(p, fused)

    loadw(w_o, wqb, D)
    ysq = sq
    ystat = km_fm
    p.S.wait(sg, sg.n)
    for c in range(NCH):
        p.S.wait(sv)
        p.S(lambda e, d=ysq, s=y_acc, c=c: e.activation(
            d[:].rearrange("p (k n) -> p k n", n=512),
            y_acc[:].rearrange("p (k t) -> p k t", t=T)[:, :, c * 512:(c + 1) * 512]
            if False else _carve3d(y_acc, c), AF.Square))
        p.S.fence(inc=(ss, 1))
        p.T.wait(ss)
        for di in range(DT):
            p.T(lambda e, o=ps0, w=ones_f, s=ysq, di=di:
                e.matmul(o[:], w[:], s[:, di * 512:(di + 1) * 512],
                         start=(di == 0), stop=(di == DT - 1)),
                inc=(tt, 1) if di == DT - 1 else None)
        p.V.wait(tt)
        p.V(lambda e, d=ystat, s=ps0, c=c: e.tensor_scalar(
            d[:, c * 512:c * 512 + 512], s[:], 1.0 / D, EPS, ALU.mult, ALU.add))
        p.V.fence(inc=(sv, 1))
        p.T.wait(sv)
        p.S.wait(sv)
    p.V(lambda e, d=ystat: e.reciprocal(d[:], d[:]))
    p.V.fence(inc=(sv, 1))
    p.S.wait(sv)
    p.S(lambda e, d=ystat: e.activation(d[:], d[:], AF.Sqrt))
    p.S.fence(inc=(ss, 1))
    p.V.wait(ss)
    for di in range(DT):
        p.V(lambda e, d=ysq, s=y_acc, r=ystat, di=di: e.tensor_mul(
            d[:, 0:T], s[:, di * T:(di + 1) * T], r[:]))
        p.V.fence()
        p.V(lambda e, d=y_acc, s=ysq, g=gwc, di=di: e.tensor_scalar(
            d[:, di * T:(di + 1) * T], s[:, 0:T], g[:, di:di + 1], None,
            ALU.mult))
        p.V.fence()
    p.V.fence(inc=(sv, 1))
    p.T.wait(sv)
    l3 = p.sem("l3")
    rl3 = Carve(res, 8192, T, F32)
    obf = Carve(res, 10240, T, F32)
    p.Y.wait(sg, sg.n)
    p.Y.wait(sv, sv.n)
    for mi in range(DT):
        p.V.wait(st, st.n)
        p.Y(lambda e, d=rl3, s=rin2, mi=mi: e.dma_start(
            out=d[:], in_=s.rearrange("(k p) t -> p k t", p=128)[:, mi, :]),
            inc=(l3, 16))
        for c in range(NCH):
            bank = banks[c]
            for ki in range(DT):
                p.T(lambda e, o=bank, w=wqb, x=y_acc, ki=ki, mi=mi, c=c:
                    e.matmul(o[:], w[:, ki * D + mi * 128: ki * D + mi * 128 + 128],
                             x[:, ki * T + c * 512: ki * T + c * 512 + 512],
                             start=(ki == 0), stop=(ki == DT - 1)),
                    inc=(tt, 1) if ki == DT - 1 else None)
        p.V.wait(tt)
        p.V.wait(l3)
        for c in range(NCH):
            p.V(lambda e, d=obf, s=banks[c], r=rl3, c=c: e.scalar_tensor_tensor(
                d[:, c * 512:c * 512 + 512], s[:], 0.5,
                r[:, c * 512:c * 512 + 512], ALU.mult, ALU.add))
        p.V.fence(inc=(sv, 1))
        p.G.wait(sv)
        p.G(lambda e, d=rout, s=obf, mi=mi: e.dma_start(
            out=d.rearrange("(k p) t -> p k t", p=128)[:, mi, :], in_=s[:]),
            inc=(st, 16))
        p.T.wait(sv)
        p.Y.wait(sv, sv.n)
        p.Y.wait(st, st.n)
    p.G.wait(st)
    return _fin(p, fused)


def _carve3d(cv, c):
    ap = cv[:]
    return ap.rearrange("p (k t) -> p k t", t=T)[:, :, c * 512:(c + 1) * 512]


def l3_ffn_inputs(inputs, core, rin_fm):
    import ml_dtypes
    E_ = 8
    self1 = np.zeros((E_, E_ * 128), np.float32)
    for e in range(E_):
        self1[e, e * 128:(e + 1) * 128] = 1.0
    def cols(v, n=8):
        return np.ascontiguousarray(np.asarray(v).reshape(n, 128).T)
    triu = (np.arange(128)[:, None] <= np.arange(128)[None, :]).astype(np.float32)
    return _with_ones({
        "rin": rin_fm, "rin2": rin_fm,
        "WG": np.ascontiguousarray(inputs["L3_ffn_Wg"].T),
        "WQ": np.ascontiguousarray(inputs["L3_ffn_Wq"].T),
        "WKE": np.ascontiguousarray(
            inputs["L3_ffn_Wk"].transpose(0, 2, 1).reshape(E_ * D, D)),
        "WVE": np.ascontiguousarray(
            inputs["L3_ffn_Wv"].transpose(0, 2, 1).reshape(E_ * D, D)),
        "WFE": np.ascontiguousarray(
            inputs["L3_ffn_Wf"].transpose(0, 2, 1).reshape(E_ * D, 256)),
        "WOT": np.ascontiguousarray(inputs["L3_ffn_Wo"].T),
        "ones": np.ones((128, 128), np.float32),
        "ident": np.eye(128, dtype=np.float32),
        "identb": np.eye(128, dtype=np.float32).astype(ml_dtypes.bfloat16),
        "SELF1": self1,
        "NRMW": cols(inputs["L3_ffn_norm_w"]),
        "GNRMW": cols(inputs["L3_ffn_gnorm_w"]),
        "TRIU01": triu.astype(ml_dtypes.bfloat16),
    })


STAGES = [
    ("l2_ffn", build_l2_ffn, l2_ffn_inputs),
    ("l2_attn", build_l2_attn, l2_attn_inputs),
    ("l1_ffn", build_l1_ffn, l1_ffn_inputs),
    ("l0_ffn", build_l0_ffn, l0_ffn_inputs),
    ("l1_attn", build_l1_attn, l1_attn_inputs),
    ("l3_attn", build_l3_attn, l3_attn_inputs),
    ("l0_attn_wkv", build_l0_attn_wkv, l0_attn_wkv_inputs),
    ("l0_attn_moe", build_l0_attn_moe, l0_attn_moe_inputs),
    ("l3_ffn", build_l3_ffn, l3_ffn_inputs),
]

SEQ = ["l0_ffn", "l0_attn_wkv", "l0_attn_moe", "l0_ffn",
       "l1_ffn", "l1_attn", "l1_ffn",
       "l2_ffn", "l2_attn", "l2_ffn",
       "l3_ffn", "l3_attn", "l3_ffn"]


def build_fused():
    p = Prog("mega")
    ones_f_id(p, p.sem("ld"))
    idf_b(p, p.sem("ld"))
    ones_root(p, p.sem("ld"))
    bld = {k: b for k, b, _ in STAGES}
    rin0 = p.dram_in("rin0", [D, T], F32)
    rout_f = p.dram_out("rout", [D, T], F32)
    scr = [p.nc.dram_tensor(f"scr{i}", [D, T], F32, kind="Internal").ap()
           for i in range(3)]
    chain = [(-1, 0, None), (0, 1, None), (1, 2, 0), (2, 0, None),
             (0, 1, None), (1, 0, None), (0, 1, None),
             (1, 0, None), (0, 1, None), (1, 0, None),
             (0, 1, None), (1, 0, None), (0, -2, None)]
    def ap(i):
        if i == -1:
            return rin0
        if i == -2:
            return rout_f
        return scr[i]
    for key, (ri, ro, r3) in zip(SEQ, chain):
        fz = {"p": p, "rin": ap(ri), "rout": ap(ro), "px": key + "_"}
        if r3 is not None:
            fz["rin3"] = ap(r3)
        p.push_scope()
        bld[key](99, fz)
        p.barrier()
        p.pop_scope()
    return p.finalize()


